# version 14
# baseline (speedup 1.0000x reference)
import numpy as np
from collections import defaultdict
from contextlib import ExitStack

import concourse.bass as bass
import concourse.mybir as mybir
from concourse.bass_utils import run_bass_kernel_spmd

F32 = mybir.dt.float32
ALU = mybir.AluOpType
AX = mybir.AxisListType.X
AF = mybir.ActivationFunctionType

D = 1024
NZ = 8
TM = 512
NSUB = 4

DN = float(np.float32(64.0 ** -0.25))
HDN2 = 0.0625
RATIO = 0.125
EPS_FEAT = float(np.float32(1e-4))
EPS_DEN = float(np.float32(0.01))
LN_EPS = 1e-5
C1024 = 1.0 / 1024.0

DEBUG = False
import os as _os
SKIP_SUBT = bool(int(_os.environ.get('SKIP_SUBT', '0')))
SKIP_XN = bool(int(_os.environ.get('SKIP_XN', '0')))
ONE_SEM = bool(int(_os.environ.get('ONE_SEM', '0')))
NOFILT = bool(int(_os.environ.get('NOFILT', '0')))
RELAX = tuple(x for x in _os.environ.get('RELAX', 'tensor').split(',') if x)
SKIP_FFN_OUT = bool(int(_os.environ.get('SKIP_FFN_OUT', '0')))
SKIP_ATTN_OUT = bool(int(_os.environ.get('SKIP_ATTN_OUT', '0')))
SKIP_C = bool(int(_os.environ.get('SKIP_C', '0')))
FILT_KEEP = set(_os.environ.get('FILT_KEEP', '').split(',')) - {''}
VMODE = _os.environ.get('VMODE', 'all')

ENGINES = ("sync", "tensor", "vector", "scalar", "gpsimd")
OP_SEM = {"sync": "sy", "tensor": "t", "vector": "v", "scalar": "sc", "gpsimd": "ge"}
DMA_SEM = {"sync": "sd", "gpsimd": "gd"}
NDMASEM = 8
DMA_SEMS = {e: [p + str(i) for i in range(NDMASEM)] for e, p in DMA_SEM.items()}
ALL_DMA_SEMS = [s for l in DMA_SEMS.values() for s in l]


class Buf:
    __slots__ = ("w", "rs", "psum", "cur_n", "glast_tok", "glast_tpos",
                 "glast_multi")

    def __init__(self, psum=False):
        self.w = None
        self.rs = []
        self.psum = psum
        self.cur_n = 0
        self.glast_tok = None
        self.glast_tpos = -10**9
        self.glast_multi = False


SERIALIZE = False


class Sched:
    DRAIN_WIN = 256

    def __init__(self):
        self.counts = defaultdict(int)
        self.ops = {e: [] for e in ENGINES}
        self.hi = defaultdict(int)
        self.floor = None
        self.tpos = 0
        self.psum_toks = set()
        self.dma_idx = defaultdict(int)
        self.dma_uses = defaultdict(int)

    def do(self, eng, fn, reads=(), writes=(), dma=False, deps=(), sem=None,
           inc=None, noinc=False, mm=None, tag=""):
        ser_dep = None
        if sem is None:
            if dma:
                k = self.dma_idx[eng] % NDMASEM
                self.dma_idx[eng] += 1
                sem = DMA_SEMS[eng][k]
                if self.dma_uses[sem] > 0:
                    ser_dep = (sem, 16 * self.dma_uses[sem])
                self.dma_uses[sem] += 1
            else:
                sem = OP_SEM[eng]
        if ONE_SEM and sem != "cc":
            sem = "u"
        if inc is None:
            inc = 16 if dma else 1
        if SERIALIZE:
            deps = list(deps) + [(s2, v2) for (s2, v2) in self.counts.items()
                                 if v2 > 0 and s2 not in ("bar",)]
        dl = []
        hard = []
        if ser_dep is not None:
            hard.append(ser_dep)
        if self.floor is not None and eng in getattr(self, "floor_exempt", ()):
            pass

        if mm is not None:
            bank, g_start, g_stop, nout = mm
            if g_start and bank.glast_multi and bank.glast_tok is not None \
                    and (self.tpos - bank.glast_tpos) < self.DRAIN_WIN:
                hard.append(bank.glast_tok)
            self.tpos += max(nout, 60)
        if self.floor is not None and eng not in getattr(self, "floor_exempt", ()):
            dl.append(self.floor)
        dl.extend(d for d in deps if d is not None)
        for b in reads:
            if b.w is not None:
                dl.append(b.w)
            if b.psum:
                for (t, te) in b.rs:
                    if te != eng:
                        dl.append(t)
        for b in writes:
            if b.w is not None:
                dl.append(b.w)
            dl.extend(t for (t, _te) in b.rs)
        waits = []
        for (ws, wv) in dl:
            if ws == OP_SEM[eng] and eng in RELAX:
                continue
            waits.append((ws, wv))
        waits.extend(hard)
        if noinc:
            tok = None
            self.ops[eng].append((fn, waits, None, 0, tag))
        else:
            self.counts[sem] += inc
            tok = (sem, self.counts[sem])
            self.ops[eng].append((fn, waits, sem, inc, tag))
        if tok is not None:
            for b in reads:
                b.rs.append((tok, eng))
            for b in writes:
                b.w = tok
                b.rs = []
        if tok is not None and any(getattr(b, "psum", False) for b in reads):
            self.psum_toks.add(tok)
        if mm is not None and tok is not None:
            bank, g_start, g_stop, nout = mm
            bank.cur_n = 1 if g_start else bank.cur_n + 1
            if g_stop:
                bank.glast_tok = tok
                bank.glast_tpos = self.tpos
                bank.glast_multi = bank.cur_n > 1
        return tok

    def barrier(self, exempt=()):
        n = 0
        for eng in ENGINES:
            if eng in exempt:
                continue
            deps = []
            if eng in DMA_SEMS:
                for sm in DMA_SEMS[eng]:
                    if self.counts[sm] > 0:
                        deps.append((sm, self.counts[sm]))
            self.do(eng, lambda e: e.nop(), deps=deps, sem="bar", inc=1)
            n += 1
        self.floor = ("bar", self.counts["bar"])
        self.floor_exempt = set(exempt)

    def flush(self, block, sems):
        hi = self.hi
        for eng in ENGINES:
            ops = self.ops[eng]
            if not ops:
                continue

            trc = getattr(self, "trace", None)

            def body(e, eng=eng, ops=ops):
                for (fn, waits, sem, inc, tag) in ops:
                    emitted = []
                    for (ws, wv) in waits:
                        key = (eng, ws)
                        if wv > hi[key]:
                            e.wait_ge(sems[ws], wv)
                            hi[key] = wv
                            emitted.append((ws, wv))
                    ins = fn(e)
                    if sem is not None:
                        ins.then_inc(sems[sem], inc)
                    if trc is not None:
                        trc.write("%s %s%s -> %s  waits=%s raw=%s\n" % (
                            eng, tag, "", (sem, self.counts.get(sem)) if sem else "noinc",
                            emitted, waits))

            getattr(block, eng)(body)
            self.ops[eng] = []


def _r3(ap, t):
    return ap.rearrange("p (z t) -> p z t", t=t)


def build_nc(nmac=8, debug=False):
    ntok = TM * nmac
    nsubs = NSUB * nmac
    nc = bass.Bass("TRN2", target_bir_lowering=False)

    xT = nc.dram_tensor("xT", [D, ntok + 1], F32, kind="ExternalInput").ap()
    wT = {}
    for name in ("wq", "wk", "wv", "wo", "wr", "wkf", "wvf"):
        wT[name] = nc.dram_tensor(name, [D, D], F32, kind="ExternalInput").ap()
    projP = nc.dram_tensor("projP", [NZ, 128, 128], F32, kind="ExternalInput").ap()
    biasT = nc.dram_tensor("biasT", [D, 8], F32, kind="ExternalInput").ap()
    bias_rows = nc.dram_tensor("bias_rows", [8, D], F32, kind="ExternalInput").ap()
    out = nc.dram_tensor("out", [D, ntok], F32, kind="ExternalOutput").ap()

    PAY = NZ * 129 + 16 + 8
    RM0 = NZ * 129
    cc_in = nc.dram_tensor("cc_in", [129, PAY], F32).ap()
    cc_out = nc.dram_tensor("cc_out", [258, PAY], F32).ap()
    groups = [[0, 1], [2, 3], [4, 5], [6, 7]]

    dbg = {}
    if debug:
        for nm, shp in [("d_stats", [128, nsubs * 2]), ("d_pay", [128, PAY]),
                        ("d_comb", [128, PAY]), ("d_kv", [128, NZ * 130]),
                        ("d_sall", [128, 16]), ("d_vsum", [1, D]),
                        ("d_xn", [128, NZ * TM]), ("d_ksb", [128, NZ * TM]),
                        ("d_e", [128, 1024]), ("d_q", [128, NZ * TM]),
                        ("d_qp", [128, 1024]), ("d_ctx", [128, NZ * TM]),
                        ("d_vsb", [128, NZ * 129]), ("d_st0", [128, 8]),
                        ("d_ev0", [128, 8]), ("d_t20", [128, 16]),
                        ("d_bc", [128, 1024]), ("d_srow", [1, 256]),
                        ("d_xm", [128, NZ * TM]), ("d_xa", [128, NZ * TM]),
                        ("d_xl", [128, NZ]), ("d_xr0", [128, NZ]),
                        ("d_xk0", [128, NZ]), ("d_xnb", [128, NZ])]:
            dbg[nm] = nc.dram_tensor(nm, shp, F32, kind="ExternalOutput").ap()

    S = Sched()

    with ExitStack() as ctx:
        sems = {}
        for snm in (ALL_DMA_SEMS + ["sy", "t", "v", "sc", "ge", "bar", "cc", "u"]):
            sems[snm] = ctx.enter_context(nc.semaphore(snm))

        ident = ctx.enter_context(nc.sbuf_tensor("ident", [128, 128], F32))
        projs = ctx.enter_context(nc.sbuf_tensor("projs", [128, NZ, 128], F32))
        ndiag = ctx.enter_context(nc.sbuf_tensor("ndiag", [128, 128], F32))
        hones = ctx.enter_context(nc.sbuf_tensor("hones", [128, 2], F32))
        onescol = ctx.enter_context(nc.sbuf_tensor("onescol", [128, 1], F32))
        epscol = ctx.enter_context(nc.sbuf_tensor("epscol", [128, 1], F32))
        ones1 = ctx.enter_context(nc.sbuf_tensor("ones1", [1, 128], F32))
        smalls = ctx.enter_context(nc.sbuf_tensor("smalls", [128, NZ, 8], F32))
        mu1m = ctx.enter_context(nc.sbuf_tensor("mu1m", [128, NZ, 2], F32))
        stats = ctx.enter_context(nc.sbuf_tensor("stats", [128, nsubs, 2], F32))
        xnsum = ctx.enter_context(nc.sbuf_tensor("xnsum", [128, NZ], F32))
        xns2 = ctx.enter_context(nc.sbuf_tensor("xns2", [128, NZ], F32))
        payload = ctx.enter_context(nc.sbuf_tensor("payload", [128, PAY], F32))
        vsrow = ctx.enter_context(nc.sbuf_tensor("vsrow", [1, PAY], F32))
        bvrow = ctx.enter_context(nc.sbuf_tensor("bvrow", [1, D], F32))
        bvbc = ctx.enter_context(nc.sbuf_tensor("bvbc", [128, NZ, 128], F32))
        kvr = ctx.enter_context(nc.sbuf_tensor("kvr", [128, NZ, 130], F32))
        sall = ctx.enter_context(nc.sbuf_tensor("sall", [128, 16], F32))
        strow = ctx.enter_context(nc.sbuf_tensor("strow", [1, 2, 128], F32))

        pb = [ctx.enter_context(nc.psum_tensor("pb%d" % i, [128, 512], F32))
              for i in range(8)]
        PB = [Buf(psum=True) for _ in range(8)]

        (B_ident, B_projs, B_ndiag, B_hones, B_ones, B_ones1, B_smalls, B_mu1m,
         B_stats, B_xnsum, B_xns2, B_pay, B_vsrow, B_bvrow, B_bvbc, B_kvr,
         B_sall, B_strow, B_out, B_eps) = (Buf() for _ in range(20))

        strow_ps = pb[7].ap().rearrange("p (a b) -> p a b", b=128)

        def ln_bcast(g, bank_r, bank_m, win):
            for col in range(2):
                S.do("tensor", lambda e, g=g, col=col: e.transpose(
                    strow_ps[0:1, 2 + col, :], stats[:, g, col:col + 1], ident[:, :]),
                    reads=[B_stats, B_ident], writes=[PB[7]],
                    mm=(PB[7], True, True, 128))
            S.do("vector", lambda e: e.tensor_copy(
                strow[0:1, :, :], strow_ps[0:1, 2:4, :]),
                reads=[PB[7]], writes=[B_strow])
            for col, bank in ((0, bank_r), (1, bank_m)):
                S.do("tensor", lambda e, col=col, bank=bank, win=win: e.matmul(
                    pb[bank][:, 128 * win:128 * (win + 1)], ones1[:, :],
                    strow[0:1, col, :], start=True, stop=True),
                    reads=[B_ones1, B_strow], writes=[PB[bank]],
                    mm=(PB[bank], True, True, 128), tag="bcast b%d w%d" % (bank, win))

        stA = ExitStack()
        xa = [stA.enter_context(nc.sbuf_tensor("xa%d" % i, [128, NZ, TM], F32)) for i in range(2)]
        xn = stA.enter_context(nc.sbuf_tensor("xnA", [128, NZ, TM], F32))
        sq = stA.enter_context(nc.sbuf_tensor("sqA", [128, NZ, TM], F32))
        BF = mybir.dt.bfloat16
        wkh = stA.enter_context(nc.sbuf_tensor("wkh", [128, NZ, D], BF))
        wkl = stA.enter_context(nc.sbuf_tensor("wkl", [128, NZ, D], BF))
        wvh = stA.enter_context(nc.sbuf_tensor("wvh", [128, NZ, D], BF))
        wvl = stA.enter_context(nc.sbuf_tensor("wvl", [128, NZ, D], BF))
        xnh = stA.enter_context(nc.sbuf_tensor("xnh", [128, NZ, TM], BF))
        xnl = stA.enter_context(nc.sbuf_tensor("xnl", [128, NZ, TM], BF))
        xsh = stA.enter_context(nc.sbuf_tensor("xsh", [128, NZ], BF))
        xsl = stA.enter_context(nc.sbuf_tensor("xsl", [128, NZ], BF))
        k_sb = stA.enter_context(nc.sbuf_tensor("k_sb", [128, NZ, TM], F32))
        v_sb = [stA.enter_context(nc.sbuf_tensor("v_sb%d" % i, [128, NZ, 129], F32)) for i in range(2)]
        e_sb = [stA.enter_context(nc.sbuf_tensor("e_sb%d" % i, [128, 1024], F32)) for i in range(2)]
        rmx = stA.enter_context(nc.sbuf_tensor("rmx", [128, 16], F32))
        dgs = stA.enter_context(nc.sbuf_tensor("dgs", [128, 16], F32))
        stev = stA.enter_context(nc.sbuf_tensor("stev", [128, 4, 2], F32))
        stev2 = stA.enter_context(nc.sbuf_tensor("stev2", [128, 2, 4, 2], F32))
        stt2 = stA.enter_context(nc.sbuf_tensor("stt2", [128, 4, 4], F32))
        dbgbc_sb = stA.enter_context(nc.sbuf_tensor("dbgbc", [128, 1024], F32)) if debug else None

        B_xa = [Buf(), Buf()]
        B_xn, B_sq, B_wk, B_wv, B_ksb = Buf(), Buf(), Buf(), Buf(), Buf()
        B_wtmpA, B_xnh, B_xnl, B_xshl = Buf(), Buf(), Buf(), Buf()
        B_vsb = [Buf(), Buf()]
        B_esb = [Buf(), Buf()]
        B_rmx, B_dgs, B_stev, B_stt2 = Buf(), Buf(), Buf(), Buf()
        B_stev2 = Buf()
        B_dbgbc = Buf()

        S.do("sync", lambda e: e.dma_start(out=projs[:, :, :], in_=projP.rearrange("z p j -> p z j")),
             dma=True, writes=[B_projs])
        S.do("sync", lambda e: e.dma_start(out=smalls[:, :, :], in_=biasT.rearrange("(z p) r -> p z r", p=128)),
             dma=True, writes=[B_smalls])
        S.do("sync", lambda e: e.dma_start(out=bvrow[:, :], in_=bias_rows[2:3, :]),
             dma=True, writes=[B_bvrow])
        for nm, hi, lo, bb in (("wk", wkh, wkl, B_wk), ("wv", wvh, wvl, B_wv)):
            S.do("gpsimd", lambda e, nm=nm, hi=hi: e.dma_start(
                out=hi[:, :, :], in_=wT[nm].rearrange("(z p) d -> p z d", p=128)),
                dma=True, writes=[bb])
            wtmpA = k_sb.ap().rearrange("p z t -> p (z t)")[:, 0:D]
            for z in range(NZ):
                S.do("sync", lambda e, nm=nm, z=z, wtmpA=wtmpA: e.dma_start(
                    out=wtmpA, in_=wT[nm].rearrange("(z p) d -> p z d", p=128)[:, z, :]),
                    dma=True, writes=[B_wtmpA])
                S.do("vector", lambda e, z=z, hi=hi, lo=lo, wtmpA=wtmpA: e.tensor_tensor(
                    lo[:, z, :], wtmpA, hi[:, z, :], ALU.subtract),
                    reads=[B_wtmpA, bb], writes=[bb])

        S.do("gpsimd", lambda e: e.memset(ident[:, :], 0.0), writes=[B_ident])
        S.do("gpsimd", lambda e: e.affine_select(
            out=ident[:, :], in_=ident[:, :], compare_op=ALU.not_equal,
            fill=1.0, base=0, pattern=[[-1, 128]], channel_multiplier=1),
            writes=[B_ident])
        S.do("gpsimd", lambda e: e.memset(ndiag[:, :], 0.0), writes=[B_ndiag])
        S.do("gpsimd", lambda e: e.memset(ndiag[0:64, 0:64], -HDN2), writes=[B_ndiag])
        S.do("gpsimd", lambda e: e.memset(ndiag[64:128, 64:128], -HDN2), writes=[B_ndiag])
        S.do("gpsimd", lambda e: e.memset(hones[:, :], 0.0), writes=[B_hones])
        S.do("gpsimd", lambda e: e.memset(hones[0:64, 0:1], HDN2), writes=[B_hones])
        S.do("gpsimd", lambda e: e.memset(hones[64:128, 1:2], HDN2), writes=[B_hones])
        S.do("gpsimd", lambda e: e.memset(onescol[:, :], 1.0), writes=[B_ones])
        S.do("gpsimd", lambda e: e.memset(epscol[:, :], LN_EPS), writes=[B_eps])
        S.do("gpsimd", lambda e: e.memset(ones1[:, :], 1.0), writes=[B_ones1])
        S.do("gpsimd", lambda e: e.memset(payload[:, 0:RM0], 0.0), writes=[B_pay])
        S.do("gpsimd", lambda e: e.memset(payload[:, RM0:RM0 + 16], -3.0e38), writes=[B_pay])
        S.do("gpsimd", lambda e: e.memset(xnsum[:, :], 0.0), writes=[B_xnsum])
        S.do("gpsimd", lambda e: e.memset(vsrow[0:1, D:PAY], 0.0), writes=[B_vsrow])
        for i in range(2):
            S.do("gpsimd", lambda e, i=i: e.memset(v_sb[i][:, :, 128:129], 1.0), writes=[B_vsb[i]])

        S.do("vector", lambda e: e.tensor_scalar(
            projs.ap().rearrange("p z j -> p (z j)"), projs.ap().rearrange("p z j -> p (z j)"),
            DN, None, ALU.mult), reads=[B_projs], writes=[B_projs])
        S.do("vector", lambda e: e.tensor_scalar(
            mu1m[:, :, 0:1], smalls[:, :, 4:5], -1.0, 1.0, ALU.mult, ALU.add),
            reads=[B_smalls], writes=[B_mu1m])
        S.do("vector", lambda e: e.tensor_scalar(
            mu1m[:, :, 1:2], smalls[:, :, 5:6], -1.0, 1.0, ALU.mult, ALU.add),
            reads=[B_smalls], writes=[B_mu1m])
        for half in range(2):
            S.do("tensor", lambda e, h=half: e.matmul(
                pb[0][:, :], ones1[:, :], bvrow[0:1, 512 * h:512 * (h + 1)],
                start=True, stop=True), reads=[B_ones1, B_bvrow], writes=[PB[0]],
                mm=(PB[0], True, True, 512))
            S.do("vector", lambda e, h=half: e.tensor_copy(
                bvbc[:, 4 * h:4 * (h + 1), :],
                pb[0].ap().rearrange("p (a b) -> p a b", b=128)),
                reads=[PB[0]], writes=[B_bvbc])

        def _load_lnexp(e):
            from concourse.hw_specs import get_activation_tables
            for idx, funcs in enumerate(get_activation_tables(e.bass.m.arch).values()):
                if AF.Ln in funcs and AF.Exp in funcs:
                    return e.add_instruction(mybir.InstLoadActFuncSet(
                        name=e.bass.get_next_instruction_name(),
                        act_func_set_id=idx, ins=[], outs=[]))
            raise RuntimeError("no ln+exp table set")
        S.do("scalar", _load_lnexp, noinc=True)

        stats_ps = pb[0].ap().rearrange("p (a b) -> p a b", b=2)
        diag_ps = pb[0].ap().rearrange("p (a b) -> p a b", b=2)
        dd_ps = [pb[1].ap().rearrange("p (a b) -> p a b", b=128),
                 pb[2].ap().rearrange("p (a b) -> p a b", b=128)]
        a_ps = [pb[3][:, 0:387].rearrange("p (a b) -> p a b", b=129),
                pb[4][:, 0:387].rearrange("p (a b) -> p a b", b=129),
                pb[7][:, 0:258].rearrange("p (a b) -> p a b", b=129)]
        pay_v = payload[:, 0:RM0].rearrange("p (z c) -> p z c", c=129)

        def emit_load(m):
            xb2, B_xb2 = xa[m % 2], B_xa[m % 2]
            c0 = 1 + TM * m
            S.do("sync", lambda e, xb=xb2, c0=c0: e.dma_start(
                out=xb[:, :, :], in_=xT.rearrange("(z p) t -> p z t", p=128)[:, :, c0:c0 + TM]),
                dma=True, writes=[B_xb2])

        def emit_sq_stats(m):
            xb2, B_xb2 = xa[m % 2], B_xa[m % 2]
            S.do("scalar", lambda e, xb=xb2: e.activation(
                sq.ap().rearrange("p z t -> p (z t)"),
                xb.ap().rearrange("p z t -> p (z t)"), AF.Square),
                reads=[B_xb2], writes=[B_sq])
            for s in range(NSUB):
                for (q, src, bsrc) in ((0, xb2, B_xb2), (1, sq, B_sq)):
                    qb = 0 if q == 0 else 6
                    for z in range(NZ):
                        S.do("tensor", lambda e, s=s, z=z, q=q, src=src, qb=qb: e.matmul(
                            pb[qb][:, s:s + 1], src[:, z, 128 * s:128 * (s + 1)],
                            onescol[:, :], start=(z == 0), stop=(z == NZ - 1)),
                            reads=[bsrc, B_ones], writes=[PB[qb]],
                            mm=(PB[qb], z == 0, z == NZ - 1, 1))
            S.do("vector", lambda e: e.tensor_copy(
                stev[:, :, 0:1], pb[0][:, 0:4].rearrange("p (a b) -> p a b", b=1)),
                reads=[PB[0]], writes=[B_stev])
            S.do("vector", lambda e: e.tensor_copy(
                stev[:, :, 1:2], pb[6][:, 0:4].rearrange("p (a b) -> p a b", b=1)),
                reads=[PB[6]], writes=[B_stev])
            S.do("vector", lambda e, m=m: e.tensor_copy(
                stev2[:, m % 2, :, :], stev[:, :, :]),
                reads=[B_stev], writes=[B_stev2])

        emit_load(0)
        emit_sq_stats(0)
        for m in range(nmac):
            xb, B_xb = xa[m % 2], B_xa[m % 2]
            if m + 1 < nmac:
                emit_load(m + 1)
            S.do("vector", lambda e, m=m: e.tensor_scalar(
                stt2[:, :, 0:1], stev2[:, m % 2, :, 0:1], C1024, None, ALU.mult),
                reads=[B_stev2], writes=[B_stt2])
            S.do("vector", lambda e, m=m: e.tensor_scalar(
                stt2[:, :, 1:2], stev2[:, m % 2, :, 1:2], C1024, None, ALU.mult),
                reads=[B_stev2], writes=[B_stt2])
            S.do("vector", lambda e: e.tensor_tensor(
                stt2[:, :, 2:3], stt2[:, :, 0:1], stt2[:, :, 0:1], ALU.mult),
                reads=[B_stt2], writes=[B_stt2])
            S.do("vector", lambda e: e.tensor_tensor(
                stt2[:, :, 3:4], stt2[:, :, 1:2], stt2[:, :, 2:3], ALU.subtract),
                reads=[B_stt2], writes=[B_stt2])
            S.do("scalar", lambda e: e.activation(
                stt2[:, :, 1:2], stt2[:, :, 3:4], AF.Ln, bias=epscol[:, :]),
                reads=[B_stt2, B_eps], writes=[B_stt2])
            S.do("scalar", lambda e, m=m: e.activation(
                stats[:, NSUB * m:NSUB * (m + 1), 0:1], stt2[:, :, 1:2], AF.Exp, scale=-0.5),
                reads=[B_stt2], writes=[B_stats])
            S.do("vector", lambda e, m=m: e.tensor_tensor(
                stats[:, NSUB * m:NSUB * (m + 1), 1:2], stt2[:, :, 0:1],
                stats[:, NSUB * m:NSUB * (m + 1), 0:1], ALU.mult),
                reads=[B_stt2, B_stats], writes=[B_stats])

            if not SKIP_XN:
              for s in range(NSUB):
                ln_bcast(NSUB * m + s, 1, 2, s)
              for z in range(NZ):
                S.do("vector", lambda e, z=z, xb=xb: e.tensor_tensor(
                    xn[:, z, :], xb[:, z, :], pb[1][:, :], ALU.mult),
                    reads=[B_xb, PB[1]], writes=[B_xn], tag="applyA m%d z%d" % (m, z))
              if debug and m == 0:
                S.do("gpsimd", lambda e: e.dma_start(out=dbg["d_xm"][:, :], in_=xn.ap().rearrange("p z t -> p (z t)")), dma=True, reads=[B_xn])
                S.do("gpsimd", lambda e, xb=xb: e.dma_start(out=dbg["d_xa"][:, :], in_=xb.ap().rearrange("p z t -> p (z t)")), dma=True, reads=[B_xb])
              for z in range(NZ):
                S.do("vector", lambda e, z=z: e.tensor_tensor(
                    xn[:, z, :], xn[:, z, :], pb[2][:, :], ALU.subtract),
                    reads=[PB[2], B_xn], writes=[B_xn])
              S.do("vector", lambda e: e.tensor_copy(
                  xnh.ap().rearrange("p z t -> p (z t)"),
                  xn.ap().rearrange("p z t -> p (z t)")),
                  reads=[B_xn], writes=[B_xnh], tag="xnh m%d" % m)
              S.do("vector", lambda e: e.tensor_tensor(
                  xnl.ap().rearrange("p z t -> p (z t)"),
                  xn.ap().rearrange("p z t -> p (z t)"),
                  xnh.ap().rearrange("p z t -> p (z t)"), ALU.subtract),
                  reads=[B_xn, B_xnh], writes=[B_xnl], tag="xnl m%d" % m)

            if SKIP_SUBT:
                continue
            if debug and m == 0:
                S.do("vector", lambda e: e.tensor_copy(
                    dbgbc_sb[:, 0:512], pb[1][:, :]), reads=[PB[1]], writes=[B_dbgbc])
                S.do("vector", lambda e: e.tensor_copy(
                    dbgbc_sb[:, 512:1024], pb[2][:, :]), reads=[PB[2]], writes=[B_dbgbc])
                S.do("gpsimd", lambda e: e.dma_start(out=dbg["d_bc"][:, :], in_=dbgbc_sb[:, :]), dma=True, reads=[B_dbgbc])
                S.do("gpsimd", lambda e: e.dma_start(out=dbg["d_srow"][:, :], in_=strow.ap().rearrange("o c t -> o (c t)")), dma=True, reads=[B_strow])
            if m + 1 < nmac:
                emit_sq_stats(m + 1)
            S.do("vector", lambda e: e.reduce_sum(
                xns2.ap().rearrange("p (z o) -> p z o", o=1),
                xn.ap(), axis=AX), reads=[B_xn], writes=[B_xns2])
            S.do("vector", lambda e: e.tensor_tensor(
                xnsum[:, :], xnsum[:, :], xns2[:, :], ALU.add),
                reads=[B_xns2, B_xnsum], writes=[B_xnsum])

            for dz in range(NZ):
                bank = 3 + dz % 2
                for pi, (wop, xop, bwop, bxop) in enumerate(
                        ((wkh, xnh, B_wk, B_xnh), (wkh, xnl, B_wk, B_xnl),
                         (wkl, xnh, B_wk, B_xnh))):
                    for z in range(NZ):
                        st = (pi == 0 and z == 0)
                        sp = (pi == 2 and z == NZ - 1)
                        S.do("tensor", lambda e, dz=dz, z=z, bank=bank, wop=wop,
                             xop=xop, st=st, sp=sp: e.matmul(
                            pb[bank][:, :], wop[:, z, 128 * dz:128 * (dz + 1)],
                            xop[:, z, :], start=st, stop=sp),
                            reads=[bwop, bxop], writes=[PB[bank]],
                            mm=(PB[bank], st, sp, 512))
                S.do("vector", lambda e, dz=dz, bank=bank: e.tensor_scalar(
                    k_sb[:, dz, :], pb[bank][:, :], smalls[:, dz, 1:2], None, ALU.add),
                    reads=[PB[bank], B_smalls],
                    writes=[B_ksb, B_wtmpA] if m == 0 else [B_ksb])
            S.do("scalar", lambda e, xb=xb: e.activation(
                xb.ap().rearrange("p z t -> p (z t)"),
                k_sb.ap().rearrange("p z t -> p (z t)"), AF.Square),
                reads=[B_ksb], writes=[B_xb])

            for s in range(NSUB):
                vb, B_vb = v_sb[s % 2], B_vsb[s % 2]
                eb, B_eb = e_sb[s % 2], B_esb[s % 2]
                for half in range(2):
                    bank = 5 + half
                    for pi, (xop, wop, bxop, bwop) in enumerate(
                            ((xnh, wvh, B_xnh, B_wv), (xnh, wvl, B_xnh, B_wv),
                             (xnl, wvh, B_xnl, B_wv))):
                        for z in range(NZ):
                            st = (pi == 0 and z == 0)
                            sp = (pi == 2 and z == NZ - 1)
                            S.do("tensor", lambda e, s=s, z=z, half=half, bank=bank,
                                 xop=xop, wop=wop, st=st, sp=sp: e.matmul(
                                pb[bank][:, :], xop[:, z, 128 * s:128 * (s + 1)],
                                wop[:, z, 512 * half:512 * (half + 1)],
                                start=st, stop=sp),
                                reads=[bxop, bwop], writes=[PB[bank]],
                                mm=(PB[bank], st, sp, 512))
                for half in range(2):
                    bank = 5 + half
                    for zz in range(4):
                        z = 4 * half + zz
                        S.do("vector", lambda e, z=z, zz=zz, bank=bank, vb=vb: e.tensor_tensor(
                            vb[:, z, 0:128], pb[bank][:, 128 * zz:128 * (zz + 1)],
                            bvbc[:, z, :], ALU.add),
                            reads=[PB[bank], B_bvbc], writes=[B_vb])

                for z in (0, 4, 1, 5, 2, 6, 3, 7):
                    bank = 1 + z // 4
                    win = z % 4
                    S.do("tensor", lambda e, z=z, s=s, bank=bank, win=win: e.matmul(
                        dd_ps[bank - 1][:, win, :], k_sb[:, z, 128 * s:128 * (s + 1)],
                        projs[:, z, :], start=True, stop=False),
                        reads=[B_ksb, B_projs], writes=[PB[bank]],
                        mm=(PB[bank], True, False, 128))
                    S.do("tensor", lambda e, z=z, s=s, bank=bank, win=win, xb=xb: e.matmul(
                        dd_ps[bank - 1][:, win, :], xb[:, z, 128 * s:128 * (s + 1)],
                        ndiag[:, :], start=False, stop=True),
                        reads=[B_xb, B_ndiag], writes=[PB[bank]],
                        mm=(PB[bank], False, True, 128))
                for z in range(NZ):
                    S.do("tensor", lambda e, z=z, s=s, xb=xb: e.matmul(
                        diag_ps[:, 8 + z, :], xb[:, z, 128 * s:128 * (s + 1)],
                        hones[:, :], start=True, stop=True),
                        reads=[B_xb, B_hones], writes=[PB[0]],
                        mm=(PB[0], True, True, 2))

                S.do("vector", lambda e: e.reduce_max(
                    rmx[:, 0:8].rearrange("p (a b) -> p a b", b=1),
                    pb[1].ap().rearrange("p (a b) -> p a b", b=64), axis=AX),
                    reads=[PB[1]], writes=[B_rmx])
                S.do("vector", lambda e: e.reduce_max(
                    rmx[:, 8:16].rearrange("p (a b) -> p a b", b=1),
                    pb[2].ap().rearrange("p (a b) -> p a b", b=64), axis=AX),
                    reads=[PB[2]], writes=[B_rmx])
                S.do("vector", lambda e: e.tensor_copy(
                    dgs.ap().rearrange("p (a b) -> p a b", b=2),
                    diag_ps[:, 8:16, :]), reads=[PB[0]], writes=[B_dgs])
                S.do("vector", lambda e: e.tensor_tensor(
                    rmx[:, :], rmx[:, :], dgs[:, :], ALU.add),
                    reads=[B_dgs, B_rmx], writes=[B_rmx])
                S.do("vector", lambda e: e.tensor_tensor(
                    payload[:, RM0:RM0 + 16], payload[:, RM0:RM0 + 16],
                    rmx[:, :], ALU.max), reads=[B_rmx, B_pay], writes=[B_pay])

                for bank in (1, 2):
                    S.do("scalar", lambda e, bank=bank, eb=eb: e.activation(
                        eb[:, 512 * (bank - 1):512 * bank], pb[bank][:, :], AF.Exp),
                        reads=[PB[bank]], writes=[B_eb])

                for z in range(NZ):
                    bidx = z // 3 if z < 6 else 2
                    widx = z % 3 if z < 6 else z - 6
                    bank = (3, 4, 7)[bidx]
                    S.do("tensor", lambda e, z=z, eb=eb, vb=vb, bidx=bidx, widx=widx: e.matmul(
                        a_ps[bidx][:, widx, :], eb[:, 128 * z:128 * (z + 1)],
                        vb[:, z, :], start=True, stop=True),
                        reads=[B_eb, B_vb], writes=[PB[bank]],
                        mm=(PB[bank], True, True, 129))
                for bidx, (bank, z0, zn) in enumerate(((3, 0, 3), (4, 3, 6), (7, 6, 8))):
                    S.do("vector", lambda e, bidx=bidx, z0=z0, zn=zn: e.tensor_tensor(
                        pay_v[:, z0:zn, :], pay_v[:, z0:zn, :],
                        a_ps[bidx][:, 0:zn - z0, :], ALU.add),
                        reads=[PB[bank], B_pay], writes=[B_pay])

            if debug and m == 0:
                S.do("gpsimd", lambda e: e.dma_start(out=dbg["d_st0"][:, :], in_=stats[:, 0:4, :].rearrange("p g c -> p (g c)")), dma=True, reads=[B_stats])
                S.do("gpsimd", lambda e: e.dma_start(out=dbg["d_ev0"][:, :], in_=stev.ap().rearrange("p a c -> p (a c)")), dma=True, reads=[B_stev])
                S.do("gpsimd", lambda e: e.dma_start(out=dbg["d_t20"][:, :], in_=stt2.ap().rearrange("p a c -> p (a c)")), dma=True, reads=[B_stt2])
                S.do("gpsimd", lambda e: e.dma_start(out=dbg["d_xn"][:, :], in_=xn.ap().rearrange("p z t -> p (z t)")), dma=True, reads=[B_xn])
                S.do("gpsimd", lambda e: e.dma_start(out=dbg["d_ksb"][:, :], in_=k_sb.ap().rearrange("p z t -> p (z t)")), dma=True, reads=[B_ksb])
                S.do("gpsimd", lambda e: e.dma_start(out=dbg["d_e"][:, :], in_=e_sb[(NSUB - 1) % 2][:, :]), dma=True, reads=[B_esb[(NSUB - 1) % 2]])
                S.do("gpsimd", lambda e: e.dma_start(out=dbg["d_vsb"][:, :], in_=v_sb[(NSUB - 1) % 2].ap().rearrange("p z c -> p (z c)")), dma=True, reads=[B_vsb[(NSUB - 1) % 2]])

        S.do("vector", lambda e: e.tensor_copy(
            payload[:, RM0 + 16:PAY], xnsum[:, :]), reads=[B_xnsum], writes=[B_pay])
        S.do("vector", lambda e: e.tensor_copy(
            xsh[:, :], xnsum[:, :]), reads=[B_xnsum], writes=[B_xshl])
        S.do("vector", lambda e: e.tensor_tensor(
            xsl[:, :], xnsum[:, :], xsh[:, :], ALU.subtract),
            reads=[B_xnsum, B_xshl], writes=[B_xshl])
        for half in range(2):
            for pi, (xop, wop) in enumerate(((xsh, wvh), (xsh, wvl), (xsl, wvh))):
                for z in range(NZ):
                    st = (pi == 0 and z == 0)
                    sp = (pi == 2 and z == NZ - 1)
                    S.do("tensor", lambda e, half=half, z=z, xop=xop, wop=wop,
                         st=st, sp=sp: e.matmul(
                        pb[5][0:1, :], xop[:, z:z + 1], wop[:, z, 512 * half:512 * (half + 1)],
                        start=st, stop=sp),
                        reads=[B_xshl, B_wv], writes=[PB[5]],
                        mm=(PB[5], st, sp, 512))
            S.do("vector", lambda e, half=half: e.scalar_tensor_tensor(
                out=vsrow[0:1, 512 * half:512 * (half + 1)],
                in0=bvrow[0:1, 512 * half:512 * (half + 1)], scalar=float(ntok),
                in1=pb[5][0:1, :], op0=ALU.mult, op1=ALU.add),
                reads=[PB[5], B_bvrow], writes=[B_vsrow])

        S.do("gpsimd", lambda e: e.dma_start(out=cc_in[0:128, :], in_=payload[:, :]),
             dma=True, reads=[B_pay])
        S.do("gpsimd", lambda e: e.dma_start(out=cc_in[128:129, :], in_=vsrow[0:1, :]),
             dma=True, reads=[B_vsrow])
        cc_gd_deps = [(sm, S.counts[sm]) for sm in DMA_SEMS["gpsimd"] if S.counts[sm] > 0]

        if debug:
            S.do("gpsimd", lambda e: e.dma_start(out=dbg["d_pay"][:, :], in_=payload[:, :]), dma=True, reads=[B_pay])
            S.do("gpsimd", lambda e: e.dma_start(out=dbg["d_stats"][:, :], in_=stats.ap().rearrange("p g c -> p (g c)")), dma=True, reads=[B_stats])

        S.barrier(exempt=("gpsimd",))
        with nc.Block() as blockA:
            S.flush(blockA, sems)
        stA.close()

        stB = ExitStack()
        xbB = [stB.enter_context(nc.sbuf_tensor("xb%d" % i, [128, NZ, TM], F32)) for i in range(2)]
        wr_sb = stB.enter_context(nc.sbuf_tensor("wr_sb", [128, NZ, D], F32))
        wkf_sb = stB.enter_context(nc.sbuf_tensor("wkf_sb", [128, NZ, D], F32))
        wvf_sb = stB.enter_context(nc.sbuf_tensor("wvf_sb", [128, NZ, D], F32))
        xr_sb = stB.enter_context(nc.sbuf_tensor("xr_sb", [128, NZ, TM], F32))
        r_sb = stB.enter_context(nc.sbuf_tensor("r_sb", [128, NZ, TM], F32))
        kfb = stB.enter_context(nc.sbuf_tensor("kfb", [128, NZ, TM], F32))
        xpcol = stB.enter_context(nc.sbuf_tensor("xpcol", [128, NZ, 1], F32))
        xlast = stB.enter_context(nc.sbuf_tensor("xlast", [128, NZ, 1], F32))
        mls = stB.enter_context(nc.sbuf_tensor("mls", [1, 4], F32))
        sqc = stB.enter_context(nc.sbuf_tensor("sqc", [128, NZ, 1], F32))

        B_xbB = [Buf(), Buf()]
        (B_wr, B_wkf, B_wvf, B_xr, B_r, B_kfb, B_xpc, B_xlast, B_mls,
         B_sqc) = (Buf() for _ in range(10))

        S.do("gpsimd", lambda e: e.collective_compute(
            "AllGather", ALU.bypass, replica_groups=groups,
            ins=[cc_in.opt()], outs=[cc_out.opt()]),
            deps=cc_gd_deps, sem="cc", inc=1)
        cc_tok = ("cc", S.counts["cc"])

        S.do("sync", lambda e: e.dma_start(
            out=xpcol[:, :, :], in_=xT.rearrange("(z p) t -> p z t", p=128)[:, :, 0:1],
            allow_slow_non_contiguous=True), dma=True, writes=[B_xpc])
        S.do("sync", lambda e: e.dma_start(
            out=xbB[0][:, :, :], in_=xT.rearrange("(z p) t -> p z t", p=128)[:, :, 1:1 + TM]),
            dma=True, writes=[B_xbB[0]])
        for nm, sb, bb in (("wr", wr_sb, B_wr), ("wkf", wkf_sb, B_wkf), ("wvf", wvf_sb, B_wvf)):
            S.do("sync", lambda e, nm=nm, sb=sb: e.dma_start(
                out=sb[:, :, :], in_=wT[nm].rearrange("(z p) d -> p z d", p=128)),
                dma=True, writes=[bb])

        S.do("vector", lambda e: e.tensor_tensor(
            sqc[:, :, :], xpcol[:, :, :], xpcol[:, :, :], ALU.mult),
            reads=[B_xpc], writes=[B_sqc])
        for (q, src, bsrc) in ((0, xpcol, B_xpc), (1, sqc, B_sqc)):
            for z in range(NZ):
                S.do("tensor", lambda e, q=q, z=z, src=src: e.matmul(
                    pb[0][0:1, q:q + 1], src[:, z, :], onescol[:, :],
                    start=(z == 0), stop=(z == NZ - 1)),
                    reads=[bsrc, B_ones], writes=[PB[0]],
                    mm=(PB[0], z == 0, z == NZ - 1, 1))
        S.do("vector", lambda e: e.tensor_scalar(
            mls[0:1, 0:2], pb[0][0:1, 0:2], C1024, None, ALU.mult),
            reads=[PB[0]], writes=[B_mls])
        S.do("vector", lambda e: e.tensor_tensor(
            mls[0:1, 2:3], mls[0:1, 0:1], mls[0:1, 0:1], ALU.mult),
            reads=[B_mls], writes=[B_mls])
        S.do("vector", lambda e: e.tensor_tensor(
            mls[0:1, 3:4], mls[0:1, 1:2], mls[0:1, 2:3], ALU.subtract),
            reads=[B_mls], writes=[B_mls])
        S.do("scalar", lambda e: e.activation(
            mls[0:1, 2:3], mls[0:1, 3:4], AF.Ln, bias=epscol[0:1, :]),
            reads=[B_mls, B_eps], writes=[B_mls])
        S.do("scalar", lambda e: e.activation(
            mls[0:1, 3:4], mls[0:1, 2:3], AF.Exp, scale=-0.5),
            reads=[B_mls], writes=[B_mls])
        S.do("tensor", lambda e: e.matmul(
            pb[1][:, 0:4], ones1[:, :], mls[0:1, 0:4], start=True, stop=True),
            reads=[B_ones1, B_mls], writes=[PB[1]], mm=(PB[1], True, True, 4))
        S.do("vector", lambda e: e.tensor_tensor(
            xpcol[:, :, :], xpcol[:, :, :],
            bass.AP(pb[1], 0, [[512, 128], [0, NZ], [1, 1]]), ALU.subtract),
            reads=[PB[1], B_xpc], writes=[B_xpc])
        S.do("vector", lambda e: e.tensor_tensor(
            xpcol[:, :, :], xpcol[:, :, :],
            bass.AP(pb[1], 3, [[512, 128], [0, NZ], [1, 1]]), ALU.mult),
            reads=[PB[1], B_xpc], writes=[B_xpc])

        S.do("scalar", lambda e: e.preload_act_table(AF.Tanh), noinc=True)

        def b_load(m):
            xb, B_xb = xbB[m % 2], B_xbB[m % 2]
            c0 = 1 + TM * m
            S.do("sync", lambda e, xb=xb, c0=c0: e.dma_start(
                out=xb[:, :, :], in_=xT.rearrange("(z p) t -> p z t", p=128)[:, :, c0:c0 + TM]),
                dma=True, writes=[B_xb])

        def b_apply_xr(m):
            xb, B_xb = xbB[m % 2], B_xbB[m % 2]
            for s in range(NSUB):
                ln_bcast(NSUB * m + s, 1, 2, s)
            for z in range(NZ):
                S.do("vector", lambda e, z=z, xb=xb: e.tensor_tensor(
                    xb[:, z, :], xb[:, z, :], pb[1][:, :], ALU.mult),
                    reads=[PB[1], B_xb], writes=[B_xb])
            for z in range(NZ):
                S.do("vector", lambda e, z=z, xb=xb: e.tensor_tensor(
                    xb[:, z, :], xb[:, z, :], pb[2][:, :], ALU.subtract),
                    reads=[PB[2], B_xb], writes=[B_xb])
            S.do("vector", lambda e, xb=xb: e.tensor_tensor(
                xr_sb[:, :, 1:TM], xb[:, :, 0:TM - 1],
                bass.AP(mu1m, 0, [[16, 128], [2, NZ], [0, TM - 1]]), ALU.mult),
                reads=[B_xb, B_mu1m], writes=[B_xr], tag="mix1 m%d" % m)
            for z in range(NZ):
                S.do("vector", lambda e, z=z, xb=xb: e.scalar_tensor_tensor(
                    out=xr_sb[:, z, 1:TM], in0=xb[:, z, 1:TM], scalar=smalls[:, z, 4:5],
                    in1=xr_sb[:, z, 1:TM], op0=ALU.mult, op1=ALU.add),
                    reads=[B_xb, B_smalls, B_xr], writes=[B_xr])

        def b_xr_bound(m):
            xb, B_xb = xbB[m % 2], B_xbB[m % 2]
            prevc, B_prevc = (xpcol, B_xpc) if m == 0 else (xlast, B_xlast)
            S.do("vector", lambda e, prevc=prevc, B_prevc=B_prevc: e.tensor_tensor(
                xr_sb[:, :, 0:1], prevc[:, :, :], mu1m[:, :, 0:1], ALU.mult),
                reads=[B_prevc, B_mu1m], writes=[B_xr])
            for z in range(NZ):
                S.do("vector", lambda e, z=z, xb=xb: e.scalar_tensor_tensor(
                    out=xr_sb[:, z, 0:1], in0=xb[:, z, 0:1], scalar=smalls[:, z, 4:5],
                    in1=xr_sb[:, z, 0:1], op0=ALU.mult, op1=ALU.add),
                    reads=[B_xb, B_smalls, B_xr], writes=[B_xr])

        def b_xkmix(m):
            xb, B_xb = xbB[m % 2], B_xbB[m % 2]
            prevc, B_prevc = (xpcol, B_xpc) if m == 0 else (xlast, B_xlast)
            S.do("vector", lambda e, xb=xb: e.tensor_tensor(
                kfb[:, :, 1:TM], xb[:, :, 0:TM - 1],
                bass.AP(mu1m, 1, [[16, 128], [2, NZ], [0, TM - 1]]), ALU.mult),
                reads=[B_xb, B_mu1m], writes=[B_kfb])
            S.do("vector", lambda e, prevc=prevc, B_prevc=B_prevc: e.tensor_tensor(
                kfb[:, :, 0:1], prevc[:, :, :], mu1m[:, :, 1:2], ALU.mult),
                reads=[B_prevc, B_mu1m], writes=[B_kfb])
            S.do("vector", lambda e, xb=xb: e.tensor_copy(
                xlast[:, :, :], xb[:, :, TM - 1:TM]), reads=[B_xb], writes=[B_xlast])
            for z in range(NZ):
                S.do("vector", lambda e, z=z, xb=xb: e.scalar_tensor_tensor(
                    out=xb[:, z, :], in0=xb[:, z, :], scalar=smalls[:, z, 5:6],
                    in1=kfb[:, z, :], op0=ALU.mult, op1=ALU.add),
                    reads=[B_smalls, B_kfb, B_xb], writes=[B_xb])

        b_apply_xr(0)
        b_xr_bound(0)
        b_xkmix(0)
        for m in range(nmac):
            xb, B_xb = xbB[m % 2], B_xbB[m % 2]

            if debug and m == 1:
                S.do("gpsimd", lambda e: e.dma_start(out=dbg["d_xl"][:, :], in_=xlast.ap().rearrange("p z o -> p (z o)"), allow_slow_non_contiguous=True), dma=True, reads=[B_xlast])
                S.do("gpsimd", lambda e: e.dma_start(out=dbg["d_xr0"][:, :], in_=xr_sb[:, :, 0:1].rearrange("p z o -> p (z o)"), allow_slow_non_contiguous=True), dma=True, reads=[B_xr])
                S.do("gpsimd", lambda e, xb=xb: e.dma_start(out=dbg["d_xk0"][:, :], in_=xb[:, :, 0:1].rearrange("p z o -> p (z o)"), allow_slow_non_contiguous=True), dma=True, reads=[B_xb])
            for dz in range(NZ):
                bank = 3 + dz % 2
                for z in range(NZ):
                    S.do("tensor", lambda e, dz=dz, z=z, bank=bank: e.matmul(
                        pb[bank][:, :], wr_sb[:, z, 128 * dz:128 * (dz + 1)],
                        xr_sb[:, z, :], start=(z == 0), stop=(z == NZ - 1)),
                        reads=[B_wr, B_xr], writes=[PB[bank]],
                        mm=(PB[bank], z == 0, z == NZ - 1, 512))
                S.do("scalar", lambda e, dz=dz, bank=bank: e.activation(
                    r_sb[:, dz, :], pb[bank][:, :], AF.Tanh, scale=0.5),
                    reads=[PB[bank]], writes=[B_r])

            if m + 1 < nmac:
                b_load(m + 1)
                b_apply_xr(m + 1)

            for dz in range(NZ):
                bank = 3 + dz % 2
                for z in range(NZ):
                    S.do("tensor", lambda e, dz=dz, z=z, bank=bank, xb=xb: e.matmul(
                        pb[bank][:, :], wkf_sb[:, z, 128 * dz:128 * (dz + 1)],
                        xb[:, z, :], start=(z == 0), stop=(z == NZ - 1)),
                        reads=[B_wkf, B_xb], writes=[PB[bank]],
                        mm=(PB[bank], z == 0, z == NZ - 1, 512))
                S.do("scalar", lambda e, dz=dz, bank=bank: e.activation(
                    kfb[:, dz, :], pb[bank][:, :], AF.Relu),
                    reads=[PB[bank]], writes=[B_kfb])
                S.do("scalar", lambda e, dz=dz: e.activation(
                    kfb[:, dz, :], kfb[:, dz, :], AF.Square),
                    reads=[B_kfb], writes=[B_kfb])

            for dz in range(NZ):
                bank = 3 + dz % 2
                for z in range(NZ):
                    S.do("tensor", lambda e, dz=dz, z=z, bank=bank: e.matmul(
                        pb[bank][:, :], wvf_sb[:, z, 128 * dz:128 * (dz + 1)],
                        kfb[:, z, :], start=(z == 0), stop=(z == NZ - 1)),
                        reads=[B_wvf, B_kfb], writes=[PB[bank]],
                        mm=(PB[bank], z == 0, z == NZ - 1, 512))
                if dz == 0:
                    S.do("vector", lambda e: e.tensor_scalar(
                        r_sb.ap().rearrange("p z t -> p (z t)"),
                        r_sb.ap().rearrange("p z t -> p (z t)"),
                        0.5, 0.5, ALU.mult, ALU.add), reads=[B_r], writes=[B_r])
                S.do("vector", lambda e, dz=dz, bank=bank: e.tensor_tensor(
                    r_sb[:, dz, :], pb[bank][:, :], r_sb[:, dz, :], ALU.mult),
                    reads=[PB[bank], B_r], writes=[B_r], tag="ffnevac dz%d" % dz)
            if not SKIP_FFN_OUT:
                S.do("gpsimd", lambda e, m=m: e.dma_start(
                    out=out.rearrange("(z p) t -> p z t", p=128)[:, :, TM * m:TM * (m + 1)],
                    in_=r_sb[:, :, :]), dma=True, reads=[B_r], writes=[B_out],
                    tag="ffnout m%d" % m)
            if m + 1 < nmac:
                b_xr_bound(m + 1)
                b_xkmix(m + 1)

        S.barrier()
        with nc.Block() as blockB:
            S.flush(blockB, sems)
        stB.close()

        stC = ExitStack()
        xbC = [stC.enter_context(nc.sbuf_tensor("xc%d" % i, [128, NZ, TM], F32)) for i in range(2)]
        xtC = stC.enter_context(nc.sbuf_tensor("xtC", [128, TM], F32))
        xnCh = stC.enter_context(nc.sbuf_tensor("xnCh", [128, NZ, TM], BF))
        xnCl = stC.enter_context(nc.sbuf_tensor("xnCl", [128, NZ, TM], BF))
        wqh = stC.enter_context(nc.sbuf_tensor("wqh", [128, NZ, D], BF))
        wql = stC.enter_context(nc.sbuf_tensor("wql", [128, NZ, D], BF))
        woh = stC.enter_context(nc.sbuf_tensor("woh", [128, NZ, D], BF))
        wol = stC.enter_context(nc.sbuf_tensor("wol", [128, NZ, D], BF))
        q_sb = stC.enter_context(nc.sbuf_tensor("q_sb", [128, NZ, TM], F32))
        sqq = stC.enter_context(nc.sbuf_tensor("sqq", [128, NZ, 128], F32))
        qp = [stC.enter_context(nc.sbuf_tensor("qp%d" % i, [128, 1024], F32)) for i in range(2)]
        qf_sb = stC.enter_context(nc.sbuf_tensor("qf_sb", [128, NZ, 128], F32))
        ctx_sb = stC.enter_context(nc.sbuf_tensor("ctx_sb", [128, 1024], F32))
        ctxfh = stC.enter_context(nc.sbuf_tensor("ctxfh", [128, NZ, TM], BF))
        ctxfl = stC.enter_context(nc.sbuf_tensor("ctxfl", [128, NZ, TM], BF))
        gat1 = stC.enter_context(nc.sbuf_tensor("gat1", [128, PAY], F32))
        vs2 = stC.enter_context(nc.sbuf_tensor("vs2", [1, D], F32))
        m16 = stC.enter_context(nc.sbuf_tensor("m16", [16, 128], F32))
        mred = stC.enter_context(nc.sbuf_tensor("mred", [16, 4], F32))
        srw = stC.enter_context(nc.sbuf_tensor("srw", [1, 16], F32))
        nrm = stC.enter_context(nc.sbuf_tensor("nrm", [128, 16], F32))
        rmq = stC.enter_context(nc.sbuf_tensor("rmq", [128, 16], F32))
        dgq = stC.enter_context(nc.sbuf_tensor("dgq", [128, 16], F32))
        epre = stC.enter_context(nc.sbuf_tensor("epre", [128, 1024], F32))

        B_xbC = [Buf(), Buf()]
        (B_xnC, B_wq, B_wo, B_q, B_sqq, B_qf, B_ctx, B_ctxf, B_gat1, B_vs2,
         B_m16, B_mred, B_srw, B_nrm, B_rmq, B_dgq, B_epre) = (Buf() for _ in range(17))
        B_xtC, B_xnChl, B_wtmpC = Buf(), Buf(), Buf()
        B_qp = [Buf(), Buf()]

        wtmpC = q_sb.ap().rearrange("p z t -> p (z t)")[:, 0:D]
        for nm, hi, lo, bb in (("wq", wqh, wql, B_wq), ("wo", woh, wol, B_wo)):
            S.do("gpsimd", lambda e, nm=nm, hi=hi: e.dma_start(
                out=hi[:, :, :], in_=wT[nm].rearrange("(z p) d -> p z d", p=128)),
                dma=True, writes=[bb], tag="wload_%s" % nm)
            for z in range(NZ):
                S.do("sync", lambda e, nm=nm, z=z: e.dma_start(
                    out=wtmpC, in_=wT[nm].rearrange("(z p) d -> p z d", p=128)[:, z, :]),
                    dma=True, writes=[B_wtmpC])
                S.do("vector", lambda e, z=z, hi=hi, lo=lo: e.tensor_tensor(
                    lo[:, z, :], wtmpC, hi[:, z, :], ALU.subtract),
                    reads=[B_wtmpC, bb], writes=[bb])

        S.do("gpsimd", lambda e: e.dma_start(out=payload[:, :], in_=cc_out[0:128, :]),
            dma=True, deps=[cc_tok], writes=[B_pay])
        S.do("gpsimd", lambda e: e.dma_start(out=gat1[:, :], in_=cc_out[129:257, :]),
            dma=True, deps=[cc_tok], writes=[B_gat1])
        S.do("gpsimd", lambda e: e.dma_start(out=vsrow[0:1, 0:D], in_=cc_out[128:129, 0:D]),
            dma=True, deps=[cc_tok], writes=[B_vsrow])
        S.do("gpsimd", lambda e: e.dma_start(out=vs2[:, :], in_=cc_out[257:258, 0:D]),
            dma=True, deps=[cc_tok], writes=[B_vs2])
        S.do("vector", lambda e: e.tensor_tensor(
            payload[:, 0:RM0], payload[:, 0:RM0], gat1[:, 0:RM0], ALU.add),
            reads=[B_gat1, B_pay], writes=[B_pay])
        S.do("vector", lambda e: e.tensor_tensor(
            payload[:, RM0:RM0 + 16], payload[:, RM0:RM0 + 16],
            gat1[:, RM0:RM0 + 16], ALU.max), reads=[B_gat1, B_pay], writes=[B_pay])
        S.do("vector", lambda e: e.tensor_tensor(
            vsrow[0:1, 0:D], vsrow[0:1, 0:D], vs2[:, :], ALU.add),
            reads=[B_vs2, B_vsrow], writes=[B_vsrow])

        S.do("tensor", lambda e: e.transpose(
            pb[0][0:16, 0:128], payload[:, RM0:RM0 + 16], ident[:, :]),
            reads=[B_pay, B_ident], writes=[PB[0]], mm=(PB[0], True, True, 128))
        S.do("vector", lambda e: e.tensor_copy(m16[:, :], pb[0][0:16, 0:128]),
            reads=[PB[0]], writes=[B_m16])
        S.do("vector", lambda e: e.reduce_max(mred[0:16, 0:1], m16[0:16, :], axis=AX),
            reads=[B_m16], writes=[B_mred])
        S.do("scalar", lambda e: e.activation(mred[0:16, 1:2], mred[0:16, 0:1], AF.Exp, scale=-1.0),
            reads=[B_mred], writes=[B_mred])
        S.do("vector", lambda e: e.tensor_scalar(
            mred[0:16, 2:3], mred[0:16, 1:2], RATIO, None, ALU.mult),
            reads=[B_mred], writes=[B_mred])
        S.do("tensor", lambda e: e.transpose(pb[0][0:1, 128:144], mred[0:16, 2:3], ident[0:16, 0:16]),
            reads=[B_mred, B_ident], writes=[PB[0]], mm=(PB[0], True, True, 16))
        S.do("vector", lambda e: e.tensor_copy(srw[:, :], pb[0][0:1, 128:144]),
            reads=[PB[0]], writes=[B_srw])
        S.do("tensor", lambda e: e.matmul(pb[0][:, 256:272], ones1[:, :], srw[0:1, :],
                                        start=True, stop=True),
            reads=[B_ones1, B_srw], writes=[PB[0]], mm=(PB[0], True, True, 16))
        S.do("vector", lambda e: e.tensor_copy(sall[:, :], pb[0][:, 256:272]),
            reads=[PB[0]], writes=[B_sall])

        S.do("gpsimd", lambda e: e.memset(kvr[:, :, :], 0.0), writes=[B_kvr])
        EPSN = float(np.float32(EPS_FEAT) * 2 * ntok)
        for z in range(NZ):
            S.do("vector", lambda e, z=z: e.tensor_scalar(
                kvr[0:64, z, 0:64], pay_v[0:64, z, 0:64], sall[0:64, 2 * z:2 * z + 1],
                None, ALU.mult), reads=[B_pay, B_sall], writes=[B_kvr])
            S.do("vector", lambda e, z=z: e.tensor_scalar(
                kvr[64:128, z, 64:128], pay_v[64:128, z, 64:128],
                sall[64:128, 2 * z + 1:2 * z + 2], None, ALU.mult),
                reads=[B_pay, B_sall], writes=[B_kvr])
            S.do("tensor", lambda e, z=z: e.matmul(
                pb[1][:, 0:128], ones1[:, :], vsrow[0:1, 128 * z:128 * (z + 1)],
                start=True, stop=True), reads=[B_ones1, B_vsrow], writes=[PB[1]],
                mm=(PB[1], True, True, 128))
            S.do("vector", lambda e, z=z: e.scalar_tensor_tensor(
                out=kvr[0:64, z, 0:64], in0=pb[1][0:64, 0:64], scalar=EPS_FEAT,
                in1=kvr[0:64, z, 0:64], op0=ALU.mult, op1=ALU.add),
                reads=[PB[1], B_kvr], writes=[B_kvr])
            S.do("vector", lambda e, z=z: e.scalar_tensor_tensor(
                out=kvr[64:128, z, 64:128], in0=pb[1][64:128, 64:128], scalar=EPS_FEAT,
                in1=kvr[64:128, z, 64:128], op0=ALU.mult, op1=ALU.add),
                reads=[PB[1], B_kvr], writes=[B_kvr])
            S.do("vector", lambda e, z=z: e.tensor_scalar(
                kvr[0:64, z, 128:129], pay_v[0:64, z, 128:129],
                sall[0:64, 2 * z:2 * z + 1], EPSN, ALU.mult, ALU.add),
                reads=[B_pay, B_sall], writes=[B_kvr])
            S.do("vector", lambda e, z=z: e.tensor_scalar(
                kvr[64:128, z, 129:130], pay_v[64:128, z, 128:129],
                sall[64:128, 2 * z + 1:2 * z + 2], EPSN, ALU.mult, ALU.add),
                reads=[B_pay, B_sall], writes=[B_kvr])

        if debug:
            S.do("gpsimd", lambda e: e.dma_start(out=dbg["d_comb"][:, :], in_=payload[:, :]), dma=True, reads=[B_pay])
            S.do("gpsimd", lambda e: e.dma_start(out=dbg["d_kv"][:, :], in_=kvr.ap().rearrange("p z c -> p (z c)")), dma=True, reads=[B_kvr])
            S.do("gpsimd", lambda e: e.dma_start(out=dbg["d_sall"][:, :], in_=sall[:, :]), dma=True, reads=[B_sall])
            S.do("gpsimd", lambda e: e.dma_start(out=dbg["d_vsum"][:, :], in_=vsrow[0:1, 0:D]), dma=True, reads=[B_vsrow])

        ddq_ps = dd_ps
        dgq_ps = pb[0].ap().rearrange("p (a b) -> p a b", b=2)
        qk_ps = [pb[7][:, 0:390].rearrange("p (a b) -> p a b", b=130),
                pb[0][:, 0:390].rearrange("p (a b) -> p a b", b=130),
                pb[3][:, 0:260].rearrange("p (a b) -> p a b", b=130)]

        def c_load(m):
            xb, B_xb = xbC[m % 2], B_xbC[m % 2]
            c0 = 1 + TM * m
            S.do("sync", lambda e, xb=xb, c0=c0: e.dma_start(
                out=xb[:, :, :], in_=xT.rearrange("(z p) t -> p z t", p=128)[:, :, c0:c0 + TM]),
                dma=True, writes=[B_xb])

        def c_apply(m):
            xb, B_xb = xbC[m % 2], B_xbC[m % 2]
            for s in range(NSUB):
                ln_bcast(NSUB * m + s, 1, 2, s)
            for z in range(NZ):
                S.do("vector", lambda e, z=z, xb=xb: e.tensor_tensor(
                    xtC[:, :], xb[:, z, :], pb[1][:, :], ALU.mult),
                    reads=[B_xb, PB[1]], writes=[B_xtC])
                S.do("vector", lambda e, z=z: e.tensor_tensor(
                    xtC[:, :], xtC[:, :], pb[2][:, :], ALU.subtract),
                    reads=[PB[2], B_xtC], writes=[B_xtC])
                S.do("vector", lambda e, z=z: e.tensor_copy(
                    xnCh[:, z, :], xtC[:, :]), reads=[B_xtC], writes=[B_xnChl])
                S.do("vector", lambda e, z=z: e.tensor_tensor(
                    xnCl[:, z, :], xtC[:, :], xnCh[:, z, :], ALU.subtract),
                    reads=[B_xtC, B_xnChl], writes=[B_xnChl])

        if not SKIP_C:
            c_load(0)
            c_apply(0)
        for m in (range(nmac) if not SKIP_C else []):
            xb, B_xb = xbC[m % 2], B_xbC[m % 2]
            for dz in range(NZ):
                bank = 3 + dz % 2
                for pi, (wop, xop) in enumerate(
                        ((wqh, xnCh), (wqh, xnCl), (wql, xnCh))):
                    for z in range(NZ):
                        st = (pi == 0 and z == 0)
                        sp = (pi == 2 and z == NZ - 1)
                        S.do("tensor", lambda e, dz=dz, z=z, bank=bank, wop=wop,
                             xop=xop, st=st, sp=sp: e.matmul(
                            pb[bank][:, :], wop[:, z, 128 * dz:128 * (dz + 1)],
                            xop[:, z, :], start=st, stop=sp),
                            reads=[B_wq, B_xnChl], writes=[PB[bank]],
                            mm=(PB[bank], st, sp, 512))
                S.do("vector", lambda e, dz=dz, bank=bank: e.tensor_scalar(
                    q_sb[:, dz, :], pb[bank][:, :], smalls[:, dz, 0:1], None, ALU.add),
                    reads=[PB[bank], B_smalls],
                    writes=[B_q, B_wtmpC] if m == 0 else [B_q])

            def emit_pre(s):
                S.do("vector", lambda e, s=s: e.tensor_tensor(
                    sqq[:, :, :], q_sb[:, :, 128 * s:128 * (s + 1)],
                    q_sb[:, :, 128 * s:128 * (s + 1)], ALU.mult),
                    reads=[B_q], writes=[B_sqq])
                for z in (0, 4, 1, 5, 2, 6, 3, 7):
                    bank = 1 + z // 4
                    win = z % 4
                    S.do("tensor", lambda e, z=z, s=s, bank=bank, win=win: e.matmul(
                        ddq_ps[bank - 1][:, win, :], q_sb[:, z, 128 * s:128 * (s + 1)],
                        projs[:, z, :], start=True, stop=True),
                        reads=[B_q, B_projs], writes=[PB[bank]],
                        mm=(PB[bank], True, True, 128))
                for z in range(NZ):
                    S.do("tensor", lambda e, z=z: e.matmul(
                        dgq_ps[:, 200 + z, :], sqq[:, z, :], hones[:, :],
                        start=True, stop=True),
                        reads=[B_sqq, B_hones], writes=[PB[0]],
                        mm=(PB[0], True, True, 2))

            emit_pre(0)
            for s in range(NSUB):
                qb, B_qb = qp[s % 2], B_qp[s % 2]
                S.do("vector", lambda e: e.reduce_max(
                    rmq[:, 0:8].rearrange("p (a b) -> p a b", b=1),
                    pb[1].ap().rearrange("p (a b) -> p a b", b=64), axis=AX),
                    reads=[PB[1]], writes=[B_rmq])
                S.do("vector", lambda e: e.reduce_max(
                    rmq[:, 8:16].rearrange("p (a b) -> p a b", b=1),
                    pb[2].ap().rearrange("p (a b) -> p a b", b=64), axis=AX),
                    reads=[PB[2]], writes=[B_rmq])
                S.do("vector", lambda e: e.tensor_copy(
                    dgq.ap().rearrange("p (a b) -> p a b", b=2),
                    dgq_ps[:, 200:208, :]), reads=[PB[0]], writes=[B_dgq])
                S.do("vector", lambda e: e.tensor_tensor(
                    rmq[:, :], rmq[:, :], dgq[:, :], ALU.add),
                    reads=[B_dgq, B_rmq], writes=[B_rmq])
                for z in range(NZ):
                    bank = 1 + z // 4
                    win = z % 4
                    for h2 in range(2):
                        S.do("vector", lambda e, z=z, win=win, h2=h2, bank=bank: e.tensor_scalar(
                            epre[:, 128 * z + 64 * h2:128 * z + 64 * (h2 + 1)],
                            ddq_ps[bank - 1][:, win, 64 * h2:64 * (h2 + 1)],
                            rmq[:, 2 * z + h2:2 * z + h2 + 1], None, ALU.subtract),
                            reads=[PB[bank], B_rmq], writes=[B_epre])
                if s + 1 < NSUB:
                    emit_pre(s + 1)
                S.do("scalar", lambda e, qb=qb: e.activation(
                    qb[:, :], epre[:, :], AF.Exp), reads=[B_epre], writes=[B_qb])
                S.do("vector", lambda e, qb=qb: e.tensor_scalar(
                    qb[:, :], qb[:, :], RATIO, EPS_FEAT, ALU.mult, ALU.add),
                    reads=[B_qb], writes=[B_qb])
                for z in range(NZ):
                    bank = 5 + z // 4
                    win = z % 4
                    S.do("tensor", lambda e, z=z, bank=bank, win=win, qb=qb: e.transpose(
                        pb[bank][:, 128 * win:128 * (win + 1)], qb[:, 128 * z:128 * (z + 1)],
                        ident[:, :]), reads=[B_qb, B_ident], writes=[PB[bank]],
                        mm=(PB[bank], True, True, 128))
                for bi, bank in enumerate((5, 6)):
                    S.do("vector", lambda e, bi=bi, bank=bank: e.tensor_copy(
                        qf_sb[:, 4 * bi:4 * (bi + 1), :],
                        pb[bank].ap().rearrange("p (a b) -> p a b", b=128)),
                        reads=[PB[bank]], writes=[B_qf])
                for z in range(NZ):
                    bidx = z // 3 if z < 6 else 2
                    widx = z % 3 if z < 6 else z - 6
                    S.do("tensor", lambda e, z=z, bidx=bidx, widx=widx: e.matmul(
                        qk_ps[bidx][:, widx, :], qf_sb[:, z, :], kvr[:, z, :],
                        start=True, stop=True),
                        reads=[B_qf, B_kvr], writes=[PB[(7, 0, 3)[bidx]]],
                        mm=(PB[(7, 0, 3)[bidx]], True, True, 130))
                for bidx, (bank, z0, zn) in enumerate(((7, 0, 3), (0, 3, 6), (3, 6, 8))):
                    S.do("vector", lambda e, bidx=bidx, z0=z0, zn=zn: e.tensor_copy(
                        nrm[:, 2 * z0:2 * zn].rearrange("p (a b) -> p a b", b=2),
                        qk_ps[bidx][:, 0:zn - z0, 128:130]),
                        reads=[PB[bank]], writes=[B_nrm])
                S.do("vector", lambda e: e.tensor_scalar(
                    nrm[:, :], nrm[:, :], EPS_DEN, None, ALU.max),
                    reads=[B_nrm], writes=[B_nrm])
                S.do("vector", lambda e: e.reciprocal(nrm[:, :], nrm[:, :]),
                    reads=[B_nrm], writes=[B_nrm])
                for z in range(NZ):
                    bidx = z // 3 if z < 6 else 2
                    widx = z % 3 if z < 6 else z - 6
                    bank = (7, 0, 3)[bidx]
                    for h2 in range(2):
                        S.do("vector", lambda e, z=z, bidx=bidx, widx=widx, h2=h2, bank=bank: e.tensor_scalar(
                            ctx_sb[:, 128 * z + 64 * h2:128 * z + 64 * (h2 + 1)],
                            qk_ps[bidx][:, widx, 64 * h2:64 * (h2 + 1)],
                            nrm[:, 2 * z + h2:2 * z + h2 + 1], None, ALU.mult),
                            reads=[PB[bank], B_nrm], writes=[B_ctx])
                for z in range(NZ):
                    bank = 5 + z // 4
                    win = z % 4
                    S.do("tensor", lambda e, z=z, bank=bank, win=win: e.transpose(
                        pb[bank][:, 128 * win:128 * (win + 1)], ctx_sb[:, 128 * z:128 * (z + 1)],
                        ident[:, :]), reads=[B_ctx, B_ident], writes=[PB[bank]],
                        mm=(PB[bank], True, True, 128))
                for bi, bank in enumerate((5, 6)):
                    S.do("vector", lambda e, bi=bi, bank=bank, s=s: e.tensor_copy(
                        ctxfh[:, 4 * bi:4 * (bi + 1), 128 * s:128 * (s + 1)],
                        pb[bank].ap().rearrange("p (a b) -> p a b", b=128)),
                        reads=[PB[bank]], writes=[B_ctxf])
                    S.do("vector", lambda e, bi=bi, bank=bank, s=s: e.tensor_tensor(
                        ctxfl[:, 4 * bi:4 * (bi + 1), 128 * s:128 * (s + 1)],
                        pb[bank].ap().rearrange("p (a b) -> p a b", b=128),
                        ctxfh[:, 4 * bi:4 * (bi + 1), 128 * s:128 * (s + 1)],
                        ALU.subtract), reads=[PB[bank], B_ctxf], writes=[B_ctxf])

                if debug and m == 0 and s == 0:
                    S.do("gpsimd", lambda e, qb=qb: e.dma_start(out=dbg["d_qp"][:, :], in_=qb[:, :]), dma=True, reads=[B_qb])

            if m + 1 < nmac:
                c_load(m + 1)
                c_apply(m + 1)
            for dz in range(NZ):
                bank = 3 + dz % 2
                for pi, (wop, xop) in enumerate(
                        ((woh, ctxfh), (woh, ctxfl), (wol, ctxfh))):
                    for z in range(NZ):
                        st = (pi == 0 and z == 0)
                        sp = (pi == 2 and z == NZ - 1)
                        S.do("tensor", lambda e, dz=dz, z=z, bank=bank, wop=wop,
                             xop=xop, st=st, sp=sp: e.matmul(
                            pb[bank][:, :], wop[:, z, 128 * dz:128 * (dz + 1)],
                            xop[:, z, :], start=st, stop=sp),
                            reads=[B_wo, B_ctxf], writes=[PB[bank]],
                            mm=(PB[bank], st, sp, 512))
                S.do("vector", lambda e, dz=dz, bank=bank, xb=xb: e.scalar_tensor_tensor(
                    out=xb[:, dz, :], in0=pb[bank][:, :], scalar=smalls[:, dz, 3:4],
                    in1=xb[:, dz, :], op0=ALU.add, op1=ALU.add),
                    reads=[PB[bank], B_smalls, B_xb], writes=[B_xb])
            if not SKIP_ATTN_OUT:
                S.do("gpsimd", lambda e, m=m, xb=xb: e.dma_start(
                    out=out.rearrange("(z p) t -> p z t", p=128)[:, :, TM * m:TM * (m + 1)],
                    in_=xb[:, :, :], accum_op=ALU.add), dma=True,
                    reads=[B_xb], writes=[B_out])

            if debug and m == 0:
                pass

        S.barrier()
        with nc.Block() as blockC:
            S.flush(blockC, sems)
        stC.close()

    return nc


def kernel(**inputs):
    x = np.asarray(inputs["L0_attn_query"], dtype=np.float32)
    Bn, N, Dm = x.shape

    wq = np.ascontiguousarray(np.asarray(inputs["L0_attn_Wq"], np.float32).T)
    wk = np.ascontiguousarray(np.asarray(inputs["L0_attn_Wk"], np.float32).T)
    wv = np.ascontiguousarray(np.asarray(inputs["L0_attn_Wv"], np.float32).T)
    wo = np.ascontiguousarray(np.asarray(inputs["L0_attn_Wo"], np.float32).T)
    wr = np.ascontiguousarray(np.asarray(inputs["L0_ffn_W_r"], np.float32).T)
    wkf = np.ascontiguousarray(np.asarray(inputs["L0_ffn_W_k"], np.float32).T)
    wvf = np.ascontiguousarray(np.asarray(inputs["L0_ffn_W_v"], np.float32).T)
    proj = np.asarray(inputs["L0_attn_proj"], np.float32)

    projP = np.zeros((NZ, 128, 128), np.float32)
    for z in range(NZ):
        for a in range(2):
            projP[z, 64 * a:64 * (a + 1), 64 * a:64 * (a + 1)] = proj[2 * z + a].T

    bias_rows = np.zeros((8, D), np.float32)
    bias_rows[0] = np.asarray(inputs["L0_attn_bq"], np.float32)
    bias_rows[1] = np.asarray(inputs["L0_attn_bk"], np.float32)
    bias_rows[2] = np.asarray(inputs["L0_attn_bv"], np.float32)
    bias_rows[3] = np.asarray(inputs["L0_attn_bo"], np.float32)
    bias_rows[4] = np.asarray(inputs["L0_ffn_mu_r"], np.float32)
    bias_rows[5] = np.asarray(inputs["L0_ffn_mu_k"], np.float32)
    biasT = np.ascontiguousarray(bias_rows.T)

    ntok = N // 2
    in_maps = []
    for c in range(8):
        b, h = c // 2, c % 2
        xTc = np.zeros((D, ntok + 1), np.float32)
        xTc[:, 1:] = x[b, ntok * h:ntok * (h + 1)].T
        if h == 1:
            xTc[:, 0] = x[b, ntok - 1]
        in_maps.append({
            "xT": xTc, "wq": wq, "wk": wk, "wv": wv, "wo": wo,
            "wr": wr, "wkf": wkf, "wvf": wvf,
            "projP": projP, "biasT": biasT, "bias_rows": bias_rows,
        })

    nc = build_nc(nmac=ntok // TM, debug=DEBUG)
    res = run_bass_kernel_spmd(nc, in_maps, core_ids=list(range(8)))

    y = np.empty((Bn, N, Dm), np.float32)
    for c in range(8):
        b, h = c // 2, c % 2
        y[b, ntok * h:ntok * (h + 1)] = res.results[c]["out"].T
    if DEBUG:
        kernel.debug_results = res.results
    return y


# version 15
# speedup vs baseline: 1.0253x; 1.0253x over previous
import numpy as np
from collections import defaultdict
from contextlib import ExitStack

import concourse.bass as bass
import concourse.mybir as mybir
from concourse.bass_utils import run_bass_kernel_spmd

F32 = mybir.dt.float32
ALU = mybir.AluOpType
AX = mybir.AxisListType.X
AF = mybir.ActivationFunctionType

D = 1024
NZ = 8
TM = 512
NSUB = 4

DN = float(np.float32(64.0 ** -0.25))
HDN2 = 0.0625
RATIO = 0.125
EPS_FEAT = float(np.float32(1e-4))
EPS_DEN = float(np.float32(0.01))
LN_EPS = 1e-5
C1024 = 1.0 / 1024.0

DEBUG = False
import os as _os
SKIP_SUBT = bool(int(_os.environ.get('SKIP_SUBT', '0')))
SKIP_XN = bool(int(_os.environ.get('SKIP_XN', '0')))
ONE_SEM = bool(int(_os.environ.get('ONE_SEM', '0')))
NOFILT = bool(int(_os.environ.get('NOFILT', '0')))
RELAX = tuple(x for x in _os.environ.get('RELAX', 'tensor').split(',') if x)
SKIP_FFN_OUT = bool(int(_os.environ.get('SKIP_FFN_OUT', '0')))
SKIP_ATTN_OUT = bool(int(_os.environ.get('SKIP_ATTN_OUT', '0')))
SKIP_C = bool(int(_os.environ.get('SKIP_C', '0')))
FILT_KEEP = set(_os.environ.get('FILT_KEEP', '').split(',')) - {''}
VMODE = _os.environ.get('VMODE', 'all')

ENGINES = ("sync", "tensor", "vector", "scalar", "gpsimd")
OP_SEM = {"sync": "sy", "tensor": "t", "vector": "v", "scalar": "sc", "gpsimd": "ge"}
DMA_SEM = {"sync": "sd", "gpsimd": "gd"}
NDMASEM = 8
DMA_SEMS = {e: [p + str(i) for i in range(NDMASEM)] for e, p in DMA_SEM.items()}
ALL_DMA_SEMS = [s for l in DMA_SEMS.values() for s in l]


class Buf:
    __slots__ = ("w", "rs", "psum", "cur_n", "glast_tok", "glast_tpos",
                 "glast_multi")

    def __init__(self, psum=False):
        self.w = None
        self.rs = []
        self.psum = psum
        self.cur_n = 0
        self.glast_tok = None
        self.glast_tpos = -10**9
        self.glast_multi = False


SERIALIZE = False


class Sched:
    DRAIN_WIN = 256

    def __init__(self):
        self.counts = defaultdict(int)
        self.ops = {e: [] for e in ENGINES}
        self.hi = defaultdict(int)
        self.floor = None
        self.tpos = 0
        self.psum_toks = set()
        self.dma_idx = defaultdict(int)
        self.dma_uses = defaultdict(int)

    def do(self, eng, fn, reads=(), writes=(), dma=False, deps=(), sem=None,
           inc=None, noinc=False, mm=None, tag=""):
        ser_dep = None
        if sem is None:
            if dma:
                k = self.dma_idx[eng] % NDMASEM
                self.dma_idx[eng] += 1
                sem = DMA_SEMS[eng][k]
                if self.dma_uses[sem] > 0:
                    ser_dep = (sem, 16 * self.dma_uses[sem])
                self.dma_uses[sem] += 1
            else:
                sem = OP_SEM[eng]
        if ONE_SEM and sem != "cc":
            sem = "u"
        if inc is None:
            inc = 16 if dma else 1
        if SERIALIZE:
            deps = list(deps) + [(s2, v2) for (s2, v2) in self.counts.items()
                                 if v2 > 0 and s2 not in ("bar",)]
        dl = []
        hard = []
        if ser_dep is not None:
            hard.append(ser_dep)
        if self.floor is not None and eng in getattr(self, "floor_exempt", ()):
            pass

        if mm is not None:
            bank, g_start, g_stop, nout = mm
            if g_start and bank.glast_multi and bank.glast_tok is not None \
                    and (self.tpos - bank.glast_tpos) < self.DRAIN_WIN:
                hard.append(bank.glast_tok)
            self.tpos += max(nout, 60)
        if self.floor is not None and eng not in getattr(self, "floor_exempt", ()):
            dl.append(self.floor)
        dl.extend(d for d in deps if d is not None)
        for b in reads:
            if b.w is not None:
                dl.append(b.w)
            if b.psum:
                for (t, te) in b.rs:
                    if te != eng:
                        dl.append(t)
        for b in writes:
            if b.w is not None:
                dl.append(b.w)
            dl.extend(t for (t, _te) in b.rs)
        waits = []
        for (ws, wv) in dl:
            if ws == OP_SEM[eng] and eng in RELAX:
                continue
            waits.append((ws, wv))
        waits.extend(hard)
        if noinc:
            tok = None
            self.ops[eng].append((fn, waits, None, 0, tag))
        else:
            self.counts[sem] += inc
            tok = (sem, self.counts[sem])
            self.ops[eng].append((fn, waits, sem, inc, tag))
        if tok is not None:
            for b in reads:
                b.rs.append((tok, eng))
            for b in writes:
                b.w = tok
                b.rs = []
        if tok is not None and any(getattr(b, "psum", False) for b in reads):
            self.psum_toks.add(tok)
        if mm is not None and tok is not None:
            bank, g_start, g_stop, nout = mm
            bank.cur_n = 1 if g_start else bank.cur_n + 1
            if g_stop:
                bank.glast_tok = tok
                bank.glast_tpos = self.tpos
                bank.glast_multi = bank.cur_n > 1
        return tok

    def barrier(self, exempt=()):
        n = 0
        for eng in ENGINES:
            if eng in exempt:
                continue
            deps = []
            if eng in DMA_SEMS:
                for sm in DMA_SEMS[eng]:
                    if self.counts[sm] > 0:
                        deps.append((sm, self.counts[sm]))
            self.do(eng, lambda e: e.nop(), deps=deps, sem="bar", inc=1)
            n += 1
        self.floor = ("bar", self.counts["bar"])
        self.floor_exempt = set(exempt)

    def flush(self, block, sems):
        hi = self.hi
        for eng in ENGINES:
            ops = self.ops[eng]
            if not ops:
                continue

            trc = getattr(self, "trace", None)

            def body(e, eng=eng, ops=ops):
                for (fn, waits, sem, inc, tag) in ops:
                    emitted = []
                    for (ws, wv) in waits:
                        key = (eng, ws)
                        if wv > hi[key]:
                            e.wait_ge(sems[ws], wv)
                            hi[key] = wv
                            emitted.append((ws, wv))
                    ins = fn(e)
                    if sem is not None:
                        ins.then_inc(sems[sem], inc)
                    if trc is not None:
                        trc.write("%s %s%s -> %s  waits=%s raw=%s\n" % (
                            eng, tag, "", (sem, self.counts.get(sem)) if sem else "noinc",
                            emitted, waits))

            getattr(block, eng)(body)
            self.ops[eng] = []


def _r3(ap, t):
    return ap.rearrange("p (z t) -> p z t", t=t)


def build_nc(nmac=8, debug=False):
    ntok = TM * nmac
    nsubs = NSUB * nmac
    nc = bass.Bass("TRN2", target_bir_lowering=False)

    xT = nc.dram_tensor("xT", [D, ntok + 1], F32, kind="ExternalInput").ap()
    wT = {}
    for name in ("wq", "wk", "wv", "wo", "wr", "wkf", "wvf"):
        wT[name] = nc.dram_tensor(name, [D, D], F32, kind="ExternalInput").ap()
    projP = nc.dram_tensor("projP", [NZ, 128, 128], F32, kind="ExternalInput").ap()
    biasT = nc.dram_tensor("biasT", [D, 8], F32, kind="ExternalInput").ap()
    bias_rows = nc.dram_tensor("bias_rows", [8, D], F32, kind="ExternalInput").ap()
    out = nc.dram_tensor("out", [D, ntok], F32, kind="ExternalOutput").ap()

    PAY = NZ * 129 + 16 + 8
    RM0 = NZ * 129
    cc_in = nc.dram_tensor("cc_in", [129, PAY], F32).ap()
    cc_out = nc.dram_tensor("cc_out", [258, PAY], F32).ap()
    groups = [[0, 1], [2, 3], [4, 5], [6, 7]]

    dbg = {}
    if debug:
        for nm, shp in [("d_stats", [128, nsubs * 2]), ("d_pay", [128, PAY]),
                        ("d_comb", [128, PAY]), ("d_kv", [128, NZ * 130]),
                        ("d_sall", [128, 16]), ("d_vsum", [1, D]),
                        ("d_xn", [128, NZ * TM]), ("d_ksb", [128, NZ * TM]),
                        ("d_e", [128, 1024]), ("d_q", [128, NZ * TM]),
                        ("d_qp", [128, 1024]), ("d_ctx", [128, NZ * TM]),
                        ("d_vsb", [128, NZ * 129]), ("d_st0", [128, 8]),
                        ("d_ev0", [128, 8]), ("d_t20", [128, 16]),
                        ("d_bc", [128, 1024]), ("d_srow", [1, 256]),
                        ("d_xm", [128, NZ * TM]), ("d_xa", [128, NZ * TM]),
                        ("d_xl", [128, NZ]), ("d_xr0", [128, NZ]),
                        ("d_xk0", [128, NZ]), ("d_xnb", [128, NZ])]:
            dbg[nm] = nc.dram_tensor(nm, shp, F32, kind="ExternalOutput").ap()

    S = Sched()

    with ExitStack() as ctx:
        sems = {}
        for snm in (ALL_DMA_SEMS + ["sy", "t", "v", "sc", "ge", "bar", "cc", "u"]):
            sems[snm] = ctx.enter_context(nc.semaphore(snm))

        ident = ctx.enter_context(nc.sbuf_tensor("ident", [128, 128], F32))
        projs = ctx.enter_context(nc.sbuf_tensor("projs", [128, NZ, 128], F32))
        ndiag = ctx.enter_context(nc.sbuf_tensor("ndiag", [128, 128], F32))
        hones = ctx.enter_context(nc.sbuf_tensor("hones", [128, 2], F32))
        onescol = ctx.enter_context(nc.sbuf_tensor("onescol", [128, 1], F32))
        epscol = ctx.enter_context(nc.sbuf_tensor("epscol", [128, 1], F32))
        ones1 = ctx.enter_context(nc.sbuf_tensor("ones1", [1, 128], F32))
        smalls = ctx.enter_context(nc.sbuf_tensor("smalls", [128, NZ, 8], F32))
        mu1m = ctx.enter_context(nc.sbuf_tensor("mu1m", [128, NZ, 2], F32))
        stats = ctx.enter_context(nc.sbuf_tensor("stats", [128, nsubs, 2], F32))
        xnsum = ctx.enter_context(nc.sbuf_tensor("xnsum", [128, NZ], F32))
        xns2 = ctx.enter_context(nc.sbuf_tensor("xns2", [128, NZ], F32))
        payload = ctx.enter_context(nc.sbuf_tensor("payload", [128, PAY], F32))
        vsrow = ctx.enter_context(nc.sbuf_tensor("vsrow", [1, PAY], F32))
        bvrow = ctx.enter_context(nc.sbuf_tensor("bvrow", [1, D], F32))
        bvbc = ctx.enter_context(nc.sbuf_tensor("bvbc", [128, NZ, 128], F32))
        kvr = ctx.enter_context(nc.sbuf_tensor("kvr", [128, NZ, 130], F32))
        sall = ctx.enter_context(nc.sbuf_tensor("sall", [128, 16], F32))
        strow = ctx.enter_context(nc.sbuf_tensor("strow", [1, 2, 128], F32))

        pb = [ctx.enter_context(nc.psum_tensor("pb%d" % i, [128, 512], F32))
              for i in range(8)]
        PB = [Buf(psum=True) for _ in range(8)]

        (B_ident, B_projs, B_ndiag, B_hones, B_ones, B_ones1, B_smalls, B_mu1m,
         B_stats, B_xnsum, B_xns2, B_pay, B_vsrow, B_bvrow, B_bvbc, B_kvr,
         B_sall, B_strow, B_out, B_eps) = (Buf() for _ in range(20))

        strow_ps = pb[7].ap().rearrange("p (a b) -> p a b", b=128)

        def ln_bcast(g, bank_r, bank_m, win):
            for col in range(2):
                S.do("tensor", lambda e, g=g, col=col: e.transpose(
                    strow_ps[0:1, 2 + col, :], stats[:, g, col:col + 1], ident[:, :]),
                    reads=[B_stats, B_ident], writes=[PB[7]],
                    mm=(PB[7], True, True, 128))
            S.do("vector", lambda e: e.tensor_copy(
                strow[0:1, :, :], strow_ps[0:1, 2:4, :]),
                reads=[PB[7]], writes=[B_strow])
            for col, bank in ((0, bank_r), (1, bank_m)):
                S.do("tensor", lambda e, col=col, bank=bank, win=win: e.matmul(
                    pb[bank][:, 128 * win:128 * (win + 1)], ones1[:, :],
                    strow[0:1, col, :], start=True, stop=True),
                    reads=[B_ones1, B_strow], writes=[PB[bank]],
                    mm=(PB[bank], True, True, 128), tag="bcast b%d w%d" % (bank, win))

        stA = ExitStack()
        xa = [stA.enter_context(nc.sbuf_tensor("xa%d" % i, [128, NZ, TM], F32)) for i in range(2)]
        xn = stA.enter_context(nc.sbuf_tensor("xnA", [128, NZ, TM], F32))
        sq = stA.enter_context(nc.sbuf_tensor("sqA", [128, NZ, TM], F32))
        BF = mybir.dt.bfloat16
        wkh = stA.enter_context(nc.sbuf_tensor("wkh", [128, NZ, D], BF))
        wkl = stA.enter_context(nc.sbuf_tensor("wkl", [128, NZ, D], BF))
        wvh = stA.enter_context(nc.sbuf_tensor("wvh", [128, NZ, D], BF))
        wvl = stA.enter_context(nc.sbuf_tensor("wvl", [128, NZ, D], BF))
        xnh = stA.enter_context(nc.sbuf_tensor("xnh", [128, NZ, TM], BF))
        xnl = stA.enter_context(nc.sbuf_tensor("xnl", [128, NZ, TM], BF))
        xsh = stA.enter_context(nc.sbuf_tensor("xsh", [128, NZ], BF))
        xsl = stA.enter_context(nc.sbuf_tensor("xsl", [128, NZ], BF))
        k_sb = stA.enter_context(nc.sbuf_tensor("k_sb", [128, NZ, TM], F32))
        v_sb = [stA.enter_context(nc.sbuf_tensor("v_sb%d" % i, [128, NZ, 129], F32)) for i in range(2)]
        e_sb = [stA.enter_context(nc.sbuf_tensor("e_sb%d" % i, [128, 1024], F32)) for i in range(2)]
        rmx = stA.enter_context(nc.sbuf_tensor("rmx", [128, 16], F32))
        dgs = stA.enter_context(nc.sbuf_tensor("dgs", [128, 16], F32))
        stev = stA.enter_context(nc.sbuf_tensor("stev", [128, 4, 2], F32))
        stev2 = stA.enter_context(nc.sbuf_tensor("stev2", [128, 2, 4, 2], F32))
        stt2 = stA.enter_context(nc.sbuf_tensor("stt2", [128, 4, 4], F32))
        dbgbc_sb = stA.enter_context(nc.sbuf_tensor("dbgbc", [128, 1024], F32)) if debug else None

        B_xa = [Buf(), Buf()]
        B_xn, B_sq, B_wk, B_wv, B_ksb = Buf(), Buf(), Buf(), Buf(), Buf()
        B_wtmpA, B_xnh, B_xnl, B_xshl = Buf(), Buf(), Buf(), Buf()
        B_wtA = [B_wtmpA, Buf()]
        B_vsb = [Buf(), Buf()]
        B_esb = [Buf(), Buf()]
        B_rmx, B_dgs, B_stev, B_stt2 = Buf(), Buf(), Buf(), Buf()
        B_stev2 = Buf()
        B_dbgbc = Buf()

        S.do("sync", lambda e: e.dma_start(out=projs[:, :, :], in_=projP.rearrange("z p j -> p z j")),
             dma=True, writes=[B_projs])
        S.do("sync", lambda e: e.dma_start(out=smalls[:, :, :], in_=biasT.rearrange("(z p) r -> p z r", p=128)),
             dma=True, writes=[B_smalls])
        S.do("sync", lambda e: e.dma_start(out=bvrow[:, :], in_=bias_rows[2:3, :]),
             dma=True, writes=[B_bvrow])
        S.do("gpsimd", lambda e: e.memset(ident[:, :], 0.0), writes=[B_ident])
        S.do("gpsimd", lambda e: e.affine_select(
            out=ident[:, :], in_=ident[:, :], compare_op=ALU.not_equal,
            fill=1.0, base=0, pattern=[[-1, 128]], channel_multiplier=1),
            writes=[B_ident])
        S.do("gpsimd", lambda e: e.memset(ndiag[:, :], 0.0), writes=[B_ndiag])
        S.do("gpsimd", lambda e: e.memset(ndiag[0:64, 0:64], -HDN2), writes=[B_ndiag])
        S.do("gpsimd", lambda e: e.memset(ndiag[64:128, 64:128], -HDN2), writes=[B_ndiag])
        S.do("gpsimd", lambda e: e.memset(hones[:, :], 0.0), writes=[B_hones])
        S.do("gpsimd", lambda e: e.memset(hones[0:64, 0:1], HDN2), writes=[B_hones])
        S.do("gpsimd", lambda e: e.memset(hones[64:128, 1:2], HDN2), writes=[B_hones])
        S.do("gpsimd", lambda e: e.memset(onescol[:, :], 1.0), writes=[B_ones])
        S.do("gpsimd", lambda e: e.memset(epscol[:, :], LN_EPS), writes=[B_eps])
        S.do("gpsimd", lambda e: e.memset(ones1[:, :], 1.0), writes=[B_ones1])
        S.do("gpsimd", lambda e: e.memset(payload[:, 0:RM0], 0.0), writes=[B_pay])
        S.do("gpsimd", lambda e: e.memset(payload[:, RM0:RM0 + 16], -3.0e38), writes=[B_pay])
        S.do("gpsimd", lambda e: e.memset(xnsum[:, :], 0.0), writes=[B_xnsum])
        S.do("gpsimd", lambda e: e.memset(vsrow[0:1, D:PAY], 0.0), writes=[B_vsrow])
        for i in range(2):
            S.do("gpsimd", lambda e, i=i: e.memset(v_sb[i][:, :, 128:129], 1.0), writes=[B_vsb[i]])

        S.do("vector", lambda e: e.tensor_scalar(
            projs.ap().rearrange("p z j -> p (z j)"), projs.ap().rearrange("p z j -> p (z j)"),
            DN, None, ALU.mult), reads=[B_projs], writes=[B_projs])
        S.do("vector", lambda e: e.tensor_scalar(
            mu1m[:, :, 0:1], smalls[:, :, 4:5], -1.0, 1.0, ALU.mult, ALU.add),
            reads=[B_smalls], writes=[B_mu1m])
        S.do("vector", lambda e: e.tensor_scalar(
            mu1m[:, :, 1:2], smalls[:, :, 5:6], -1.0, 1.0, ALU.mult, ALU.add),
            reads=[B_smalls], writes=[B_mu1m])
        for half in range(2):
            S.do("tensor", lambda e, h=half: e.matmul(
                pb[0][:, :], ones1[:, :], bvrow[0:1, 512 * h:512 * (h + 1)],
                start=True, stop=True), reads=[B_ones1, B_bvrow], writes=[PB[0]],
                mm=(PB[0], True, True, 512))
            S.do("vector", lambda e, h=half: e.tensor_copy(
                bvbc[:, 4 * h:4 * (h + 1), :],
                pb[0].ap().rearrange("p (a b) -> p a b", b=128)),
                reads=[PB[0]], writes=[B_bvbc])

        def _load_lnexp(e):
            from concourse.hw_specs import get_activation_tables
            for idx, funcs in enumerate(get_activation_tables(e.bass.m.arch).values()):
                if AF.Ln in funcs and AF.Exp in funcs:
                    return e.add_instruction(mybir.InstLoadActFuncSet(
                        name=e.bass.get_next_instruction_name(),
                        act_func_set_id=idx, ins=[], outs=[]))
            raise RuntimeError("no ln+exp table set")
        S.do("scalar", _load_lnexp, noinc=True)

        stats_ps = pb[0].ap().rearrange("p (a b) -> p a b", b=2)
        diag_ps = pb[0].ap().rearrange("p (a b) -> p a b", b=2)
        dd_ps = [pb[1].ap().rearrange("p (a b) -> p a b", b=128),
                 pb[2].ap().rearrange("p (a b) -> p a b", b=128)]
        a_ps = [pb[3][:, 0:387].rearrange("p (a b) -> p a b", b=129),
                pb[4][:, 0:387].rearrange("p (a b) -> p a b", b=129),
                pb[7][:, 0:258].rearrange("p (a b) -> p a b", b=129)]
        pay_v = payload[:, 0:RM0].rearrange("p (z c) -> p z c", c=129)

        def emit_load(m):
            xb2, B_xb2 = xa[m % 2], B_xa[m % 2]
            c0 = 1 + TM * m
            S.do("sync", lambda e, xb=xb2, c0=c0: e.dma_start(
                out=xb[:, :, :], in_=xT.rearrange("(z p) t -> p z t", p=128)[:, :, c0:c0 + TM]),
                dma=True, writes=[B_xb2])

        def emit_sq_stats(m):
            xb2, B_xb2 = xa[m % 2], B_xa[m % 2]
            S.do("scalar", lambda e, xb=xb2: e.activation(
                sq.ap().rearrange("p z t -> p (z t)"),
                xb.ap().rearrange("p z t -> p (z t)"), AF.Square),
                reads=[B_xb2], writes=[B_sq])
            for s in range(NSUB):
                for (q, src, bsrc) in ((0, xb2, B_xb2), (1, sq, B_sq)):
                    qb = 0 if q == 0 else 6
                    for z in range(NZ):
                        S.do("tensor", lambda e, s=s, z=z, q=q, src=src, qb=qb: e.matmul(
                            pb[qb][:, s:s + 1], src[:, z, 128 * s:128 * (s + 1)],
                            onescol[:, :], start=(z == 0), stop=(z == NZ - 1)),
                            reads=[bsrc, B_ones], writes=[PB[qb]],
                            mm=(PB[qb], z == 0, z == NZ - 1, 1))
            S.do("vector", lambda e: e.tensor_copy(
                stev[:, :, 0:1], pb[0][:, 0:4].rearrange("p (a b) -> p a b", b=1)),
                reads=[PB[0]], writes=[B_stev])
            S.do("vector", lambda e: e.tensor_copy(
                stev[:, :, 1:2], pb[6][:, 0:4].rearrange("p (a b) -> p a b", b=1)),
                reads=[PB[6]], writes=[B_stev])
            S.do("vector", lambda e, m=m: e.tensor_copy(
                stev2[:, m % 2, :, :], stev[:, :, :]),
                reads=[B_stev], writes=[B_stev2])

        emit_load(0)
        emit_sq_stats(0)
        wtmpA2 = [k_sb.ap().rearrange("p z t -> p (z t)")[:, D * i:D * (i + 1)]
                  for i in range(2)]
        wi_a = [0]
        for nm, hi, lo, bb in (("wk", wkh, wkl, B_wk), ("wv", wvh, wvl, B_wv)):
            S.do("gpsimd", lambda e, nm=nm, hi=hi: e.dma_start(
                out=hi[:, :, :], in_=wT[nm].rearrange("(z p) d -> p z d", p=128)),
                dma=True, writes=[bb])
            for z in range(NZ):
                i = wi_a[0] % 2
                wi_a[0] += 1
                tmp, bt = wtmpA2[i], B_wtA[i]
                S.do("sync", lambda e, nm=nm, z=z, tmp=tmp: e.dma_start(
                    out=tmp, in_=wT[nm].rearrange("(z p) d -> p z d", p=128)[:, z, :]),
                    dma=True, writes=[bt])
                S.do("gpsimd", lambda e, z=z, hi=hi, lo=lo, tmp=tmp: e.tensor_tensor(
                    lo[:, z, :], tmp, hi[:, z, :], ALU.subtract),
                    reads=[bt, bb], writes=[bb, bt])
        for m in range(nmac):
            xb, B_xb = xa[m % 2], B_xa[m % 2]
            if m + 1 < nmac:
                emit_load(m + 1)
            S.do("vector", lambda e, m=m: e.tensor_scalar(
                stt2[:, :, 0:1], stev2[:, m % 2, :, 0:1], C1024, None, ALU.mult),
                reads=[B_stev2], writes=[B_stt2])
            S.do("vector", lambda e, m=m: e.tensor_scalar(
                stt2[:, :, 1:2], stev2[:, m % 2, :, 1:2], C1024, None, ALU.mult),
                reads=[B_stev2], writes=[B_stt2])
            S.do("vector", lambda e: e.tensor_tensor(
                stt2[:, :, 2:3], stt2[:, :, 0:1], stt2[:, :, 0:1], ALU.mult),
                reads=[B_stt2], writes=[B_stt2])
            S.do("vector", lambda e: e.tensor_tensor(
                stt2[:, :, 3:4], stt2[:, :, 1:2], stt2[:, :, 2:3], ALU.subtract),
                reads=[B_stt2], writes=[B_stt2])
            S.do("scalar", lambda e: e.activation(
                stt2[:, :, 1:2], stt2[:, :, 3:4], AF.Ln, bias=epscol[:, :]),
                reads=[B_stt2, B_eps], writes=[B_stt2])
            S.do("scalar", lambda e, m=m: e.activation(
                stats[:, NSUB * m:NSUB * (m + 1), 0:1], stt2[:, :, 1:2], AF.Exp, scale=-0.5),
                reads=[B_stt2], writes=[B_stats])
            S.do("vector", lambda e, m=m: e.tensor_tensor(
                stats[:, NSUB * m:NSUB * (m + 1), 1:2], stt2[:, :, 0:1],
                stats[:, NSUB * m:NSUB * (m + 1), 0:1], ALU.mult),
                reads=[B_stt2, B_stats], writes=[B_stats])

            if not SKIP_XN:
              for s in range(NSUB):
                ln_bcast(NSUB * m + s, 1, 2, s)
              for z in range(NZ):
                S.do("vector", lambda e, z=z, xb=xb: e.tensor_tensor(
                    xn[:, z, :], xb[:, z, :], pb[1][:, :], ALU.mult),
                    reads=[B_xb, PB[1]], writes=[B_xn], tag="applyA m%d z%d" % (m, z))
              if debug and m == 0:
                S.do("gpsimd", lambda e: e.dma_start(out=dbg["d_xm"][:, :], in_=xn.ap().rearrange("p z t -> p (z t)")), dma=True, reads=[B_xn])
                S.do("gpsimd", lambda e, xb=xb: e.dma_start(out=dbg["d_xa"][:, :], in_=xb.ap().rearrange("p z t -> p (z t)")), dma=True, reads=[B_xb])
              for z in range(NZ):
                S.do("vector", lambda e, z=z: e.tensor_tensor(
                    xn[:, z, :], xn[:, z, :], pb[2][:, :], ALU.subtract),
                    reads=[PB[2], B_xn], writes=[B_xn])
              S.do("vector", lambda e: e.tensor_copy(
                  xnh.ap().rearrange("p z t -> p (z t)"),
                  xn.ap().rearrange("p z t -> p (z t)")),
                  reads=[B_xn], writes=[B_xnh], tag="xnh m%d" % m)
              S.do("vector", lambda e: e.tensor_tensor(
                  xnl.ap().rearrange("p z t -> p (z t)"),
                  xn.ap().rearrange("p z t -> p (z t)"),
                  xnh.ap().rearrange("p z t -> p (z t)"), ALU.subtract),
                  reads=[B_xn, B_xnh], writes=[B_xnl], tag="xnl m%d" % m)

            if SKIP_SUBT:
                continue
            if debug and m == 0:
                S.do("vector", lambda e: e.tensor_copy(
                    dbgbc_sb[:, 0:512], pb[1][:, :]), reads=[PB[1]], writes=[B_dbgbc])
                S.do("vector", lambda e: e.tensor_copy(
                    dbgbc_sb[:, 512:1024], pb[2][:, :]), reads=[PB[2]], writes=[B_dbgbc])
                S.do("gpsimd", lambda e: e.dma_start(out=dbg["d_bc"][:, :], in_=dbgbc_sb[:, :]), dma=True, reads=[B_dbgbc])
                S.do("gpsimd", lambda e: e.dma_start(out=dbg["d_srow"][:, :], in_=strow.ap().rearrange("o c t -> o (c t)")), dma=True, reads=[B_strow])
            if m + 1 < nmac:
                emit_sq_stats(m + 1)
            S.do("vector", lambda e: e.reduce_sum(
                xns2.ap().rearrange("p (z o) -> p z o", o=1),
                xn.ap(), axis=AX), reads=[B_xn], writes=[B_xns2])
            S.do("vector", lambda e: e.tensor_tensor(
                xnsum[:, :], xnsum[:, :], xns2[:, :], ALU.add),
                reads=[B_xns2, B_xnsum], writes=[B_xnsum])

            for dz in range(NZ):
                bank = 3 + dz % 2
                for pi, (wop, xop, bwop, bxop) in enumerate(
                        ((wkh, xnh, B_wk, B_xnh), (wkh, xnl, B_wk, B_xnl),
                         (wkl, xnh, B_wk, B_xnh))):
                    for z in range(NZ):
                        st = (pi == 0 and z == 0)
                        sp = (pi == 2 and z == NZ - 1)
                        S.do("tensor", lambda e, dz=dz, z=z, bank=bank, wop=wop,
                             xop=xop, st=st, sp=sp: e.matmul(
                            pb[bank][:, :], wop[:, z, 128 * dz:128 * (dz + 1)],
                            xop[:, z, :], start=st, stop=sp),
                            reads=[bwop, bxop], writes=[PB[bank]],
                            mm=(PB[bank], st, sp, 512))
                S.do("vector", lambda e, dz=dz, bank=bank: e.tensor_scalar(
                    k_sb[:, dz, :], pb[bank][:, :], smalls[:, dz, 1:2], None, ALU.add),
                    reads=[PB[bank], B_smalls],
                    writes=[B_ksb] + (B_wtA if m == 0 else []))
            S.do("scalar", lambda e, xb=xb: e.activation(
                xb.ap().rearrange("p z t -> p (z t)"),
                k_sb.ap().rearrange("p z t -> p (z t)"), AF.Square),
                reads=[B_ksb], writes=[B_xb])

            for s in range(NSUB):
                vb, B_vb = v_sb[s % 2], B_vsb[s % 2]
                eb, B_eb = e_sb[s % 2], B_esb[s % 2]
                for half in range(2):
                    bank = 5 + half
                    for pi, (xop, wop, bxop, bwop) in enumerate(
                            ((xnh, wvh, B_xnh, B_wv), (xnh, wvl, B_xnh, B_wv),
                             (xnl, wvh, B_xnl, B_wv))):
                        for z in range(NZ):
                            st = (pi == 0 and z == 0)
                            sp = (pi == 2 and z == NZ - 1)
                            S.do("tensor", lambda e, s=s, z=z, half=half, bank=bank,
                                 xop=xop, wop=wop, st=st, sp=sp: e.matmul(
                                pb[bank][:, :], xop[:, z, 128 * s:128 * (s + 1)],
                                wop[:, z, 512 * half:512 * (half + 1)],
                                start=st, stop=sp),
                                reads=[bxop, bwop], writes=[PB[bank]],
                                mm=(PB[bank], st, sp, 512))
                for half in range(2):
                    bank = 5 + half
                    for zz in range(4):
                        z = 4 * half + zz
                        S.do("vector", lambda e, z=z, zz=zz, bank=bank, vb=vb: e.tensor_tensor(
                            vb[:, z, 0:128], pb[bank][:, 128 * zz:128 * (zz + 1)],
                            bvbc[:, z, :], ALU.add),
                            reads=[PB[bank], B_bvbc], writes=[B_vb])

                for z in (0, 4, 1, 5, 2, 6, 3, 7):
                    bank = 1 + z // 4
                    win = z % 4
                    S.do("tensor", lambda e, z=z, s=s, bank=bank, win=win: e.matmul(
                        dd_ps[bank - 1][:, win, :], k_sb[:, z, 128 * s:128 * (s + 1)],
                        projs[:, z, :], start=True, stop=False),
                        reads=[B_ksb, B_projs], writes=[PB[bank]],
                        mm=(PB[bank], True, False, 128))
                    S.do("tensor", lambda e, z=z, s=s, bank=bank, win=win, xb=xb: e.matmul(
                        dd_ps[bank - 1][:, win, :], xb[:, z, 128 * s:128 * (s + 1)],
                        ndiag[:, :], start=False, stop=True),
                        reads=[B_xb, B_ndiag], writes=[PB[bank]],
                        mm=(PB[bank], False, True, 128))
                for z in range(NZ):
                    S.do("tensor", lambda e, z=z, s=s, xb=xb: e.matmul(
                        diag_ps[:, 8 + z, :], xb[:, z, 128 * s:128 * (s + 1)],
                        hones[:, :], start=True, stop=True),
                        reads=[B_xb, B_hones], writes=[PB[0]],
                        mm=(PB[0], True, True, 2))

                S.do("vector", lambda e: e.reduce_max(
                    rmx[:, 0:8].rearrange("p (a b) -> p a b", b=1),
                    pb[1].ap().rearrange("p (a b) -> p a b", b=64), axis=AX),
                    reads=[PB[1]], writes=[B_rmx])
                S.do("vector", lambda e: e.reduce_max(
                    rmx[:, 8:16].rearrange("p (a b) -> p a b", b=1),
                    pb[2].ap().rearrange("p (a b) -> p a b", b=64), axis=AX),
                    reads=[PB[2]], writes=[B_rmx])
                S.do("vector", lambda e: e.tensor_copy(
                    dgs.ap().rearrange("p (a b) -> p a b", b=2),
                    diag_ps[:, 8:16, :]), reads=[PB[0]], writes=[B_dgs])
                S.do("vector", lambda e: e.tensor_tensor(
                    rmx[:, :], rmx[:, :], dgs[:, :], ALU.add),
                    reads=[B_dgs, B_rmx], writes=[B_rmx])
                S.do("vector", lambda e: e.tensor_tensor(
                    payload[:, RM0:RM0 + 16], payload[:, RM0:RM0 + 16],
                    rmx[:, :], ALU.max), reads=[B_rmx, B_pay], writes=[B_pay])

                for bank in (1, 2):
                    S.do("scalar", lambda e, bank=bank, eb=eb: e.activation(
                        eb[:, 512 * (bank - 1):512 * bank], pb[bank][:, :], AF.Exp),
                        reads=[PB[bank]], writes=[B_eb])

                for z in range(NZ):
                    bidx = z // 3 if z < 6 else 2
                    widx = z % 3 if z < 6 else z - 6
                    bank = (3, 4, 7)[bidx]
                    S.do("tensor", lambda e, z=z, eb=eb, vb=vb, bidx=bidx, widx=widx: e.matmul(
                        a_ps[bidx][:, widx, :], eb[:, 128 * z:128 * (z + 1)],
                        vb[:, z, :], start=True, stop=True),
                        reads=[B_eb, B_vb], writes=[PB[bank]],
                        mm=(PB[bank], True, True, 129))
                for bidx, (bank, z0, zn) in enumerate(((3, 0, 3), (4, 3, 6), (7, 6, 8))):
                    S.do("vector", lambda e, bidx=bidx, z0=z0, zn=zn: e.tensor_tensor(
                        pay_v[:, z0:zn, :], pay_v[:, z0:zn, :],
                        a_ps[bidx][:, 0:zn - z0, :], ALU.add),
                        reads=[PB[bank], B_pay], writes=[B_pay])

            if debug and m == 0:
                S.do("gpsimd", lambda e: e.dma_start(out=dbg["d_st0"][:, :], in_=stats[:, 0:4, :].rearrange("p g c -> p (g c)")), dma=True, reads=[B_stats])
                S.do("gpsimd", lambda e: e.dma_start(out=dbg["d_ev0"][:, :], in_=stev.ap().rearrange("p a c -> p (a c)")), dma=True, reads=[B_stev])
                S.do("gpsimd", lambda e: e.dma_start(out=dbg["d_t20"][:, :], in_=stt2.ap().rearrange("p a c -> p (a c)")), dma=True, reads=[B_stt2])
                S.do("gpsimd", lambda e: e.dma_start(out=dbg["d_xn"][:, :], in_=xn.ap().rearrange("p z t -> p (z t)")), dma=True, reads=[B_xn])
                S.do("gpsimd", lambda e: e.dma_start(out=dbg["d_ksb"][:, :], in_=k_sb.ap().rearrange("p z t -> p (z t)")), dma=True, reads=[B_ksb])
                S.do("gpsimd", lambda e: e.dma_start(out=dbg["d_e"][:, :], in_=e_sb[(NSUB - 1) % 2][:, :]), dma=True, reads=[B_esb[(NSUB - 1) % 2]])
                S.do("gpsimd", lambda e: e.dma_start(out=dbg["d_vsb"][:, :], in_=v_sb[(NSUB - 1) % 2].ap().rearrange("p z c -> p (z c)")), dma=True, reads=[B_vsb[(NSUB - 1) % 2]])

        S.do("vector", lambda e: e.tensor_copy(
            payload[:, RM0 + 16:PAY], xnsum[:, :]), reads=[B_xnsum], writes=[B_pay])
        S.do("vector", lambda e: e.tensor_copy(
            xsh[:, :], xnsum[:, :]), reads=[B_xnsum], writes=[B_xshl])
        S.do("vector", lambda e: e.tensor_tensor(
            xsl[:, :], xnsum[:, :], xsh[:, :], ALU.subtract),
            reads=[B_xnsum, B_xshl], writes=[B_xshl])
        for half in range(2):
            for pi, (xop, wop) in enumerate(((xsh, wvh), (xsh, wvl), (xsl, wvh))):
                for z in range(NZ):
                    st = (pi == 0 and z == 0)
                    sp = (pi == 2 and z == NZ - 1)
                    S.do("tensor", lambda e, half=half, z=z, xop=xop, wop=wop,
                         st=st, sp=sp: e.matmul(
                        pb[5][0:1, :], xop[:, z:z + 1], wop[:, z, 512 * half:512 * (half + 1)],
                        start=st, stop=sp),
                        reads=[B_xshl, B_wv], writes=[PB[5]],
                        mm=(PB[5], st, sp, 512))
            S.do("vector", lambda e, half=half: e.scalar_tensor_tensor(
                out=vsrow[0:1, 512 * half:512 * (half + 1)],
                in0=bvrow[0:1, 512 * half:512 * (half + 1)], scalar=float(ntok),
                in1=pb[5][0:1, :], op0=ALU.mult, op1=ALU.add),
                reads=[PB[5], B_bvrow], writes=[B_vsrow])

        S.do("gpsimd", lambda e: e.dma_start(out=cc_in[0:128, :], in_=payload[:, :]),
             dma=True, reads=[B_pay])
        S.do("gpsimd", lambda e: e.dma_start(out=cc_in[128:129, :], in_=vsrow[0:1, :]),
             dma=True, reads=[B_vsrow])
        cc_gd_deps = [(sm, S.counts[sm]) for sm in DMA_SEMS["gpsimd"] if S.counts[sm] > 0]

        if debug:
            S.do("gpsimd", lambda e: e.dma_start(out=dbg["d_pay"][:, :], in_=payload[:, :]), dma=True, reads=[B_pay])
            S.do("gpsimd", lambda e: e.dma_start(out=dbg["d_stats"][:, :], in_=stats.ap().rearrange("p g c -> p (g c)")), dma=True, reads=[B_stats])

        S.barrier(exempt=("gpsimd",))
        with nc.Block() as blockA:
            S.flush(blockA, sems)
        stA.close()

        stB = ExitStack()
        xbB = [stB.enter_context(nc.sbuf_tensor("xb%d" % i, [128, NZ, TM], F32)) for i in range(2)]
        wr_sb = stB.enter_context(nc.sbuf_tensor("wr_sb", [128, NZ, D], F32))
        wkf_sb = stB.enter_context(nc.sbuf_tensor("wkf_sb", [128, NZ, D], F32))
        wvf_sb = stB.enter_context(nc.sbuf_tensor("wvf_sb", [128, NZ, D], F32))
        xr_sb = stB.enter_context(nc.sbuf_tensor("xr_sb", [128, NZ, TM], F32))
        r_sb = stB.enter_context(nc.sbuf_tensor("r_sb", [128, NZ, TM], F32))
        kfb = stB.enter_context(nc.sbuf_tensor("kfb", [128, NZ, TM], F32))
        xpcol = stB.enter_context(nc.sbuf_tensor("xpcol", [128, NZ, 1], F32))
        xlast = stB.enter_context(nc.sbuf_tensor("xlast", [128, NZ, 1], F32))
        mls = stB.enter_context(nc.sbuf_tensor("mls", [1, 4], F32))
        sqc = stB.enter_context(nc.sbuf_tensor("sqc", [128, NZ, 1], F32))

        B_xbB = [Buf(), Buf()]
        (B_wr, B_wkf, B_wvf, B_xr, B_r, B_kfb, B_xpc, B_xlast, B_mls,
         B_sqc) = (Buf() for _ in range(10))

        S.do("gpsimd", lambda e: e.collective_compute(
            "AllGather", ALU.bypass, replica_groups=groups,
            ins=[cc_in.opt()], outs=[cc_out.opt()]),
            deps=cc_gd_deps, sem="cc", inc=1)
        cc_tok = ("cc", S.counts["cc"])

        S.do("sync", lambda e: e.dma_start(
            out=xpcol[:, :, :], in_=xT.rearrange("(z p) t -> p z t", p=128)[:, :, 0:1],
            allow_slow_non_contiguous=True), dma=True, writes=[B_xpc])
        S.do("sync", lambda e: e.dma_start(
            out=xbB[0][:, :, :], in_=xT.rearrange("(z p) t -> p z t", p=128)[:, :, 1:1 + TM]),
            dma=True, writes=[B_xbB[0]])
        for nm, sb, bb in (("wr", wr_sb, B_wr), ("wkf", wkf_sb, B_wkf), ("wvf", wvf_sb, B_wvf)):
            S.do("sync", lambda e, nm=nm, sb=sb: e.dma_start(
                out=sb[:, :, :], in_=wT[nm].rearrange("(z p) d -> p z d", p=128)),
                dma=True, writes=[bb])

        S.do("vector", lambda e: e.tensor_tensor(
            sqc[:, :, :], xpcol[:, :, :], xpcol[:, :, :], ALU.mult),
            reads=[B_xpc], writes=[B_sqc])
        for (q, src, bsrc) in ((0, xpcol, B_xpc), (1, sqc, B_sqc)):
            for z in range(NZ):
                S.do("tensor", lambda e, q=q, z=z, src=src: e.matmul(
                    pb[0][0:1, q:q + 1], src[:, z, :], onescol[:, :],
                    start=(z == 0), stop=(z == NZ - 1)),
                    reads=[bsrc, B_ones], writes=[PB[0]],
                    mm=(PB[0], z == 0, z == NZ - 1, 1))
        S.do("vector", lambda e: e.tensor_scalar(
            mls[0:1, 0:2], pb[0][0:1, 0:2], C1024, None, ALU.mult),
            reads=[PB[0]], writes=[B_mls])
        S.do("vector", lambda e: e.tensor_tensor(
            mls[0:1, 2:3], mls[0:1, 0:1], mls[0:1, 0:1], ALU.mult),
            reads=[B_mls], writes=[B_mls])
        S.do("vector", lambda e: e.tensor_tensor(
            mls[0:1, 3:4], mls[0:1, 1:2], mls[0:1, 2:3], ALU.subtract),
            reads=[B_mls], writes=[B_mls])
        S.do("scalar", lambda e: e.activation(
            mls[0:1, 2:3], mls[0:1, 3:4], AF.Ln, bias=epscol[0:1, :]),
            reads=[B_mls, B_eps], writes=[B_mls])
        S.do("scalar", lambda e: e.activation(
            mls[0:1, 3:4], mls[0:1, 2:3], AF.Exp, scale=-0.5),
            reads=[B_mls], writes=[B_mls])
        S.do("tensor", lambda e: e.matmul(
            pb[1][:, 0:4], ones1[:, :], mls[0:1, 0:4], start=True, stop=True),
            reads=[B_ones1, B_mls], writes=[PB[1]], mm=(PB[1], True, True, 4))
        S.do("vector", lambda e: e.tensor_tensor(
            xpcol[:, :, :], xpcol[:, :, :],
            bass.AP(pb[1], 0, [[512, 128], [0, NZ], [1, 1]]), ALU.subtract),
            reads=[PB[1], B_xpc], writes=[B_xpc])
        S.do("vector", lambda e: e.tensor_tensor(
            xpcol[:, :, :], xpcol[:, :, :],
            bass.AP(pb[1], 3, [[512, 128], [0, NZ], [1, 1]]), ALU.mult),
            reads=[PB[1], B_xpc], writes=[B_xpc])

        S.do("scalar", lambda e: e.preload_act_table(AF.Tanh), noinc=True)

        def b_load(m):
            xb, B_xb = xbB[m % 2], B_xbB[m % 2]
            c0 = 1 + TM * m
            S.do("sync", lambda e, xb=xb, c0=c0: e.dma_start(
                out=xb[:, :, :], in_=xT.rearrange("(z p) t -> p z t", p=128)[:, :, c0:c0 + TM]),
                dma=True, writes=[B_xb])

        def b_apply_xr(m):
            xb, B_xb = xbB[m % 2], B_xbB[m % 2]
            for s in range(NSUB):
                ln_bcast(NSUB * m + s, 1, 2, s)
            for z in range(NZ):
                S.do("vector", lambda e, z=z, xb=xb: e.tensor_tensor(
                    xb[:, z, :], xb[:, z, :], pb[1][:, :], ALU.mult),
                    reads=[PB[1], B_xb], writes=[B_xb])
            for z in range(NZ):
                S.do("vector", lambda e, z=z, xb=xb: e.tensor_tensor(
                    xb[:, z, :], xb[:, z, :], pb[2][:, :], ALU.subtract),
                    reads=[PB[2], B_xb], writes=[B_xb])
            S.do("vector", lambda e, xb=xb: e.tensor_tensor(
                xr_sb[:, :, 1:TM], xb[:, :, 0:TM - 1],
                bass.AP(mu1m, 0, [[16, 128], [2, NZ], [0, TM - 1]]), ALU.mult),
                reads=[B_xb, B_mu1m], writes=[B_xr], tag="mix1 m%d" % m)
            for z in range(NZ):
                S.do("vector", lambda e, z=z, xb=xb: e.scalar_tensor_tensor(
                    out=xr_sb[:, z, 1:TM], in0=xb[:, z, 1:TM], scalar=smalls[:, z, 4:5],
                    in1=xr_sb[:, z, 1:TM], op0=ALU.mult, op1=ALU.add),
                    reads=[B_xb, B_smalls, B_xr], writes=[B_xr])

        def b_xr_bound(m):
            xb, B_xb = xbB[m % 2], B_xbB[m % 2]
            prevc, B_prevc = (xpcol, B_xpc) if m == 0 else (xlast, B_xlast)
            S.do("vector", lambda e, prevc=prevc, B_prevc=B_prevc: e.tensor_tensor(
                xr_sb[:, :, 0:1], prevc[:, :, :], mu1m[:, :, 0:1], ALU.mult),
                reads=[B_prevc, B_mu1m], writes=[B_xr])
            for z in range(NZ):
                S.do("vector", lambda e, z=z, xb=xb: e.scalar_tensor_tensor(
                    out=xr_sb[:, z, 0:1], in0=xb[:, z, 0:1], scalar=smalls[:, z, 4:5],
                    in1=xr_sb[:, z, 0:1], op0=ALU.mult, op1=ALU.add),
                    reads=[B_xb, B_smalls, B_xr], writes=[B_xr])

        def b_xkmix(m):
            xb, B_xb = xbB[m % 2], B_xbB[m % 2]
            prevc, B_prevc = (xpcol, B_xpc) if m == 0 else (xlast, B_xlast)
            S.do("vector", lambda e, xb=xb: e.tensor_tensor(
                kfb[:, :, 1:TM], xb[:, :, 0:TM - 1],
                bass.AP(mu1m, 1, [[16, 128], [2, NZ], [0, TM - 1]]), ALU.mult),
                reads=[B_xb, B_mu1m], writes=[B_kfb])
            S.do("vector", lambda e, prevc=prevc, B_prevc=B_prevc: e.tensor_tensor(
                kfb[:, :, 0:1], prevc[:, :, :], mu1m[:, :, 1:2], ALU.mult),
                reads=[B_prevc, B_mu1m], writes=[B_kfb])
            S.do("vector", lambda e, xb=xb: e.tensor_copy(
                xlast[:, :, :], xb[:, :, TM - 1:TM]), reads=[B_xb], writes=[B_xlast])
            for z in range(NZ):
                S.do("vector", lambda e, z=z, xb=xb: e.scalar_tensor_tensor(
                    out=xb[:, z, :], in0=xb[:, z, :], scalar=smalls[:, z, 5:6],
                    in1=kfb[:, z, :], op0=ALU.mult, op1=ALU.add),
                    reads=[B_smalls, B_kfb, B_xb], writes=[B_xb])

        b_apply_xr(0)
        b_xr_bound(0)
        b_xkmix(0)
        for m in range(nmac):
            xb, B_xb = xbB[m % 2], B_xbB[m % 2]

            if debug and m == 1:
                S.do("gpsimd", lambda e: e.dma_start(out=dbg["d_xl"][:, :], in_=xlast.ap().rearrange("p z o -> p (z o)"), allow_slow_non_contiguous=True), dma=True, reads=[B_xlast])
                S.do("gpsimd", lambda e: e.dma_start(out=dbg["d_xr0"][:, :], in_=xr_sb[:, :, 0:1].rearrange("p z o -> p (z o)"), allow_slow_non_contiguous=True), dma=True, reads=[B_xr])
                S.do("gpsimd", lambda e, xb=xb: e.dma_start(out=dbg["d_xk0"][:, :], in_=xb[:, :, 0:1].rearrange("p z o -> p (z o)"), allow_slow_non_contiguous=True), dma=True, reads=[B_xb])
            for dz in range(NZ):
                bank = 3 + dz % 2
                for z in range(NZ):
                    S.do("tensor", lambda e, dz=dz, z=z, bank=bank: e.matmul(
                        pb[bank][:, :], wr_sb[:, z, 128 * dz:128 * (dz + 1)],
                        xr_sb[:, z, :], start=(z == 0), stop=(z == NZ - 1)),
                        reads=[B_wr, B_xr], writes=[PB[bank]],
                        mm=(PB[bank], z == 0, z == NZ - 1, 512))
                S.do("scalar", lambda e, dz=dz, bank=bank: e.activation(
                    r_sb[:, dz, :], pb[bank][:, :], AF.Tanh, scale=0.5),
                    reads=[PB[bank]], writes=[B_r])

            if m + 1 < nmac:
                b_load(m + 1)
                b_apply_xr(m + 1)

            for dz in range(NZ):
                bank = 3 + dz % 2
                for z in range(NZ):
                    S.do("tensor", lambda e, dz=dz, z=z, bank=bank, xb=xb: e.matmul(
                        pb[bank][:, :], wkf_sb[:, z, 128 * dz:128 * (dz + 1)],
                        xb[:, z, :], start=(z == 0), stop=(z == NZ - 1)),
                        reads=[B_wkf, B_xb], writes=[PB[bank]],
                        mm=(PB[bank], z == 0, z == NZ - 1, 512))
                S.do("scalar", lambda e, dz=dz, bank=bank: e.activation(
                    kfb[:, dz, :], pb[bank][:, :], AF.Relu),
                    reads=[PB[bank]], writes=[B_kfb])
                S.do("scalar", lambda e, dz=dz: e.activation(
                    kfb[:, dz, :], kfb[:, dz, :], AF.Square),
                    reads=[B_kfb], writes=[B_kfb])

            for dz in range(NZ):
                bank = 3 + dz % 2
                for z in range(NZ):
                    S.do("tensor", lambda e, dz=dz, z=z, bank=bank: e.matmul(
                        pb[bank][:, :], wvf_sb[:, z, 128 * dz:128 * (dz + 1)],
                        kfb[:, z, :], start=(z == 0), stop=(z == NZ - 1)),
                        reads=[B_wvf, B_kfb], writes=[PB[bank]],
                        mm=(PB[bank], z == 0, z == NZ - 1, 512))
                if dz == 0:
                    S.do("vector", lambda e: e.tensor_scalar(
                        r_sb.ap().rearrange("p z t -> p (z t)"),
                        r_sb.ap().rearrange("p z t -> p (z t)"),
                        0.5, 0.5, ALU.mult, ALU.add), reads=[B_r], writes=[B_r])
                S.do("vector", lambda e, dz=dz, bank=bank: e.tensor_tensor(
                    r_sb[:, dz, :], pb[bank][:, :], r_sb[:, dz, :], ALU.mult),
                    reads=[PB[bank], B_r], writes=[B_r], tag="ffnevac dz%d" % dz)
            if not SKIP_FFN_OUT:
                S.do("gpsimd", lambda e, m=m: e.dma_start(
                    out=out.rearrange("(z p) t -> p z t", p=128)[:, :, TM * m:TM * (m + 1)],
                    in_=r_sb[:, :, :]), dma=True, reads=[B_r], writes=[B_out],
                    tag="ffnout m%d" % m)
            if m + 1 < nmac:
                b_xr_bound(m + 1)
                b_xkmix(m + 1)

        S.barrier()
        with nc.Block() as blockB:
            S.flush(blockB, sems)
        stB.close()

        stC = ExitStack()
        xbC = [stC.enter_context(nc.sbuf_tensor("xc%d" % i, [128, NZ, TM], F32)) for i in range(2)]
        xtC = stC.enter_context(nc.sbuf_tensor("xtC", [128, TM], F32))
        xnCh = stC.enter_context(nc.sbuf_tensor("xnCh", [128, NZ, TM], BF))
        xnCl = stC.enter_context(nc.sbuf_tensor("xnCl", [128, NZ, TM], BF))
        wqh = stC.enter_context(nc.sbuf_tensor("wqh", [128, NZ, D], BF))
        wql = stC.enter_context(nc.sbuf_tensor("wql", [128, NZ, D], BF))
        woh = stC.enter_context(nc.sbuf_tensor("woh", [128, NZ, D], BF))
        wol = stC.enter_context(nc.sbuf_tensor("wol", [128, NZ, D], BF))
        q_sb = stC.enter_context(nc.sbuf_tensor("q_sb", [128, NZ, TM], F32))
        sqq = stC.enter_context(nc.sbuf_tensor("sqq", [128, NZ, 128], F32))
        qp = [stC.enter_context(nc.sbuf_tensor("qp%d" % i, [128, 1024], F32)) for i in range(2)]
        qf_sb = stC.enter_context(nc.sbuf_tensor("qf_sb", [128, NZ, 128], F32))
        ctx_sb = stC.enter_context(nc.sbuf_tensor("ctx_sb", [128, 1024], F32))
        ctxfh = stC.enter_context(nc.sbuf_tensor("ctxfh", [128, NZ, TM], BF))
        ctxfl = stC.enter_context(nc.sbuf_tensor("ctxfl", [128, NZ, TM], BF))
        gat1 = stC.enter_context(nc.sbuf_tensor("gat1", [128, PAY], F32))
        vs2 = stC.enter_context(nc.sbuf_tensor("vs2", [1, D], F32))
        m16 = stC.enter_context(nc.sbuf_tensor("m16", [16, 128], F32))
        mred = stC.enter_context(nc.sbuf_tensor("mred", [16, 4], F32))
        srw = stC.enter_context(nc.sbuf_tensor("srw", [1, 16], F32))
        nrm = stC.enter_context(nc.sbuf_tensor("nrm", [128, 16], F32))
        rmq = stC.enter_context(nc.sbuf_tensor("rmq", [128, 16], F32))
        dgq = stC.enter_context(nc.sbuf_tensor("dgq", [128, 16], F32))
        epre = stC.enter_context(nc.sbuf_tensor("epre", [128, 1024], F32))

        B_xbC = [Buf(), Buf()]
        (B_xnC, B_wq, B_wo, B_q, B_sqq, B_qf, B_ctx, B_ctxf, B_gat1, B_vs2,
         B_m16, B_mred, B_srw, B_nrm, B_rmq, B_dgq, B_epre) = (Buf() for _ in range(17))
        B_xtC, B_xnChl, B_wtmpC = Buf(), Buf(), Buf()
        B_wtC = [B_wtmpC, Buf()]
        B_qp = [Buf(), Buf()]

        S.do("gpsimd", lambda e: e.dma_start(out=payload[:, :], in_=cc_out[0:128, :]),
            dma=True, deps=[cc_tok], writes=[B_pay])
        S.do("gpsimd", lambda e: e.dma_start(out=gat1[:, :], in_=cc_out[129:257, :]),
            dma=True, deps=[cc_tok], writes=[B_gat1])
        S.do("gpsimd", lambda e: e.dma_start(out=vsrow[0:1, 0:D], in_=cc_out[128:129, 0:D]),
            dma=True, deps=[cc_tok], writes=[B_vsrow])
        S.do("gpsimd", lambda e: e.dma_start(out=vs2[:, :], in_=cc_out[257:258, 0:D]),
            dma=True, deps=[cc_tok], writes=[B_vs2])
        S.do("vector", lambda e: e.tensor_tensor(
            payload[:, 0:RM0], payload[:, 0:RM0], gat1[:, 0:RM0], ALU.add),
            reads=[B_gat1, B_pay], writes=[B_pay])
        S.do("vector", lambda e: e.tensor_tensor(
            payload[:, RM0:RM0 + 16], payload[:, RM0:RM0 + 16],
            gat1[:, RM0:RM0 + 16], ALU.max), reads=[B_gat1, B_pay], writes=[B_pay])
        S.do("vector", lambda e: e.tensor_tensor(
            vsrow[0:1, 0:D], vsrow[0:1, 0:D], vs2[:, :], ALU.add),
            reads=[B_vs2, B_vsrow], writes=[B_vsrow])

        S.do("tensor", lambda e: e.transpose(
            pb[0][0:16, 0:128], payload[:, RM0:RM0 + 16], ident[:, :]),
            reads=[B_pay, B_ident], writes=[PB[0]], mm=(PB[0], True, True, 128))
        S.do("vector", lambda e: e.tensor_copy(m16[:, :], pb[0][0:16, 0:128]),
            reads=[PB[0]], writes=[B_m16])
        S.do("vector", lambda e: e.reduce_max(mred[0:16, 0:1], m16[0:16, :], axis=AX),
            reads=[B_m16], writes=[B_mred])
        S.do("scalar", lambda e: e.activation(mred[0:16, 1:2], mred[0:16, 0:1], AF.Exp, scale=-1.0),
            reads=[B_mred], writes=[B_mred])
        S.do("vector", lambda e: e.tensor_scalar(
            mred[0:16, 2:3], mred[0:16, 1:2], RATIO, None, ALU.mult),
            reads=[B_mred], writes=[B_mred])
        S.do("tensor", lambda e: e.transpose(pb[0][0:1, 128:144], mred[0:16, 2:3], ident[0:16, 0:16]),
            reads=[B_mred, B_ident], writes=[PB[0]], mm=(PB[0], True, True, 16))
        S.do("vector", lambda e: e.tensor_copy(srw[:, :], pb[0][0:1, 128:144]),
            reads=[PB[0]], writes=[B_srw])
        S.do("tensor", lambda e: e.matmul(pb[0][:, 256:272], ones1[:, :], srw[0:1, :],
                                        start=True, stop=True),
            reads=[B_ones1, B_srw], writes=[PB[0]], mm=(PB[0], True, True, 16))
        S.do("vector", lambda e: e.tensor_copy(sall[:, :], pb[0][:, 256:272]),
            reads=[PB[0]], writes=[B_sall])

        S.do("gpsimd", lambda e: e.memset(kvr[:, :, :], 0.0), writes=[B_kvr])
        EPSN = float(np.float32(EPS_FEAT) * 2 * ntok)
        for z in range(NZ):
            S.do("vector", lambda e, z=z: e.tensor_scalar(
                kvr[0:64, z, 0:64], pay_v[0:64, z, 0:64], sall[0:64, 2 * z:2 * z + 1],
                None, ALU.mult), reads=[B_pay, B_sall], writes=[B_kvr])
            S.do("vector", lambda e, z=z: e.tensor_scalar(
                kvr[64:128, z, 64:128], pay_v[64:128, z, 64:128],
                sall[64:128, 2 * z + 1:2 * z + 2], None, ALU.mult),
                reads=[B_pay, B_sall], writes=[B_kvr])
            S.do("tensor", lambda e, z=z: e.matmul(
                pb[1][:, 0:128], ones1[:, :], vsrow[0:1, 128 * z:128 * (z + 1)],
                start=True, stop=True), reads=[B_ones1, B_vsrow], writes=[PB[1]],
                mm=(PB[1], True, True, 128))
            S.do("vector", lambda e, z=z: e.scalar_tensor_tensor(
                out=kvr[0:64, z, 0:64], in0=pb[1][0:64, 0:64], scalar=EPS_FEAT,
                in1=kvr[0:64, z, 0:64], op0=ALU.mult, op1=ALU.add),
                reads=[PB[1], B_kvr], writes=[B_kvr])
            S.do("vector", lambda e, z=z: e.scalar_tensor_tensor(
                out=kvr[64:128, z, 64:128], in0=pb[1][64:128, 64:128], scalar=EPS_FEAT,
                in1=kvr[64:128, z, 64:128], op0=ALU.mult, op1=ALU.add),
                reads=[PB[1], B_kvr], writes=[B_kvr])
            S.do("vector", lambda e, z=z: e.tensor_scalar(
                kvr[0:64, z, 128:129], pay_v[0:64, z, 128:129],
                sall[0:64, 2 * z:2 * z + 1], EPSN, ALU.mult, ALU.add),
                reads=[B_pay, B_sall], writes=[B_kvr])
            S.do("vector", lambda e, z=z: e.tensor_scalar(
                kvr[64:128, z, 129:130], pay_v[64:128, z, 128:129],
                sall[64:128, 2 * z + 1:2 * z + 2], EPSN, ALU.mult, ALU.add),
                reads=[B_pay, B_sall], writes=[B_kvr])

        if debug:
            S.do("gpsimd", lambda e: e.dma_start(out=dbg["d_comb"][:, :], in_=payload[:, :]), dma=True, reads=[B_pay])
            S.do("gpsimd", lambda e: e.dma_start(out=dbg["d_kv"][:, :], in_=kvr.ap().rearrange("p z c -> p (z c)")), dma=True, reads=[B_kvr])
            S.do("gpsimd", lambda e: e.dma_start(out=dbg["d_sall"][:, :], in_=sall[:, :]), dma=True, reads=[B_sall])
            S.do("gpsimd", lambda e: e.dma_start(out=dbg["d_vsum"][:, :], in_=vsrow[0:1, 0:D]), dma=True, reads=[B_vsrow])

        ddq_ps = dd_ps
        dgq_ps = pb[0].ap().rearrange("p (a b) -> p a b", b=2)
        qk_ps = [pb[7][:, 0:390].rearrange("p (a b) -> p a b", b=130),
                pb[0][:, 0:390].rearrange("p (a b) -> p a b", b=130),
                pb[3][:, 0:260].rearrange("p (a b) -> p a b", b=130)]

        def c_load(m):
            xb, B_xb = xbC[m % 2], B_xbC[m % 2]
            c0 = 1 + TM * m
            S.do("sync", lambda e, xb=xb, c0=c0: e.dma_start(
                out=xb[:, :, :], in_=xT.rearrange("(z p) t -> p z t", p=128)[:, :, c0:c0 + TM]),
                dma=True, writes=[B_xb])

        def c_apply(m):
            xb, B_xb = xbC[m % 2], B_xbC[m % 2]
            for s in range(NSUB):
                ln_bcast(NSUB * m + s, 1, 2, s)
            for z in range(NZ):
                S.do("vector", lambda e, z=z, xb=xb: e.tensor_tensor(
                    xtC[:, :], xb[:, z, :], pb[1][:, :], ALU.mult),
                    reads=[B_xb, PB[1]], writes=[B_xtC])
                S.do("vector", lambda e, z=z: e.tensor_tensor(
                    xtC[:, :], xtC[:, :], pb[2][:, :], ALU.subtract),
                    reads=[PB[2], B_xtC], writes=[B_xtC])
                S.do("vector", lambda e, z=z: e.tensor_copy(
                    xnCh[:, z, :], xtC[:, :]), reads=[B_xtC], writes=[B_xnChl])
                S.do("vector", lambda e, z=z: e.tensor_tensor(
                    xnCl[:, z, :], xtC[:, :], xnCh[:, z, :], ALU.subtract),
                    reads=[B_xtC, B_xnChl], writes=[B_xnChl])

        if not SKIP_C:
            c_load(0)
        wtmpC2 = [q_sb.ap().rearrange("p z t -> p (z t)")[:, D * i:D * (i + 1)]
                  for i in range(2)]
        wi_c = [0]
        for nm, hi, lo, bb in (("wq", wqh, wql, B_wq), ("wo", woh, wol, B_wo)):
            S.do("gpsimd", lambda e, nm=nm, hi=hi: e.dma_start(
                out=hi[:, :, :], in_=wT[nm].rearrange("(z p) d -> p z d", p=128)),
                dma=True, writes=[bb], tag="wload_%s" % nm)
            for z in range(NZ):
                i = wi_c[0] % 2
                wi_c[0] += 1
                tmp, bt = wtmpC2[i], B_wtC[i]
                S.do("sync", lambda e, nm=nm, z=z, tmp=tmp: e.dma_start(
                    out=tmp, in_=wT[nm].rearrange("(z p) d -> p z d", p=128)[:, z, :]),
                    dma=True, writes=[bt])
                S.do("gpsimd", lambda e, z=z, hi=hi, lo=lo, tmp=tmp: e.tensor_tensor(
                    lo[:, z, :], tmp, hi[:, z, :], ALU.subtract),
                    reads=[bt, bb], writes=[bb, bt])
        if not SKIP_C:
            c_apply(0)
        for m in (range(nmac) if not SKIP_C else []):
            xb, B_xb = xbC[m % 2], B_xbC[m % 2]
            for dz in range(NZ):
                bank = 3 + dz % 2
                for pi, (wop, xop) in enumerate(
                        ((wqh, xnCh), (wqh, xnCl), (wql, xnCh))):
                    for z in range(NZ):
                        st = (pi == 0 and z == 0)
                        sp = (pi == 2 and z == NZ - 1)
                        S.do("tensor", lambda e, dz=dz, z=z, bank=bank, wop=wop,
                             xop=xop, st=st, sp=sp: e.matmul(
                            pb[bank][:, :], wop[:, z, 128 * dz:128 * (dz + 1)],
                            xop[:, z, :], start=st, stop=sp),
                            reads=[B_wq, B_xnChl], writes=[PB[bank]],
                            mm=(PB[bank], st, sp, 512))
                S.do("vector", lambda e, dz=dz, bank=bank: e.tensor_scalar(
                    q_sb[:, dz, :], pb[bank][:, :], smalls[:, dz, 0:1], None, ALU.add),
                    reads=[PB[bank], B_smalls],
                    writes=[B_q] + (B_wtC if m == 0 else []))

            def emit_pre(s):
                S.do("vector", lambda e, s=s: e.tensor_tensor(
                    sqq[:, :, :], q_sb[:, :, 128 * s:128 * (s + 1)],
                    q_sb[:, :, 128 * s:128 * (s + 1)], ALU.mult),
                    reads=[B_q], writes=[B_sqq])
                for z in (0, 4, 1, 5, 2, 6, 3, 7):
                    bank = 1 + z // 4
                    win = z % 4
                    S.do("tensor", lambda e, z=z, s=s, bank=bank, win=win: e.matmul(
                        ddq_ps[bank - 1][:, win, :], q_sb[:, z, 128 * s:128 * (s + 1)],
                        projs[:, z, :], start=True, stop=True),
                        reads=[B_q, B_projs], writes=[PB[bank]],
                        mm=(PB[bank], True, True, 128))
                for z in range(NZ):
                    S.do("tensor", lambda e, z=z: e.matmul(
                        dgq_ps[:, 200 + z, :], sqq[:, z, :], hones[:, :],
                        start=True, stop=True),
                        reads=[B_sqq, B_hones], writes=[PB[0]],
                        mm=(PB[0], True, True, 2))

            emit_pre(0)
            for s in range(NSUB):
                qb, B_qb = qp[s % 2], B_qp[s % 2]
                S.do("vector", lambda e: e.reduce_max(
                    rmq[:, 0:8].rearrange("p (a b) -> p a b", b=1),
                    pb[1].ap().rearrange("p (a b) -> p a b", b=64), axis=AX),
                    reads=[PB[1]], writes=[B_rmq])
                S.do("vector", lambda e: e.reduce_max(
                    rmq[:, 8:16].rearrange("p (a b) -> p a b", b=1),
                    pb[2].ap().rearrange("p (a b) -> p a b", b=64), axis=AX),
                    reads=[PB[2]], writes=[B_rmq])
                S.do("vector", lambda e: e.tensor_copy(
                    dgq.ap().rearrange("p (a b) -> p a b", b=2),
                    dgq_ps[:, 200:208, :]), reads=[PB[0]], writes=[B_dgq])
                S.do("vector", lambda e: e.tensor_tensor(
                    rmq[:, :], rmq[:, :], dgq[:, :], ALU.add),
                    reads=[B_dgq, B_rmq], writes=[B_rmq])
                for z in range(NZ):
                    bank = 1 + z // 4
                    win = z % 4
                    for h2 in range(2):
                        S.do("vector", lambda e, z=z, win=win, h2=h2, bank=bank: e.tensor_scalar(
                            epre[:, 128 * z + 64 * h2:128 * z + 64 * (h2 + 1)],
                            ddq_ps[bank - 1][:, win, 64 * h2:64 * (h2 + 1)],
                            rmq[:, 2 * z + h2:2 * z + h2 + 1], None, ALU.subtract),
                            reads=[PB[bank], B_rmq], writes=[B_epre])
                if s + 1 < NSUB:
                    emit_pre(s + 1)
                S.do("scalar", lambda e, qb=qb: e.activation(
                    qb[:, :], epre[:, :], AF.Exp), reads=[B_epre], writes=[B_qb])
                S.do("vector", lambda e, qb=qb: e.tensor_scalar(
                    qb[:, :], qb[:, :], RATIO, EPS_FEAT, ALU.mult, ALU.add),
                    reads=[B_qb], writes=[B_qb])
                for z in range(NZ):
                    bank = 5 + z // 4
                    win = z % 4
                    S.do("tensor", lambda e, z=z, bank=bank, win=win, qb=qb: e.transpose(
                        pb[bank][:, 128 * win:128 * (win + 1)], qb[:, 128 * z:128 * (z + 1)],
                        ident[:, :]), reads=[B_qb, B_ident], writes=[PB[bank]],
                        mm=(PB[bank], True, True, 128))
                for bi, bank in enumerate((5, 6)):
                    S.do("vector", lambda e, bi=bi, bank=bank: e.tensor_copy(
                        qf_sb[:, 4 * bi:4 * (bi + 1), :],
                        pb[bank].ap().rearrange("p (a b) -> p a b", b=128)),
                        reads=[PB[bank]], writes=[B_qf])
                for z in range(NZ):
                    bidx = z // 3 if z < 6 else 2
                    widx = z % 3 if z < 6 else z - 6
                    S.do("tensor", lambda e, z=z, bidx=bidx, widx=widx: e.matmul(
                        qk_ps[bidx][:, widx, :], qf_sb[:, z, :], kvr[:, z, :],
                        start=True, stop=True),
                        reads=[B_qf, B_kvr], writes=[PB[(7, 0, 3)[bidx]]],
                        mm=(PB[(7, 0, 3)[bidx]], True, True, 130))
                for bidx, (bank, z0, zn) in enumerate(((7, 0, 3), (0, 3, 6), (3, 6, 8))):
                    S.do("vector", lambda e, bidx=bidx, z0=z0, zn=zn: e.tensor_copy(
                        nrm[:, 2 * z0:2 * zn].rearrange("p (a b) -> p a b", b=2),
                        qk_ps[bidx][:, 0:zn - z0, 128:130]),
                        reads=[PB[bank]], writes=[B_nrm])
                S.do("vector", lambda e: e.tensor_scalar(
                    nrm[:, :], nrm[:, :], EPS_DEN, None, ALU.max),
                    reads=[B_nrm], writes=[B_nrm])
                S.do("vector", lambda e: e.reciprocal(nrm[:, :], nrm[:, :]),
                    reads=[B_nrm], writes=[B_nrm])
                for z in range(NZ):
                    bidx = z // 3 if z < 6 else 2
                    widx = z % 3 if z < 6 else z - 6
                    bank = (7, 0, 3)[bidx]
                    for h2 in range(2):
                        S.do("vector", lambda e, z=z, bidx=bidx, widx=widx, h2=h2, bank=bank: e.tensor_scalar(
                            ctx_sb[:, 128 * z + 64 * h2:128 * z + 64 * (h2 + 1)],
                            qk_ps[bidx][:, widx, 64 * h2:64 * (h2 + 1)],
                            nrm[:, 2 * z + h2:2 * z + h2 + 1], None, ALU.mult),
                            reads=[PB[bank], B_nrm], writes=[B_ctx])
                for z in range(NZ):
                    bank = 5 + z // 4
                    win = z % 4
                    S.do("tensor", lambda e, z=z, bank=bank, win=win: e.transpose(
                        pb[bank][:, 128 * win:128 * (win + 1)], ctx_sb[:, 128 * z:128 * (z + 1)],
                        ident[:, :]), reads=[B_ctx, B_ident], writes=[PB[bank]],
                        mm=(PB[bank], True, True, 128))
                for bi, bank in enumerate((5, 6)):
                    S.do("vector", lambda e, bi=bi, bank=bank, s=s: e.tensor_copy(
                        ctxfh[:, 4 * bi:4 * (bi + 1), 128 * s:128 * (s + 1)],
                        pb[bank].ap().rearrange("p (a b) -> p a b", b=128)),
                        reads=[PB[bank]], writes=[B_ctxf])
                    S.do("vector", lambda e, bi=bi, bank=bank, s=s: e.tensor_tensor(
                        ctxfl[:, 4 * bi:4 * (bi + 1), 128 * s:128 * (s + 1)],
                        pb[bank].ap().rearrange("p (a b) -> p a b", b=128),
                        ctxfh[:, 4 * bi:4 * (bi + 1), 128 * s:128 * (s + 1)],
                        ALU.subtract), reads=[PB[bank], B_ctxf], writes=[B_ctxf])

                if debug and m == 0 and s == 0:
                    S.do("gpsimd", lambda e, qb=qb: e.dma_start(out=dbg["d_qp"][:, :], in_=qb[:, :]), dma=True, reads=[B_qb])

            if m + 1 < nmac:
                c_load(m + 1)
                c_apply(m + 1)
            for dz in range(NZ):
                bank = 3 + dz % 2
                for pi, (wop, xop) in enumerate(
                        ((woh, ctxfh), (woh, ctxfl), (wol, ctxfh))):
                    for z in range(NZ):
                        st = (pi == 0 and z == 0)
                        sp = (pi == 2 and z == NZ - 1)
                        S.do("tensor", lambda e, dz=dz, z=z, bank=bank, wop=wop,
                             xop=xop, st=st, sp=sp: e.matmul(
                            pb[bank][:, :], wop[:, z, 128 * dz:128 * (dz + 1)],
                            xop[:, z, :], start=st, stop=sp),
                            reads=[B_wo, B_ctxf], writes=[PB[bank]],
                            mm=(PB[bank], st, sp, 512))
                S.do("vector", lambda e, dz=dz, bank=bank, xb=xb: e.scalar_tensor_tensor(
                    out=xb[:, dz, :], in0=pb[bank][:, :], scalar=smalls[:, dz, 3:4],
                    in1=xb[:, dz, :], op0=ALU.add, op1=ALU.add),
                    reads=[PB[bank], B_smalls, B_xb], writes=[B_xb])
            if not SKIP_ATTN_OUT:
                S.do("gpsimd", lambda e, m=m, xb=xb: e.dma_start(
                    out=out.rearrange("(z p) t -> p z t", p=128)[:, :, TM * m:TM * (m + 1)],
                    in_=xb[:, :, :], accum_op=ALU.add), dma=True,
                    reads=[B_xb], writes=[B_out])

            if debug and m == 0:
                pass

        S.barrier()
        with nc.Block() as blockC:
            S.flush(blockC, sems)
        stC.close()

    return nc


def kernel(**inputs):
    x = np.asarray(inputs["L0_attn_query"], dtype=np.float32)
    Bn, N, Dm = x.shape

    wq = np.ascontiguousarray(np.asarray(inputs["L0_attn_Wq"], np.float32).T)
    wk = np.ascontiguousarray(np.asarray(inputs["L0_attn_Wk"], np.float32).T)
    wv = np.ascontiguousarray(np.asarray(inputs["L0_attn_Wv"], np.float32).T)
    wo = np.ascontiguousarray(np.asarray(inputs["L0_attn_Wo"], np.float32).T)
    wr = np.ascontiguousarray(np.asarray(inputs["L0_ffn_W_r"], np.float32).T)
    wkf = np.ascontiguousarray(np.asarray(inputs["L0_ffn_W_k"], np.float32).T)
    wvf = np.ascontiguousarray(np.asarray(inputs["L0_ffn_W_v"], np.float32).T)
    proj = np.asarray(inputs["L0_attn_proj"], np.float32)

    projP = np.zeros((NZ, 128, 128), np.float32)
    for z in range(NZ):
        for a in range(2):
            projP[z, 64 * a:64 * (a + 1), 64 * a:64 * (a + 1)] = proj[2 * z + a].T

    bias_rows = np.zeros((8, D), np.float32)
    bias_rows[0] = np.asarray(inputs["L0_attn_bq"], np.float32)
    bias_rows[1] = np.asarray(inputs["L0_attn_bk"], np.float32)
    bias_rows[2] = np.asarray(inputs["L0_attn_bv"], np.float32)
    bias_rows[3] = np.asarray(inputs["L0_attn_bo"], np.float32)
    bias_rows[4] = np.asarray(inputs["L0_ffn_mu_r"], np.float32)
    bias_rows[5] = np.asarray(inputs["L0_ffn_mu_k"], np.float32)
    biasT = np.ascontiguousarray(bias_rows.T)

    ntok = N // 2
    in_maps = []
    for c in range(8):
        b, h = c // 2, c % 2
        xTc = np.zeros((D, ntok + 1), np.float32)
        xTc[:, 1:] = x[b, ntok * h:ntok * (h + 1)].T
        if h == 1:
            xTc[:, 0] = x[b, ntok - 1]
        in_maps.append({
            "xT": xTc, "wq": wq, "wk": wk, "wv": wv, "wo": wo,
            "wr": wr, "wkf": wkf, "wvf": wvf,
            "projP": projP, "biasT": biasT, "bias_rows": bias_rows,
        })

    nc = build_nc(nmac=ntok // TM, debug=DEBUG)
    res = run_bass_kernel_spmd(nc, in_maps, core_ids=list(range(8)))

    y = np.empty((Bn, N, Dm), np.float32)
    for c in range(8):
        b, h = c // 2, c % 2
        y[b, ntok * h:ntok * (h + 1)] = res.results[c]["out"].T
    if DEBUG:
        kernel.debug_results = res.results
    return y


# version 16 (final)
# speedup vs baseline: 1.0292x; 1.0038x over previous
import numpy as np
from collections import defaultdict
from contextlib import ExitStack

import concourse.bass as bass
import concourse.mybir as mybir
from concourse.bass_utils import run_bass_kernel_spmd

F32 = mybir.dt.float32
ALU = mybir.AluOpType
AX = mybir.AxisListType.X
AF = mybir.ActivationFunctionType

D = 1024
NZ = 8
TM = 512
NSUB = 4

DN = float(np.float32(64.0 ** -0.25))
HDN2 = 0.0625
RATIO = 0.125
LNRATIO = float(np.float32(np.log(0.125)))
EPS_FEAT = float(np.float32(1e-4))
EPS_DEN = float(np.float32(0.01))
LN_EPS = 1e-5
C1024 = 1.0 / 1024.0

DEBUG = False
import os as _os
SKIP_SUBT = bool(int(_os.environ.get('SKIP_SUBT', '0')))
SKIP_XN = bool(int(_os.environ.get('SKIP_XN', '0')))
ONE_SEM = bool(int(_os.environ.get('ONE_SEM', '0')))
NOFILT = bool(int(_os.environ.get('NOFILT', '0')))
RELAX = tuple(x for x in _os.environ.get('RELAX', 'tensor').split(',') if x)
SKIP_FFN_OUT = bool(int(_os.environ.get('SKIP_FFN_OUT', '0')))
SKIP_ATTN_OUT = bool(int(_os.environ.get('SKIP_ATTN_OUT', '0')))
SKIP_C = bool(int(_os.environ.get('SKIP_C', '0')))
FILT_KEEP = set(_os.environ.get('FILT_KEEP', '').split(',')) - {''}
VMODE = _os.environ.get('VMODE', 'all')

ENGINES = ("sync", "tensor", "vector", "scalar", "gpsimd")
OP_SEM = {"sync": "sy", "tensor": "t", "vector": "v", "scalar": "sc", "gpsimd": "ge"}
DMA_SEM = {"sync": "sd", "gpsimd": "gd"}
NDMASEM = 8
DMA_SEMS = {e: [p + str(i) for i in range(NDMASEM)] for e, p in DMA_SEM.items()}
ALL_DMA_SEMS = [s for l in DMA_SEMS.values() for s in l]


class Buf:
    __slots__ = ("w", "rs", "psum", "cur_n", "glast_tok", "glast_tpos",
                 "glast_multi")

    def __init__(self, psum=False):
        self.w = None
        self.rs = []
        self.psum = psum
        self.cur_n = 0
        self.glast_tok = None
        self.glast_tpos = -10**9
        self.glast_multi = False


SERIALIZE = False


class Sched:
    DRAIN_WIN = 256

    def __init__(self):
        self.counts = defaultdict(int)
        self.ops = {e: [] for e in ENGINES}
        self.hi = defaultdict(int)
        self.floor = None
        self.tpos = 0
        self.psum_toks = set()
        self.dma_idx = defaultdict(int)
        self.dma_uses = defaultdict(int)

    def do(self, eng, fn, reads=(), writes=(), dma=False, deps=(), sem=None,
           inc=None, noinc=False, mm=None, tag=""):
        ser_dep = None
        if sem is None:
            if dma:
                k = self.dma_idx[eng] % NDMASEM
                self.dma_idx[eng] += 1
                sem = DMA_SEMS[eng][k]
                if self.dma_uses[sem] > 0:
                    ser_dep = (sem, 16 * self.dma_uses[sem])
                self.dma_uses[sem] += 1
            else:
                sem = OP_SEM[eng]
        if ONE_SEM and sem != "cc":
            sem = "u"
        if inc is None:
            inc = 16 if dma else 1
        if SERIALIZE:
            deps = list(deps) + [(s2, v2) for (s2, v2) in self.counts.items()
                                 if v2 > 0 and s2 not in ("bar",)]
        dl = []
        hard = []
        if ser_dep is not None:
            hard.append(ser_dep)
        if self.floor is not None and eng in getattr(self, "floor_exempt", ()):
            pass

        if mm is not None:
            bank, g_start, g_stop, nout = mm
            if g_start and bank.glast_multi and bank.glast_tok is not None \
                    and (self.tpos - bank.glast_tpos) < self.DRAIN_WIN:
                hard.append(bank.glast_tok)
            self.tpos += max(nout, 60)
        if self.floor is not None and eng not in getattr(self, "floor_exempt", ()):
            dl.append(self.floor)
        dl.extend(d for d in deps if d is not None)
        for b in reads:
            if b.w is not None:
                dl.append(b.w)
            if b.psum:
                for (t, te) in b.rs:
                    if te != eng:
                        dl.append(t)
        for b in writes:
            if b.w is not None:
                dl.append(b.w)
            dl.extend(t for (t, _te) in b.rs)
        waits = []
        for (ws, wv) in dl:
            if ws == OP_SEM[eng] and eng in RELAX:
                continue
            waits.append((ws, wv))
        waits.extend(hard)
        if noinc:
            tok = None
            self.ops[eng].append((fn, waits, None, 0, tag))
        else:
            self.counts[sem] += inc
            tok = (sem, self.counts[sem])
            self.ops[eng].append((fn, waits, sem, inc, tag))
        if tok is not None:
            for b in reads:
                b.rs.append((tok, eng))
            for b in writes:
                b.w = tok
                b.rs = []
        if tok is not None and any(getattr(b, "psum", False) for b in reads):
            self.psum_toks.add(tok)
        if mm is not None and tok is not None:
            bank, g_start, g_stop, nout = mm
            bank.cur_n = 1 if g_start else bank.cur_n + 1
            if g_stop:
                bank.glast_tok = tok
                bank.glast_tpos = self.tpos
                bank.glast_multi = bank.cur_n > 1
        return tok

    def barrier(self, exempt=()):
        n = 0
        for eng in ENGINES:
            if eng in exempt:
                continue
            deps = []
            if eng in DMA_SEMS:
                for sm in DMA_SEMS[eng]:
                    if self.counts[sm] > 0:
                        deps.append((sm, self.counts[sm]))
            self.do(eng, lambda e: e.nop(), deps=deps, sem="bar", inc=1)
            n += 1
        self.floor = ("bar", self.counts["bar"])
        self.floor_exempt = set(exempt)

    def flush(self, block, sems):
        hi = self.hi
        for eng in ENGINES:
            ops = self.ops[eng]
            if not ops:
                continue

            trc = getattr(self, "trace", None)

            def body(e, eng=eng, ops=ops):
                for (fn, waits, sem, inc, tag) in ops:
                    emitted = []
                    for (ws, wv) in waits:
                        key = (eng, ws)
                        if wv > hi[key]:
                            e.wait_ge(sems[ws], wv)
                            hi[key] = wv
                            emitted.append((ws, wv))
                    ins = fn(e)
                    if sem is not None:
                        ins.then_inc(sems[sem], inc)
                    if trc is not None:
                        trc.write("%s %s%s -> %s  waits=%s raw=%s\n" % (
                            eng, tag, "", (sem, self.counts.get(sem)) if sem else "noinc",
                            emitted, waits))

            getattr(block, eng)(body)
            self.ops[eng] = []


def _r3(ap, t):
    return ap.rearrange("p (z t) -> p z t", t=t)


def build_nc(nmac=8, debug=False):
    ntok = TM * nmac
    nsubs = NSUB * nmac
    nc = bass.Bass("TRN2", target_bir_lowering=False)

    xT = nc.dram_tensor("xT", [D, ntok + 1], F32, kind="ExternalInput").ap()
    wT = {}
    for name in ("wq", "wk", "wv", "wo", "wr", "wkf", "wvf"):
        wT[name] = nc.dram_tensor(name, [D, D], F32, kind="ExternalInput").ap()
    projP = nc.dram_tensor("projP", [NZ, 128, 128], F32, kind="ExternalInput").ap()
    biasT = nc.dram_tensor("biasT", [D, 8], F32, kind="ExternalInput").ap()
    bias_rows = nc.dram_tensor("bias_rows", [8, D], F32, kind="ExternalInput").ap()
    out = nc.dram_tensor("out", [D, ntok], F32, kind="ExternalOutput").ap()

    PAY = NZ * 129 + 16 + 8
    RM0 = NZ * 129
    cc_in = nc.dram_tensor("cc_in", [129, PAY], F32).ap()
    cc_out = nc.dram_tensor("cc_out", [258, PAY], F32).ap()
    groups = [[0, 1], [2, 3], [4, 5], [6, 7]]

    dbg = {}
    if debug:
        for nm, shp in [("d_stats", [128, nsubs * 2]), ("d_pay", [128, PAY]),
                        ("d_comb", [128, PAY]), ("d_kv", [128, NZ * 130]),
                        ("d_sall", [128, 16]), ("d_vsum", [1, D]),
                        ("d_xn", [128, NZ * TM]), ("d_ksb", [128, NZ * TM]),
                        ("d_e", [128, 1024]), ("d_q", [128, NZ * TM]),
                        ("d_qp", [128, 1024]), ("d_ctx", [128, NZ * TM]),
                        ("d_vsb", [128, NZ * 129]), ("d_st0", [128, 8]),
                        ("d_ev0", [128, 8]), ("d_t20", [128, 16]),
                        ("d_bc", [128, 1024]), ("d_srow", [1, 256]),
                        ("d_xm", [128, NZ * TM]), ("d_xa", [128, NZ * TM]),
                        ("d_xl", [128, NZ]), ("d_xr0", [128, NZ]),
                        ("d_xk0", [128, NZ]), ("d_xnb", [128, NZ])]:
            dbg[nm] = nc.dram_tensor(nm, shp, F32, kind="ExternalOutput").ap()

    S = Sched()

    with ExitStack() as ctx:
        sems = {}
        for snm in (ALL_DMA_SEMS + ["sy", "t", "v", "sc", "ge", "bar", "cc", "u"]):
            sems[snm] = ctx.enter_context(nc.semaphore(snm))

        ident = ctx.enter_context(nc.sbuf_tensor("ident", [128, 128], F32))
        projs = ctx.enter_context(nc.sbuf_tensor("projs", [128, NZ, 128], F32))
        ndiag = ctx.enter_context(nc.sbuf_tensor("ndiag", [128, 128], F32))
        hones = ctx.enter_context(nc.sbuf_tensor("hones", [128, 2], F32))
        onescol = ctx.enter_context(nc.sbuf_tensor("onescol", [128, 1], F32))
        epscol = ctx.enter_context(nc.sbuf_tensor("epscol", [128, 1], F32))
        ones1 = ctx.enter_context(nc.sbuf_tensor("ones1", [1, 128], F32))
        smalls = ctx.enter_context(nc.sbuf_tensor("smalls", [128, NZ, 8], F32))
        mu1m = ctx.enter_context(nc.sbuf_tensor("mu1m", [128, NZ, 2], F32))
        stats = ctx.enter_context(nc.sbuf_tensor("stats", [128, nsubs, 2], F32))
        xnsum = ctx.enter_context(nc.sbuf_tensor("xnsum", [128, NZ], F32))
        xns2 = ctx.enter_context(nc.sbuf_tensor("xns2", [128, NZ], F32))
        payload = ctx.enter_context(nc.sbuf_tensor("payload", [128, PAY], F32))
        vsrow = ctx.enter_context(nc.sbuf_tensor("vsrow", [1, PAY], F32))
        bvrow = ctx.enter_context(nc.sbuf_tensor("bvrow", [1, D], F32))
        bvbc = ctx.enter_context(nc.sbuf_tensor("bvbc", [128, NZ, 128], F32))
        kvr = ctx.enter_context(nc.sbuf_tensor("kvr", [128, NZ, 130], F32))
        sall = ctx.enter_context(nc.sbuf_tensor("sall", [128, 16], F32))
        strow = ctx.enter_context(nc.sbuf_tensor("strow", [1, 2, 128], F32))

        pb = [ctx.enter_context(nc.psum_tensor("pb%d" % i, [128, 512], F32))
              for i in range(8)]
        PB = [Buf(psum=True) for _ in range(8)]

        (B_ident, B_projs, B_ndiag, B_hones, B_ones, B_ones1, B_smalls, B_mu1m,
         B_stats, B_xnsum, B_xns2, B_pay, B_vsrow, B_bvrow, B_bvbc, B_kvr,
         B_sall, B_strow, B_out, B_eps) = (Buf() for _ in range(20))

        strow_ps = pb[7].ap().rearrange("p (a b) -> p a b", b=128)

        def ln_bcast(g, bank_r, bank_m, win):
            for col in range(2):
                S.do("tensor", lambda e, g=g, col=col: e.transpose(
                    strow_ps[0:1, 2 + col, :], stats[:, g, col:col + 1], ident[:, :]),
                    reads=[B_stats, B_ident], writes=[PB[7]],
                    mm=(PB[7], True, True, 128))
            S.do("vector", lambda e: e.tensor_copy(
                strow[0:1, :, :], strow_ps[0:1, 2:4, :]),
                reads=[PB[7]], writes=[B_strow])
            for col, bank in ((0, bank_r), (1, bank_m)):
                S.do("tensor", lambda e, col=col, bank=bank, win=win: e.matmul(
                    pb[bank][:, 128 * win:128 * (win + 1)], ones1[:, :],
                    strow[0:1, col, :], start=True, stop=True),
                    reads=[B_ones1, B_strow], writes=[PB[bank]],
                    mm=(PB[bank], True, True, 128), tag="bcast b%d w%d" % (bank, win))

        stA = ExitStack()
        xa = [stA.enter_context(nc.sbuf_tensor("xa%d" % i, [128, NZ, TM], F32)) for i in range(2)]
        xn = stA.enter_context(nc.sbuf_tensor("xnA", [128, NZ, TM], F32))
        sq = stA.enter_context(nc.sbuf_tensor("sqA", [128, NZ, TM], F32))
        BF = mybir.dt.bfloat16
        wkh = stA.enter_context(nc.sbuf_tensor("wkh", [128, NZ, D], BF))
        wkl = stA.enter_context(nc.sbuf_tensor("wkl", [128, NZ, D], BF))
        wvh = stA.enter_context(nc.sbuf_tensor("wvh", [128, NZ, D], BF))
        wvl = stA.enter_context(nc.sbuf_tensor("wvl", [128, NZ, D], BF))
        xnh = stA.enter_context(nc.sbuf_tensor("xnh", [128, NZ, TM], BF))
        xnl = stA.enter_context(nc.sbuf_tensor("xnl", [128, NZ, TM], BF))
        xsh = stA.enter_context(nc.sbuf_tensor("xsh", [128, NZ], BF))
        xsl = stA.enter_context(nc.sbuf_tensor("xsl", [128, NZ], BF))
        k_sb = stA.enter_context(nc.sbuf_tensor("k_sb", [128, NZ, TM], F32))
        v_sb = [stA.enter_context(nc.sbuf_tensor("v_sb%d" % i, [128, NZ, 129], F32)) for i in range(2)]
        e_sb = [stA.enter_context(nc.sbuf_tensor("e_sb%d" % i, [128, 1024], F32)) for i in range(2)]
        rmx = stA.enter_context(nc.sbuf_tensor("rmx", [128, 16], F32))
        dgs = stA.enter_context(nc.sbuf_tensor("dgs", [128, 16], F32))
        stev = stA.enter_context(nc.sbuf_tensor("stev", [128, 4, 2], F32))
        stev2 = stA.enter_context(nc.sbuf_tensor("stev2", [128, 2, 4, 2], F32))
        stt2 = stA.enter_context(nc.sbuf_tensor("stt2", [128, 4, 4], F32))
        dbgbc_sb = stA.enter_context(nc.sbuf_tensor("dbgbc", [128, 1024], F32)) if debug else None

        B_xa = [Buf(), Buf()]
        B_xn, B_sq, B_wk, B_wv, B_ksb = Buf(), Buf(), Buf(), Buf(), Buf()
        B_wtmpA, B_xnh, B_xnl, B_xshl = Buf(), Buf(), Buf(), Buf()
        B_wtA = [B_wtmpA, Buf()]
        B_vsb = [Buf(), Buf()]
        B_esb = [Buf(), Buf()]
        B_rmx, B_dgs, B_stev, B_stt2 = Buf(), Buf(), Buf(), Buf()
        B_stev2 = Buf()
        B_dbgbc = Buf()

        S.do("sync", lambda e: e.dma_start(out=projs[:, :, :], in_=projP.rearrange("z p j -> p z j")),
             dma=True, writes=[B_projs])
        S.do("sync", lambda e: e.dma_start(out=smalls[:, :, :], in_=biasT.rearrange("(z p) r -> p z r", p=128)),
             dma=True, writes=[B_smalls])
        S.do("sync", lambda e: e.dma_start(out=bvrow[:, :], in_=bias_rows[2:3, :]),
             dma=True, writes=[B_bvrow])
        S.do("gpsimd", lambda e: e.memset(ident[:, :], 0.0), writes=[B_ident])
        S.do("gpsimd", lambda e: e.affine_select(
            out=ident[:, :], in_=ident[:, :], compare_op=ALU.not_equal,
            fill=1.0, base=0, pattern=[[-1, 128]], channel_multiplier=1),
            writes=[B_ident])
        S.do("gpsimd", lambda e: e.memset(ndiag[:, :], 0.0), writes=[B_ndiag])
        S.do("gpsimd", lambda e: e.memset(ndiag[0:64, 0:64], -HDN2), writes=[B_ndiag])
        S.do("gpsimd", lambda e: e.memset(ndiag[64:128, 64:128], -HDN2), writes=[B_ndiag])
        S.do("gpsimd", lambda e: e.memset(hones[:, :], 0.0), writes=[B_hones])
        S.do("gpsimd", lambda e: e.memset(hones[0:64, 0:1], HDN2), writes=[B_hones])
        S.do("gpsimd", lambda e: e.memset(hones[64:128, 1:2], HDN2), writes=[B_hones])
        S.do("gpsimd", lambda e: e.memset(onescol[:, :], 1.0), writes=[B_ones])
        S.do("gpsimd", lambda e: e.memset(epscol[:, :], LN_EPS), writes=[B_eps])
        S.do("gpsimd", lambda e: e.memset(ones1[:, :], 1.0), writes=[B_ones1])
        S.do("gpsimd", lambda e: e.memset(payload[:, 0:RM0], 0.0), writes=[B_pay])
        S.do("gpsimd", lambda e: e.memset(payload[:, RM0:RM0 + 16], -3.0e38), writes=[B_pay])
        S.do("gpsimd", lambda e: e.memset(xnsum[:, :], 0.0), writes=[B_xnsum])
        S.do("gpsimd", lambda e: e.memset(vsrow[0:1, D:PAY], 0.0), writes=[B_vsrow])
        for i in range(2):
            S.do("gpsimd", lambda e, i=i: e.memset(v_sb[i][:, :, 128:129], 1.0), writes=[B_vsb[i]])

        S.do("vector", lambda e: e.tensor_scalar(
            projs.ap().rearrange("p z j -> p (z j)"), projs.ap().rearrange("p z j -> p (z j)"),
            DN, None, ALU.mult), reads=[B_projs], writes=[B_projs])
        S.do("vector", lambda e: e.tensor_scalar(
            mu1m[:, :, 0:1], smalls[:, :, 4:5], -1.0, 1.0, ALU.mult, ALU.add),
            reads=[B_smalls], writes=[B_mu1m])
        S.do("vector", lambda e: e.tensor_scalar(
            mu1m[:, :, 1:2], smalls[:, :, 5:6], -1.0, 1.0, ALU.mult, ALU.add),
            reads=[B_smalls], writes=[B_mu1m])
        for half in range(2):
            S.do("tensor", lambda e, h=half: e.matmul(
                pb[0][:, :], ones1[:, :], bvrow[0:1, 512 * h:512 * (h + 1)],
                start=True, stop=True), reads=[B_ones1, B_bvrow], writes=[PB[0]],
                mm=(PB[0], True, True, 512))
            S.do("vector", lambda e, h=half: e.tensor_copy(
                bvbc[:, 4 * h:4 * (h + 1), :],
                pb[0].ap().rearrange("p (a b) -> p a b", b=128)),
                reads=[PB[0]], writes=[B_bvbc])

        def _load_lnexp(e):
            from concourse.hw_specs import get_activation_tables
            for idx, funcs in enumerate(get_activation_tables(e.bass.m.arch).values()):
                if AF.Ln in funcs and AF.Exp in funcs:
                    return e.add_instruction(mybir.InstLoadActFuncSet(
                        name=e.bass.get_next_instruction_name(),
                        act_func_set_id=idx, ins=[], outs=[]))
            raise RuntimeError("no ln+exp table set")
        S.do("scalar", _load_lnexp, noinc=True)

        stats_ps = pb[0].ap().rearrange("p (a b) -> p a b", b=2)
        diag_ps = pb[0].ap().rearrange("p (a b) -> p a b", b=2)
        dd_ps = [pb[1].ap().rearrange("p (a b) -> p a b", b=128),
                 pb[2].ap().rearrange("p (a b) -> p a b", b=128)]
        a_ps = [pb[3][:, 0:387].rearrange("p (a b) -> p a b", b=129),
                pb[4][:, 0:387].rearrange("p (a b) -> p a b", b=129),
                pb[7][:, 0:258].rearrange("p (a b) -> p a b", b=129)]
        pay_v = payload[:, 0:RM0].rearrange("p (z c) -> p z c", c=129)

        def emit_load(m):
            xb2, B_xb2 = xa[m % 2], B_xa[m % 2]
            c0 = 1 + TM * m
            S.do("sync", lambda e, xb=xb2, c0=c0: e.dma_start(
                out=xb[:, :, :], in_=xT.rearrange("(z p) t -> p z t", p=128)[:, :, c0:c0 + TM]),
                dma=True, writes=[B_xb2])

        def emit_sq_stats(m):
            xb2, B_xb2 = xa[m % 2], B_xa[m % 2]
            S.do("scalar", lambda e, xb=xb2: e.activation(
                sq.ap().rearrange("p z t -> p (z t)"),
                xb.ap().rearrange("p z t -> p (z t)"), AF.Square),
                reads=[B_xb2], writes=[B_sq])
            for s in range(NSUB):
                for (q, src, bsrc) in ((0, xb2, B_xb2), (1, sq, B_sq)):
                    qb = 0 if q == 0 else 6
                    for z in range(NZ):
                        S.do("tensor", lambda e, s=s, z=z, q=q, src=src, qb=qb: e.matmul(
                            pb[qb][:, s:s + 1], src[:, z, 128 * s:128 * (s + 1)],
                            onescol[:, :], start=(z == 0), stop=(z == NZ - 1)),
                            reads=[bsrc, B_ones], writes=[PB[qb]],
                            mm=(PB[qb], z == 0, z == NZ - 1, 1))
            S.do("vector", lambda e: e.tensor_copy(
                stev[:, :, 0:1], pb[0][:, 0:4].rearrange("p (a b) -> p a b", b=1)),
                reads=[PB[0]], writes=[B_stev])
            S.do("vector", lambda e: e.tensor_copy(
                stev[:, :, 1:2], pb[6][:, 0:4].rearrange("p (a b) -> p a b", b=1)),
                reads=[PB[6]], writes=[B_stev])
            S.do("vector", lambda e, m=m: e.tensor_copy(
                stev2[:, m % 2, :, :], stev[:, :, :]),
                reads=[B_stev], writes=[B_stev2])

        emit_load(0)
        emit_sq_stats(0)
        wtmpA2 = [k_sb.ap().rearrange("p z t -> p (z t)")[:, D * i:D * (i + 1)]
                  for i in range(2)]
        wi_a = [0]
        for nm, hi, lo, bb in (("wk", wkh, wkl, B_wk), ("wv", wvh, wvl, B_wv)):
            S.do("gpsimd", lambda e, nm=nm, hi=hi: e.dma_start(
                out=hi[:, :, :], in_=wT[nm].rearrange("(z p) d -> p z d", p=128)),
                dma=True, writes=[bb])
            for z in range(NZ):
                i = wi_a[0] % 2
                wi_a[0] += 1
                tmp, bt = wtmpA2[i], B_wtA[i]
                S.do("sync", lambda e, nm=nm, z=z, tmp=tmp: e.dma_start(
                    out=tmp, in_=wT[nm].rearrange("(z p) d -> p z d", p=128)[:, z, :]),
                    dma=True, writes=[bt])
                S.do("gpsimd", lambda e, z=z, hi=hi, lo=lo, tmp=tmp: e.tensor_tensor(
                    lo[:, z, :], tmp, hi[:, z, :], ALU.subtract),
                    reads=[bt, bb], writes=[bb, bt])
        for m in range(nmac):
            xb, B_xb = xa[m % 2], B_xa[m % 2]
            if m + 1 < nmac:
                emit_load(m + 1)
            S.do("vector", lambda e, m=m: e.tensor_scalar(
                stt2[:, :, 0:1], stev2[:, m % 2, :, 0:1], C1024, None, ALU.mult),
                reads=[B_stev2], writes=[B_stt2])
            S.do("vector", lambda e, m=m: e.tensor_scalar(
                stt2[:, :, 1:2], stev2[:, m % 2, :, 1:2], C1024, None, ALU.mult),
                reads=[B_stev2], writes=[B_stt2])
            S.do("vector", lambda e: e.tensor_tensor(
                stt2[:, :, 2:3], stt2[:, :, 0:1], stt2[:, :, 0:1], ALU.mult),
                reads=[B_stt2], writes=[B_stt2])
            S.do("vector", lambda e: e.tensor_tensor(
                stt2[:, :, 3:4], stt2[:, :, 1:2], stt2[:, :, 2:3], ALU.subtract),
                reads=[B_stt2], writes=[B_stt2])
            S.do("scalar", lambda e: e.activation(
                stt2[:, :, 1:2], stt2[:, :, 3:4], AF.Ln, bias=epscol[:, :]),
                reads=[B_stt2, B_eps], writes=[B_stt2])
            S.do("scalar", lambda e, m=m: e.activation(
                stats[:, NSUB * m:NSUB * (m + 1), 0:1], stt2[:, :, 1:2], AF.Exp, scale=-0.5),
                reads=[B_stt2], writes=[B_stats])
            S.do("vector", lambda e, m=m: e.tensor_tensor(
                stats[:, NSUB * m:NSUB * (m + 1), 1:2], stt2[:, :, 0:1],
                stats[:, NSUB * m:NSUB * (m + 1), 0:1], ALU.mult),
                reads=[B_stt2, B_stats], writes=[B_stats])

            if not SKIP_XN:
              for s in range(NSUB):
                ln_bcast(NSUB * m + s, 1, 2, s)
              for z in range(NZ):
                S.do("vector", lambda e, z=z, xb=xb: e.tensor_tensor(
                    xn[:, z, :], xb[:, z, :], pb[1][:, :], ALU.mult),
                    reads=[B_xb, PB[1]], writes=[B_xn], tag="applyA m%d z%d" % (m, z))
              if debug and m == 0:
                S.do("gpsimd", lambda e: e.dma_start(out=dbg["d_xm"][:, :], in_=xn.ap().rearrange("p z t -> p (z t)")), dma=True, reads=[B_xn])
                S.do("gpsimd", lambda e, xb=xb: e.dma_start(out=dbg["d_xa"][:, :], in_=xb.ap().rearrange("p z t -> p (z t)")), dma=True, reads=[B_xb])
              for z in range(NZ):
                S.do("vector", lambda e, z=z: e.tensor_tensor(
                    xn[:, z, :], xn[:, z, :], pb[2][:, :], ALU.subtract),
                    reads=[PB[2], B_xn], writes=[B_xn])
              S.do("vector", lambda e: e.tensor_copy(
                  xnh.ap().rearrange("p z t -> p (z t)"),
                  xn.ap().rearrange("p z t -> p (z t)")),
                  reads=[B_xn], writes=[B_xnh], tag="xnh m%d" % m)
              S.do("vector", lambda e: e.tensor_tensor(
                  xnl.ap().rearrange("p z t -> p (z t)"),
                  xn.ap().rearrange("p z t -> p (z t)"),
                  xnh.ap().rearrange("p z t -> p (z t)"), ALU.subtract),
                  reads=[B_xn, B_xnh], writes=[B_xnl], tag="xnl m%d" % m)

            if SKIP_SUBT:
                continue
            if debug and m == 0:
                S.do("vector", lambda e: e.tensor_copy(
                    dbgbc_sb[:, 0:512], pb[1][:, :]), reads=[PB[1]], writes=[B_dbgbc])
                S.do("vector", lambda e: e.tensor_copy(
                    dbgbc_sb[:, 512:1024], pb[2][:, :]), reads=[PB[2]], writes=[B_dbgbc])
                S.do("gpsimd", lambda e: e.dma_start(out=dbg["d_bc"][:, :], in_=dbgbc_sb[:, :]), dma=True, reads=[B_dbgbc])
                S.do("gpsimd", lambda e: e.dma_start(out=dbg["d_srow"][:, :], in_=strow.ap().rearrange("o c t -> o (c t)")), dma=True, reads=[B_strow])
            if m + 1 < nmac:
                emit_sq_stats(m + 1)
            S.do("vector", lambda e: e.reduce_sum(
                xns2.ap().rearrange("p (z o) -> p z o", o=1),
                xn.ap(), axis=AX), reads=[B_xn], writes=[B_xns2])
            S.do("vector", lambda e: e.tensor_tensor(
                xnsum[:, :], xnsum[:, :], xns2[:, :], ALU.add),
                reads=[B_xns2, B_xnsum], writes=[B_xnsum])

            for dz in range(NZ):
                bank = 3 + dz % 2
                for pi, (wop, xop, bwop, bxop) in enumerate(
                        ((wkh, xnh, B_wk, B_xnh), (wkh, xnl, B_wk, B_xnl),
                         (wkl, xnh, B_wk, B_xnh))):
                    for z in range(NZ):
                        st = (pi == 0 and z == 0)
                        sp = (pi == 2 and z == NZ - 1)
                        S.do("tensor", lambda e, dz=dz, z=z, bank=bank, wop=wop,
                             xop=xop, st=st, sp=sp: e.matmul(
                            pb[bank][:, :], wop[:, z, 128 * dz:128 * (dz + 1)],
                            xop[:, z, :], start=st, stop=sp),
                            reads=[bwop, bxop], writes=[PB[bank]],
                            mm=(PB[bank], st, sp, 512))
                S.do("vector", lambda e, dz=dz, bank=bank: e.tensor_scalar(
                    k_sb[:, dz, :], pb[bank][:, :], smalls[:, dz, 1:2], None, ALU.add),
                    reads=[PB[bank], B_smalls],
                    writes=[B_ksb] + (B_wtA if m == 0 else []))
            S.do("scalar", lambda e, xb=xb: e.activation(
                xb.ap().rearrange("p z t -> p (z t)"),
                k_sb.ap().rearrange("p z t -> p (z t)"), AF.Square),
                reads=[B_ksb], writes=[B_xb])

            for s in range(NSUB):
                vb, B_vb = v_sb[s % 2], B_vsb[s % 2]
                eb, B_eb = e_sb[s % 2], B_esb[s % 2]
                for half in range(2):
                    bank = 5 + half
                    for pi, (xop, wop, bxop, bwop) in enumerate(
                            ((xnh, wvh, B_xnh, B_wv), (xnh, wvl, B_xnh, B_wv),
                             (xnl, wvh, B_xnl, B_wv))):
                        for z in range(NZ):
                            st = (pi == 0 and z == 0)
                            sp = (pi == 2 and z == NZ - 1)
                            S.do("tensor", lambda e, s=s, z=z, half=half, bank=bank,
                                 xop=xop, wop=wop, st=st, sp=sp: e.matmul(
                                pb[bank][:, :], xop[:, z, 128 * s:128 * (s + 1)],
                                wop[:, z, 512 * half:512 * (half + 1)],
                                start=st, stop=sp),
                                reads=[bxop, bwop], writes=[PB[bank]],
                                mm=(PB[bank], st, sp, 512))
                for half in range(2):
                    bank = 5 + half
                    for zz in range(4):
                        z = 4 * half + zz
                        S.do("vector", lambda e, z=z, zz=zz, bank=bank, vb=vb: e.tensor_tensor(
                            vb[:, z, 0:128], pb[bank][:, 128 * zz:128 * (zz + 1)],
                            bvbc[:, z, :], ALU.add),
                            reads=[PB[bank], B_bvbc], writes=[B_vb])

                for z in (0, 4, 1, 5, 2, 6, 3, 7):
                    bank = 1 + z // 4
                    win = z % 4
                    S.do("tensor", lambda e, z=z, s=s, bank=bank, win=win: e.matmul(
                        dd_ps[bank - 1][:, win, :], k_sb[:, z, 128 * s:128 * (s + 1)],
                        projs[:, z, :], start=True, stop=False),
                        reads=[B_ksb, B_projs], writes=[PB[bank]],
                        mm=(PB[bank], True, False, 128))
                    S.do("tensor", lambda e, z=z, s=s, bank=bank, win=win, xb=xb: e.matmul(
                        dd_ps[bank - 1][:, win, :], xb[:, z, 128 * s:128 * (s + 1)],
                        ndiag[:, :], start=False, stop=True),
                        reads=[B_xb, B_ndiag], writes=[PB[bank]],
                        mm=(PB[bank], False, True, 128))
                for z in range(NZ):
                    S.do("tensor", lambda e, z=z, s=s, xb=xb: e.matmul(
                        diag_ps[:, 8 + z, :], xb[:, z, 128 * s:128 * (s + 1)],
                        hones[:, :], start=True, stop=True),
                        reads=[B_xb, B_hones], writes=[PB[0]],
                        mm=(PB[0], True, True, 2))

                S.do("vector", lambda e: e.reduce_max(
                    rmx[:, 0:8].rearrange("p (a b) -> p a b", b=1),
                    pb[1].ap().rearrange("p (a b) -> p a b", b=64), axis=AX),
                    reads=[PB[1]], writes=[B_rmx])
                S.do("vector", lambda e: e.reduce_max(
                    rmx[:, 8:16].rearrange("p (a b) -> p a b", b=1),
                    pb[2].ap().rearrange("p (a b) -> p a b", b=64), axis=AX),
                    reads=[PB[2]], writes=[B_rmx])
                S.do("vector", lambda e: e.tensor_copy(
                    dgs.ap().rearrange("p (a b) -> p a b", b=2),
                    diag_ps[:, 8:16, :]), reads=[PB[0]], writes=[B_dgs])
                S.do("vector", lambda e: e.tensor_tensor(
                    rmx[:, :], rmx[:, :], dgs[:, :], ALU.add),
                    reads=[B_dgs, B_rmx], writes=[B_rmx])
                S.do("vector", lambda e: e.tensor_tensor(
                    payload[:, RM0:RM0 + 16], payload[:, RM0:RM0 + 16],
                    rmx[:, :], ALU.max), reads=[B_rmx, B_pay], writes=[B_pay])

                for bank in (1, 2):
                    S.do("scalar", lambda e, bank=bank, eb=eb: e.activation(
                        eb[:, 512 * (bank - 1):512 * bank], pb[bank][:, :], AF.Exp),
                        reads=[PB[bank]], writes=[B_eb])

                for z in range(NZ):
                    bidx = z // 3 if z < 6 else 2
                    widx = z % 3 if z < 6 else z - 6
                    bank = (3, 4, 7)[bidx]
                    S.do("tensor", lambda e, z=z, eb=eb, vb=vb, bidx=bidx, widx=widx: e.matmul(
                        a_ps[bidx][:, widx, :], eb[:, 128 * z:128 * (z + 1)],
                        vb[:, z, :], start=True, stop=True),
                        reads=[B_eb, B_vb], writes=[PB[bank]],
                        mm=(PB[bank], True, True, 129))
                for bidx, (bank, z0, zn) in enumerate(((3, 0, 3), (4, 3, 6), (7, 6, 8))):
                    S.do("vector", lambda e, bidx=bidx, z0=z0, zn=zn: e.tensor_tensor(
                        pay_v[:, z0:zn, :], pay_v[:, z0:zn, :],
                        a_ps[bidx][:, 0:zn - z0, :], ALU.add),
                        reads=[PB[bank], B_pay], writes=[B_pay])

            if debug and m == 0:
                S.do("gpsimd", lambda e: e.dma_start(out=dbg["d_st0"][:, :], in_=stats[:, 0:4, :].rearrange("p g c -> p (g c)")), dma=True, reads=[B_stats])
                S.do("gpsimd", lambda e: e.dma_start(out=dbg["d_ev0"][:, :], in_=stev.ap().rearrange("p a c -> p (a c)")), dma=True, reads=[B_stev])
                S.do("gpsimd", lambda e: e.dma_start(out=dbg["d_t20"][:, :], in_=stt2.ap().rearrange("p a c -> p (a c)")), dma=True, reads=[B_stt2])
                S.do("gpsimd", lambda e: e.dma_start(out=dbg["d_xn"][:, :], in_=xn.ap().rearrange("p z t -> p (z t)")), dma=True, reads=[B_xn])
                S.do("gpsimd", lambda e: e.dma_start(out=dbg["d_ksb"][:, :], in_=k_sb.ap().rearrange("p z t -> p (z t)")), dma=True, reads=[B_ksb])
                S.do("gpsimd", lambda e: e.dma_start(out=dbg["d_e"][:, :], in_=e_sb[(NSUB - 1) % 2][:, :]), dma=True, reads=[B_esb[(NSUB - 1) % 2]])
                S.do("gpsimd", lambda e: e.dma_start(out=dbg["d_vsb"][:, :], in_=v_sb[(NSUB - 1) % 2].ap().rearrange("p z c -> p (z c)")), dma=True, reads=[B_vsb[(NSUB - 1) % 2]])

        S.do("vector", lambda e: e.tensor_copy(
            payload[:, RM0 + 16:PAY], xnsum[:, :]), reads=[B_xnsum], writes=[B_pay])
        S.do("vector", lambda e: e.tensor_copy(
            xsh[:, :], xnsum[:, :]), reads=[B_xnsum], writes=[B_xshl])
        S.do("vector", lambda e: e.tensor_tensor(
            xsl[:, :], xnsum[:, :], xsh[:, :], ALU.subtract),
            reads=[B_xnsum, B_xshl], writes=[B_xshl])
        for half in range(2):
            for pi, (xop, wop) in enumerate(((xsh, wvh), (xsh, wvl), (xsl, wvh))):
                for z in range(NZ):
                    st = (pi == 0 and z == 0)
                    sp = (pi == 2 and z == NZ - 1)
                    S.do("tensor", lambda e, half=half, z=z, xop=xop, wop=wop,
                         st=st, sp=sp: e.matmul(
                        pb[5][0:1, :], xop[:, z:z + 1], wop[:, z, 512 * half:512 * (half + 1)],
                        start=st, stop=sp),
                        reads=[B_xshl, B_wv], writes=[PB[5]],
                        mm=(PB[5], st, sp, 512))
            S.do("vector", lambda e, half=half: e.scalar_tensor_tensor(
                out=vsrow[0:1, 512 * half:512 * (half + 1)],
                in0=bvrow[0:1, 512 * half:512 * (half + 1)], scalar=float(ntok),
                in1=pb[5][0:1, :], op0=ALU.mult, op1=ALU.add),
                reads=[PB[5], B_bvrow], writes=[B_vsrow])

        S.do("gpsimd", lambda e: e.dma_start(out=cc_in[0:128, :], in_=payload[:, :]),
             dma=True, reads=[B_pay])
        S.do("gpsimd", lambda e: e.dma_start(out=cc_in[128:129, :], in_=vsrow[0:1, :]),
             dma=True, reads=[B_vsrow])
        cc_gd_deps = [(sm, S.counts[sm]) for sm in DMA_SEMS["gpsimd"] if S.counts[sm] > 0]

        if debug:
            S.do("gpsimd", lambda e: e.dma_start(out=dbg["d_pay"][:, :], in_=payload[:, :]), dma=True, reads=[B_pay])
            S.do("gpsimd", lambda e: e.dma_start(out=dbg["d_stats"][:, :], in_=stats.ap().rearrange("p g c -> p (g c)")), dma=True, reads=[B_stats])

        S.barrier(exempt=("gpsimd",))
        with nc.Block() as blockA:
            S.flush(blockA, sems)
        stA.close()

        stB = ExitStack()
        xbB = [stB.enter_context(nc.sbuf_tensor("xb%d" % i, [128, NZ, TM], F32)) for i in range(2)]
        wr_sb = stB.enter_context(nc.sbuf_tensor("wr_sb", [128, NZ, D], F32))
        wkf_sb = stB.enter_context(nc.sbuf_tensor("wkf_sb", [128, NZ, D], F32))
        wvf_sb = stB.enter_context(nc.sbuf_tensor("wvf_sb", [128, NZ, D], F32))
        xr_sb = stB.enter_context(nc.sbuf_tensor("xr_sb", [128, NZ, TM], F32))
        r_sb = stB.enter_context(nc.sbuf_tensor("r_sb", [128, NZ, TM], F32))
        kfb = stB.enter_context(nc.sbuf_tensor("kfb", [128, NZ, TM], F32))
        xpcol = stB.enter_context(nc.sbuf_tensor("xpcol", [128, NZ, 1], F32))
        xlast = stB.enter_context(nc.sbuf_tensor("xlast", [128, NZ, 1], F32))
        mls = stB.enter_context(nc.sbuf_tensor("mls", [1, 4], F32))
        sqc = stB.enter_context(nc.sbuf_tensor("sqc", [128, NZ, 1], F32))

        B_xbB = [Buf(), Buf()]
        (B_wr, B_wkf, B_wvf, B_xr, B_r, B_kfb, B_xpc, B_xlast, B_mls,
         B_sqc) = (Buf() for _ in range(10))

        S.do("gpsimd", lambda e: e.collective_compute(
            "AllGather", ALU.bypass, replica_groups=groups,
            ins=[cc_in.opt()], outs=[cc_out.opt()]),
            deps=cc_gd_deps, sem="cc", inc=1)
        cc_tok = ("cc", S.counts["cc"])

        S.do("sync", lambda e: e.dma_start(
            out=xpcol[:, :, :], in_=xT.rearrange("(z p) t -> p z t", p=128)[:, :, 0:1],
            allow_slow_non_contiguous=True), dma=True, writes=[B_xpc])
        S.do("sync", lambda e: e.dma_start(
            out=xbB[0][:, :, :], in_=xT.rearrange("(z p) t -> p z t", p=128)[:, :, 1:1 + TM]),
            dma=True, writes=[B_xbB[0]])
        for nm, sb, bb in (("wr", wr_sb, B_wr), ("wkf", wkf_sb, B_wkf), ("wvf", wvf_sb, B_wvf)):
            S.do("sync", lambda e, nm=nm, sb=sb: e.dma_start(
                out=sb[:, :, :], in_=wT[nm].rearrange("(z p) d -> p z d", p=128)),
                dma=True, writes=[bb])

        S.do("vector", lambda e: e.tensor_tensor(
            sqc[:, :, :], xpcol[:, :, :], xpcol[:, :, :], ALU.mult),
            reads=[B_xpc], writes=[B_sqc])
        for (q, src, bsrc) in ((0, xpcol, B_xpc), (1, sqc, B_sqc)):
            for z in range(NZ):
                S.do("tensor", lambda e, q=q, z=z, src=src: e.matmul(
                    pb[0][0:1, q:q + 1], src[:, z, :], onescol[:, :],
                    start=(z == 0), stop=(z == NZ - 1)),
                    reads=[bsrc, B_ones], writes=[PB[0]],
                    mm=(PB[0], z == 0, z == NZ - 1, 1))
        S.do("vector", lambda e: e.tensor_scalar(
            mls[0:1, 0:2], pb[0][0:1, 0:2], C1024, None, ALU.mult),
            reads=[PB[0]], writes=[B_mls])
        S.do("vector", lambda e: e.tensor_tensor(
            mls[0:1, 2:3], mls[0:1, 0:1], mls[0:1, 0:1], ALU.mult),
            reads=[B_mls], writes=[B_mls])
        S.do("vector", lambda e: e.tensor_tensor(
            mls[0:1, 3:4], mls[0:1, 1:2], mls[0:1, 2:3], ALU.subtract),
            reads=[B_mls], writes=[B_mls])
        S.do("scalar", lambda e: e.activation(
            mls[0:1, 2:3], mls[0:1, 3:4], AF.Ln, bias=epscol[0:1, :]),
            reads=[B_mls, B_eps], writes=[B_mls])
        S.do("scalar", lambda e: e.activation(
            mls[0:1, 3:4], mls[0:1, 2:3], AF.Exp, scale=-0.5),
            reads=[B_mls], writes=[B_mls])
        S.do("tensor", lambda e: e.matmul(
            pb[1][:, 0:4], ones1[:, :], mls[0:1, 0:4], start=True, stop=True),
            reads=[B_ones1, B_mls], writes=[PB[1]], mm=(PB[1], True, True, 4))
        S.do("vector", lambda e: e.tensor_tensor(
            xpcol[:, :, :], xpcol[:, :, :],
            bass.AP(pb[1], 0, [[512, 128], [0, NZ], [1, 1]]), ALU.subtract),
            reads=[PB[1], B_xpc], writes=[B_xpc])
        S.do("vector", lambda e: e.tensor_tensor(
            xpcol[:, :, :], xpcol[:, :, :],
            bass.AP(pb[1], 3, [[512, 128], [0, NZ], [1, 1]]), ALU.mult),
            reads=[PB[1], B_xpc], writes=[B_xpc])

        S.do("scalar", lambda e: e.preload_act_table(AF.Tanh), noinc=True)

        def b_load(m):
            xb, B_xb = xbB[m % 2], B_xbB[m % 2]
            c0 = 1 + TM * m
            S.do("sync", lambda e, xb=xb, c0=c0: e.dma_start(
                out=xb[:, :, :], in_=xT.rearrange("(z p) t -> p z t", p=128)[:, :, c0:c0 + TM]),
                dma=True, writes=[B_xb])

        def b_apply_xr(m):
            xb, B_xb = xbB[m % 2], B_xbB[m % 2]
            for s in range(NSUB):
                ln_bcast(NSUB * m + s, 1, 2, s)
            for z in range(NZ):
                S.do("vector", lambda e, z=z, xb=xb: e.tensor_tensor(
                    xb[:, z, :], xb[:, z, :], pb[1][:, :], ALU.mult),
                    reads=[PB[1], B_xb], writes=[B_xb])
            for z in range(NZ):
                S.do("vector", lambda e, z=z, xb=xb: e.tensor_tensor(
                    xb[:, z, :], xb[:, z, :], pb[2][:, :], ALU.subtract),
                    reads=[PB[2], B_xb], writes=[B_xb])
            S.do("vector", lambda e, xb=xb: e.tensor_tensor(
                xr_sb[:, :, 1:TM], xb[:, :, 0:TM - 1],
                bass.AP(mu1m, 0, [[16, 128], [2, NZ], [0, TM - 1]]), ALU.mult),
                reads=[B_xb, B_mu1m], writes=[B_xr], tag="mix1 m%d" % m)
            for z in range(NZ):
                S.do("vector", lambda e, z=z, xb=xb: e.scalar_tensor_tensor(
                    out=xr_sb[:, z, 1:TM], in0=xb[:, z, 1:TM], scalar=smalls[:, z, 4:5],
                    in1=xr_sb[:, z, 1:TM], op0=ALU.mult, op1=ALU.add),
                    reads=[B_xb, B_smalls, B_xr], writes=[B_xr])

        def b_xr_bound(m):
            xb, B_xb = xbB[m % 2], B_xbB[m % 2]
            prevc, B_prevc = (xpcol, B_xpc) if m == 0 else (xlast, B_xlast)
            S.do("vector", lambda e, prevc=prevc, B_prevc=B_prevc: e.tensor_tensor(
                xr_sb[:, :, 0:1], prevc[:, :, :], mu1m[:, :, 0:1], ALU.mult),
                reads=[B_prevc, B_mu1m], writes=[B_xr])
            for z in range(NZ):
                S.do("vector", lambda e, z=z, xb=xb: e.scalar_tensor_tensor(
                    out=xr_sb[:, z, 0:1], in0=xb[:, z, 0:1], scalar=smalls[:, z, 4:5],
                    in1=xr_sb[:, z, 0:1], op0=ALU.mult, op1=ALU.add),
                    reads=[B_xb, B_smalls, B_xr], writes=[B_xr])

        def b_xkmix(m):
            xb, B_xb = xbB[m % 2], B_xbB[m % 2]
            prevc, B_prevc = (xpcol, B_xpc) if m == 0 else (xlast, B_xlast)
            S.do("vector", lambda e, xb=xb: e.tensor_tensor(
                kfb[:, :, 1:TM], xb[:, :, 0:TM - 1],
                bass.AP(mu1m, 1, [[16, 128], [2, NZ], [0, TM - 1]]), ALU.mult),
                reads=[B_xb, B_mu1m], writes=[B_kfb])
            S.do("vector", lambda e, prevc=prevc, B_prevc=B_prevc: e.tensor_tensor(
                kfb[:, :, 0:1], prevc[:, :, :], mu1m[:, :, 1:2], ALU.mult),
                reads=[B_prevc, B_mu1m], writes=[B_kfb])
            S.do("vector", lambda e, xb=xb: e.tensor_copy(
                xlast[:, :, :], xb[:, :, TM - 1:TM]), reads=[B_xb], writes=[B_xlast])
            for z in range(NZ):
                S.do("vector", lambda e, z=z, xb=xb: e.scalar_tensor_tensor(
                    out=xb[:, z, :], in0=xb[:, z, :], scalar=smalls[:, z, 5:6],
                    in1=kfb[:, z, :], op0=ALU.mult, op1=ALU.add),
                    reads=[B_smalls, B_kfb, B_xb], writes=[B_xb])

        b_apply_xr(0)
        b_xr_bound(0)
        b_xkmix(0)
        for m in range(nmac):
            xb, B_xb = xbB[m % 2], B_xbB[m % 2]

            if debug and m == 1:
                S.do("gpsimd", lambda e: e.dma_start(out=dbg["d_xl"][:, :], in_=xlast.ap().rearrange("p z o -> p (z o)"), allow_slow_non_contiguous=True), dma=True, reads=[B_xlast])
                S.do("gpsimd", lambda e: e.dma_start(out=dbg["d_xr0"][:, :], in_=xr_sb[:, :, 0:1].rearrange("p z o -> p (z o)"), allow_slow_non_contiguous=True), dma=True, reads=[B_xr])
                S.do("gpsimd", lambda e, xb=xb: e.dma_start(out=dbg["d_xk0"][:, :], in_=xb[:, :, 0:1].rearrange("p z o -> p (z o)"), allow_slow_non_contiguous=True), dma=True, reads=[B_xb])
            for dz in range(NZ):
                bank = 3 + dz % 2
                for z in range(NZ):
                    S.do("tensor", lambda e, dz=dz, z=z, bank=bank: e.matmul(
                        pb[bank][:, :], wr_sb[:, z, 128 * dz:128 * (dz + 1)],
                        xr_sb[:, z, :], start=(z == 0), stop=(z == NZ - 1)),
                        reads=[B_wr, B_xr], writes=[PB[bank]],
                        mm=(PB[bank], z == 0, z == NZ - 1, 512))
                S.do("scalar", lambda e, dz=dz, bank=bank: e.activation(
                    r_sb[:, dz, :], pb[bank][:, :], AF.Tanh, scale=0.5),
                    reads=[PB[bank]], writes=[B_r])

            if m + 1 < nmac:
                b_load(m + 1)
                b_apply_xr(m + 1)

            for dz in range(NZ):
                bank = 3 + dz % 2
                for z in range(NZ):
                    S.do("tensor", lambda e, dz=dz, z=z, bank=bank, xb=xb: e.matmul(
                        pb[bank][:, :], wkf_sb[:, z, 128 * dz:128 * (dz + 1)],
                        xb[:, z, :], start=(z == 0), stop=(z == NZ - 1)),
                        reads=[B_wkf, B_xb], writes=[PB[bank]],
                        mm=(PB[bank], z == 0, z == NZ - 1, 512))
                S.do("scalar", lambda e, dz=dz, bank=bank: e.activation(
                    kfb[:, dz, :], pb[bank][:, :], AF.Relu),
                    reads=[PB[bank]], writes=[B_kfb])
                S.do("scalar", lambda e, dz=dz: e.activation(
                    kfb[:, dz, :], kfb[:, dz, :], AF.Square),
                    reads=[B_kfb], writes=[B_kfb])

            for dz in range(NZ):
                bank = 3 + dz % 2
                for z in range(NZ):
                    S.do("tensor", lambda e, dz=dz, z=z, bank=bank: e.matmul(
                        pb[bank][:, :], wvf_sb[:, z, 128 * dz:128 * (dz + 1)],
                        kfb[:, z, :], start=(z == 0), stop=(z == NZ - 1)),
                        reads=[B_wvf, B_kfb], writes=[PB[bank]],
                        mm=(PB[bank], z == 0, z == NZ - 1, 512))
                if dz == 0:
                    S.do("vector", lambda e: e.tensor_scalar(
                        r_sb.ap().rearrange("p z t -> p (z t)"),
                        r_sb.ap().rearrange("p z t -> p (z t)"),
                        0.5, 0.5, ALU.mult, ALU.add), reads=[B_r], writes=[B_r])
                S.do("vector", lambda e, dz=dz, bank=bank: e.tensor_tensor(
                    r_sb[:, dz, :], pb[bank][:, :], r_sb[:, dz, :], ALU.mult),
                    reads=[PB[bank], B_r], writes=[B_r], tag="ffnevac dz%d" % dz)
            if not SKIP_FFN_OUT:
                S.do("gpsimd", lambda e, m=m: e.dma_start(
                    out=out.rearrange("(z p) t -> p z t", p=128)[:, :, TM * m:TM * (m + 1)],
                    in_=r_sb[:, :, :]), dma=True, reads=[B_r], writes=[B_out],
                    tag="ffnout m%d" % m)
            if m + 1 < nmac:
                b_xr_bound(m + 1)
                b_xkmix(m + 1)

        S.barrier()
        with nc.Block() as blockB:
            S.flush(blockB, sems)
        stB.close()

        stC = ExitStack()
        xbC = [stC.enter_context(nc.sbuf_tensor("xc%d" % i, [128, NZ, TM], F32)) for i in range(2)]
        xtC = stC.enter_context(nc.sbuf_tensor("xtC", [128, TM], F32))
        xnCh = stC.enter_context(nc.sbuf_tensor("xnCh", [128, NZ, TM], BF))
        xnCl = stC.enter_context(nc.sbuf_tensor("xnCl", [128, NZ, TM], BF))
        wqh = stC.enter_context(nc.sbuf_tensor("wqh", [128, NZ, D], BF))
        wql = stC.enter_context(nc.sbuf_tensor("wql", [128, NZ, D], BF))
        woh = stC.enter_context(nc.sbuf_tensor("woh", [128, NZ, D], BF))
        wol = stC.enter_context(nc.sbuf_tensor("wol", [128, NZ, D], BF))
        q_sb = stC.enter_context(nc.sbuf_tensor("q_sb", [128, NZ, TM], F32))
        sqq = stC.enter_context(nc.sbuf_tensor("sqq", [128, NZ, 128], F32))
        qp = [stC.enter_context(nc.sbuf_tensor("qp%d" % i, [128, 1024], F32)) for i in range(2)]
        qf_sb = stC.enter_context(nc.sbuf_tensor("qf_sb", [128, NZ, 128], F32))
        ctx_sb = stC.enter_context(nc.sbuf_tensor("ctx_sb", [128, 1024], F32))
        ctxfh = stC.enter_context(nc.sbuf_tensor("ctxfh", [128, NZ, TM], BF))
        ctxfl = stC.enter_context(nc.sbuf_tensor("ctxfl", [128, NZ, TM], BF))
        gat1 = stC.enter_context(nc.sbuf_tensor("gat1", [128, PAY], F32))
        vs2 = stC.enter_context(nc.sbuf_tensor("vs2", [1, D], F32))
        m16 = stC.enter_context(nc.sbuf_tensor("m16", [16, 128], F32))
        mred = stC.enter_context(nc.sbuf_tensor("mred", [16, 4], F32))
        srw = stC.enter_context(nc.sbuf_tensor("srw", [1, 16], F32))
        nrm = stC.enter_context(nc.sbuf_tensor("nrm", [128, 16], F32))
        rmq = stC.enter_context(nc.sbuf_tensor("rmq", [128, 16], F32))
        rmqn = stC.enter_context(nc.sbuf_tensor("rmqn", [128, 16], F32))
        dgq = stC.enter_context(nc.sbuf_tensor("dgq", [128, 16], F32))
        epre = stC.enter_context(nc.sbuf_tensor("epre", [128, 1024], F32))

        B_xbC = [Buf(), Buf()]
        (B_xnC, B_wq, B_wo, B_q, B_sqq, B_qf, B_ctx, B_ctxf, B_gat1, B_vs2,
         B_m16, B_mred, B_srw, B_nrm, B_rmq, B_dgq, B_epre) = (Buf() for _ in range(17))
        B_xtC, B_xnChl, B_wtmpC = Buf(), Buf(), Buf()
        B_rmqn = Buf()
        B_wtC = [B_wtmpC, Buf()]
        B_qp = [Buf(), Buf()]

        S.do("gpsimd", lambda e: e.dma_start(out=payload[:, :], in_=cc_out[0:128, :]),
            dma=True, deps=[cc_tok], writes=[B_pay])
        S.do("gpsimd", lambda e: e.dma_start(out=gat1[:, :], in_=cc_out[129:257, :]),
            dma=True, deps=[cc_tok], writes=[B_gat1])
        S.do("gpsimd", lambda e: e.dma_start(out=vsrow[0:1, 0:D], in_=cc_out[128:129, 0:D]),
            dma=True, deps=[cc_tok], writes=[B_vsrow])
        S.do("gpsimd", lambda e: e.dma_start(out=vs2[:, :], in_=cc_out[257:258, 0:D]),
            dma=True, deps=[cc_tok], writes=[B_vs2])
        S.do("vector", lambda e: e.tensor_tensor(
            payload[:, 0:RM0], payload[:, 0:RM0], gat1[:, 0:RM0], ALU.add),
            reads=[B_gat1, B_pay], writes=[B_pay])
        S.do("vector", lambda e: e.tensor_tensor(
            payload[:, RM0:RM0 + 16], payload[:, RM0:RM0 + 16],
            gat1[:, RM0:RM0 + 16], ALU.max), reads=[B_gat1, B_pay], writes=[B_pay])
        S.do("vector", lambda e: e.tensor_tensor(
            vsrow[0:1, 0:D], vsrow[0:1, 0:D], vs2[:, :], ALU.add),
            reads=[B_vs2, B_vsrow], writes=[B_vsrow])

        S.do("tensor", lambda e: e.transpose(
            pb[0][0:16, 0:128], payload[:, RM0:RM0 + 16], ident[:, :]),
            reads=[B_pay, B_ident], writes=[PB[0]], mm=(PB[0], True, True, 128))
        S.do("vector", lambda e: e.tensor_copy(m16[:, :], pb[0][0:16, 0:128]),
            reads=[PB[0]], writes=[B_m16])
        S.do("vector", lambda e: e.reduce_max(mred[0:16, 0:1], m16[0:16, :], axis=AX),
            reads=[B_m16], writes=[B_mred])
        S.do("scalar", lambda e: e.activation(mred[0:16, 1:2], mred[0:16, 0:1], AF.Exp, scale=-1.0),
            reads=[B_mred], writes=[B_mred])
        S.do("vector", lambda e: e.tensor_scalar(
            mred[0:16, 2:3], mred[0:16, 1:2], RATIO, None, ALU.mult),
            reads=[B_mred], writes=[B_mred])
        S.do("tensor", lambda e: e.transpose(pb[0][0:1, 128:144], mred[0:16, 2:3], ident[0:16, 0:16]),
            reads=[B_mred, B_ident], writes=[PB[0]], mm=(PB[0], True, True, 16))
        S.do("vector", lambda e: e.tensor_copy(srw[:, :], pb[0][0:1, 128:144]),
            reads=[PB[0]], writes=[B_srw])
        S.do("tensor", lambda e: e.matmul(pb[0][:, 256:272], ones1[:, :], srw[0:1, :],
                                        start=True, stop=True),
            reads=[B_ones1, B_srw], writes=[PB[0]], mm=(PB[0], True, True, 16))
        S.do("vector", lambda e: e.tensor_copy(sall[:, :], pb[0][:, 256:272]),
            reads=[PB[0]], writes=[B_sall])

        S.do("gpsimd", lambda e: e.memset(kvr[:, :, :], 0.0), writes=[B_kvr])
        EPSN = float(np.float32(EPS_FEAT) * 2 * ntok)
        for z in range(NZ):
            S.do("vector", lambda e, z=z: e.tensor_scalar(
                kvr[0:64, z, 0:64], pay_v[0:64, z, 0:64], sall[0:64, 2 * z:2 * z + 1],
                None, ALU.mult), reads=[B_pay, B_sall], writes=[B_kvr])
            S.do("vector", lambda e, z=z: e.tensor_scalar(
                kvr[64:128, z, 64:128], pay_v[64:128, z, 64:128],
                sall[64:128, 2 * z + 1:2 * z + 2], None, ALU.mult),
                reads=[B_pay, B_sall], writes=[B_kvr])
            S.do("tensor", lambda e, z=z: e.matmul(
                pb[1][:, 0:128], ones1[:, :], vsrow[0:1, 128 * z:128 * (z + 1)],
                start=True, stop=True), reads=[B_ones1, B_vsrow], writes=[PB[1]],
                mm=(PB[1], True, True, 128))
            S.do("vector", lambda e, z=z: e.scalar_tensor_tensor(
                out=kvr[0:64, z, 0:64], in0=pb[1][0:64, 0:64], scalar=EPS_FEAT,
                in1=kvr[0:64, z, 0:64], op0=ALU.mult, op1=ALU.add),
                reads=[PB[1], B_kvr], writes=[B_kvr])
            S.do("vector", lambda e, z=z: e.scalar_tensor_tensor(
                out=kvr[64:128, z, 64:128], in0=pb[1][64:128, 64:128], scalar=EPS_FEAT,
                in1=kvr[64:128, z, 64:128], op0=ALU.mult, op1=ALU.add),
                reads=[PB[1], B_kvr], writes=[B_kvr])
            S.do("vector", lambda e, z=z: e.tensor_scalar(
                kvr[0:64, z, 128:129], pay_v[0:64, z, 128:129],
                sall[0:64, 2 * z:2 * z + 1], EPSN, ALU.mult, ALU.add),
                reads=[B_pay, B_sall], writes=[B_kvr])
            S.do("vector", lambda e, z=z: e.tensor_scalar(
                kvr[64:128, z, 129:130], pay_v[64:128, z, 128:129],
                sall[64:128, 2 * z + 1:2 * z + 2], EPSN, ALU.mult, ALU.add),
                reads=[B_pay, B_sall], writes=[B_kvr])

        if debug:
            S.do("gpsimd", lambda e: e.dma_start(out=dbg["d_comb"][:, :], in_=payload[:, :]), dma=True, reads=[B_pay])
            S.do("gpsimd", lambda e: e.dma_start(out=dbg["d_kv"][:, :], in_=kvr.ap().rearrange("p z c -> p (z c)")), dma=True, reads=[B_kvr])
            S.do("gpsimd", lambda e: e.dma_start(out=dbg["d_sall"][:, :], in_=sall[:, :]), dma=True, reads=[B_sall])
            S.do("gpsimd", lambda e: e.dma_start(out=dbg["d_vsum"][:, :], in_=vsrow[0:1, 0:D]), dma=True, reads=[B_vsrow])

        ddq_ps = dd_ps
        dgq_ps = pb[0].ap().rearrange("p (a b) -> p a b", b=2)
        qk_ps = [pb[7][:, 0:390].rearrange("p (a b) -> p a b", b=130),
                pb[0][:, 0:390].rearrange("p (a b) -> p a b", b=130),
                pb[3][:, 0:260].rearrange("p (a b) -> p a b", b=130)]

        def c_load(m):
            xb, B_xb = xbC[m % 2], B_xbC[m % 2]
            c0 = 1 + TM * m
            S.do("sync", lambda e, xb=xb, c0=c0: e.dma_start(
                out=xb[:, :, :], in_=xT.rearrange("(z p) t -> p z t", p=128)[:, :, c0:c0 + TM]),
                dma=True, writes=[B_xb])

        def c_apply(m):
            xb, B_xb = xbC[m % 2], B_xbC[m % 2]
            for s in range(NSUB):
                ln_bcast(NSUB * m + s, 1, 2, s)
            for z in range(NZ):
                S.do("vector", lambda e, z=z, xb=xb: e.tensor_tensor(
                    xtC[:, :], xb[:, z, :], pb[1][:, :], ALU.mult),
                    reads=[B_xb, PB[1]], writes=[B_xtC])
                S.do("vector", lambda e, z=z: e.tensor_tensor(
                    xtC[:, :], xtC[:, :], pb[2][:, :], ALU.subtract),
                    reads=[PB[2], B_xtC], writes=[B_xtC])
                S.do("vector", lambda e, z=z: e.tensor_copy(
                    xnCh[:, z, :], xtC[:, :]), reads=[B_xtC], writes=[B_xnChl])
                S.do("vector", lambda e, z=z: e.tensor_tensor(
                    xnCl[:, z, :], xtC[:, :], xnCh[:, z, :], ALU.subtract),
                    reads=[B_xtC, B_xnChl], writes=[B_xnChl])

        if not SKIP_C:
            c_load(0)
        wtmpC2 = [q_sb.ap().rearrange("p z t -> p (z t)")[:, D * i:D * (i + 1)]
                  for i in range(2)]
        wi_c = [0]
        for nm, hi, lo, bb in (("wq", wqh, wql, B_wq), ("wo", woh, wol, B_wo)):
            S.do("gpsimd", lambda e, nm=nm, hi=hi: e.dma_start(
                out=hi[:, :, :], in_=wT[nm].rearrange("(z p) d -> p z d", p=128)),
                dma=True, writes=[bb], tag="wload_%s" % nm)
            for z in range(NZ):
                i = wi_c[0] % 2
                wi_c[0] += 1
                tmp, bt = wtmpC2[i], B_wtC[i]
                S.do("sync", lambda e, nm=nm, z=z, tmp=tmp: e.dma_start(
                    out=tmp, in_=wT[nm].rearrange("(z p) d -> p z d", p=128)[:, z, :]),
                    dma=True, writes=[bt])
                S.do("gpsimd", lambda e, z=z, hi=hi, lo=lo, tmp=tmp: e.tensor_tensor(
                    lo[:, z, :], tmp, hi[:, z, :], ALU.subtract),
                    reads=[bt, bb], writes=[bb, bt])
        if not SKIP_C:
            c_apply(0)
        for m in (range(nmac) if not SKIP_C else []):
            xb, B_xb = xbC[m % 2], B_xbC[m % 2]
            for dz in range(NZ):
                bank = 3 + dz % 2
                for pi, (wop, xop) in enumerate(
                        ((wqh, xnCh), (wqh, xnCl), (wql, xnCh))):
                    for z in range(NZ):
                        st = (pi == 0 and z == 0)
                        sp = (pi == 2 and z == NZ - 1)
                        S.do("tensor", lambda e, dz=dz, z=z, bank=bank, wop=wop,
                             xop=xop, st=st, sp=sp: e.matmul(
                            pb[bank][:, :], wop[:, z, 128 * dz:128 * (dz + 1)],
                            xop[:, z, :], start=st, stop=sp),
                            reads=[B_wq, B_xnChl], writes=[PB[bank]],
                            mm=(PB[bank], st, sp, 512))
                S.do("vector", lambda e, dz=dz, bank=bank: e.tensor_scalar(
                    q_sb[:, dz, :], pb[bank][:, :], smalls[:, dz, 0:1], None, ALU.add),
                    reads=[PB[bank], B_smalls],
                    writes=[B_q] + (B_wtC if m == 0 else []))

            def emit_pre(s):
                S.do("vector", lambda e, s=s: e.tensor_tensor(
                    sqq[:, :, :], q_sb[:, :, 128 * s:128 * (s + 1)],
                    q_sb[:, :, 128 * s:128 * (s + 1)], ALU.mult),
                    reads=[B_q], writes=[B_sqq])
                for z in (0, 4, 1, 5, 2, 6, 3, 7):
                    bank = 1 + z // 4
                    win = z % 4
                    S.do("tensor", lambda e, z=z, s=s, bank=bank, win=win: e.matmul(
                        ddq_ps[bank - 1][:, win, :], q_sb[:, z, 128 * s:128 * (s + 1)],
                        projs[:, z, :], start=True, stop=True),
                        reads=[B_q, B_projs], writes=[PB[bank]],
                        mm=(PB[bank], True, True, 128))
                for z in range(NZ):
                    S.do("tensor", lambda e, z=z: e.matmul(
                        dgq_ps[:, 200 + z, :], sqq[:, z, :], hones[:, :],
                        start=True, stop=True),
                        reads=[B_sqq, B_hones], writes=[PB[0]],
                        mm=(PB[0], True, True, 2))

            emit_pre(0)
            for s in range(NSUB):
                qb, B_qb = qp[s % 2], B_qp[s % 2]
                S.do("vector", lambda e: e.reduce_max(
                    rmq[:, 0:8].rearrange("p (a b) -> p a b", b=1),
                    pb[1].ap().rearrange("p (a b) -> p a b", b=64), axis=AX),
                    reads=[PB[1]], writes=[B_rmq])
                S.do("vector", lambda e: e.reduce_max(
                    rmq[:, 8:16].rearrange("p (a b) -> p a b", b=1),
                    pb[2].ap().rearrange("p (a b) -> p a b", b=64), axis=AX),
                    reads=[PB[2]], writes=[B_rmq])
                S.do("vector", lambda e: e.tensor_copy(
                    dgq.ap().rearrange("p (a b) -> p a b", b=2),
                    dgq_ps[:, 200:208, :]), reads=[PB[0]], writes=[B_dgq])
                S.do("vector", lambda e: e.tensor_tensor(
                    rmq[:, :], rmq[:, :], dgq[:, :], ALU.add),
                    reads=[B_dgq, B_rmq], writes=[B_rmq])
                S.do("vector", lambda e: e.tensor_scalar(
                    rmqn[:, :], rmq[:, :], -1.0, LNRATIO, ALU.mult, ALU.add),
                    reads=[B_rmq], writes=[B_rmqn])
                for z in range(NZ):
                    bank = 1 + z // 4
                    win = z % 4
                    for h2 in range(2):
                        S.do("scalar", lambda e, z=z, win=win, h2=h2, bank=bank, qb=qb: e.activation(
                            qb[:, 128 * z + 64 * h2:128 * z + 64 * (h2 + 1)],
                            ddq_ps[bank - 1][:, win, 64 * h2:64 * (h2 + 1)],
                            AF.Exp, bias=rmqn[:, 2 * z + h2:2 * z + h2 + 1]),
                            reads=[PB[bank], B_rmqn], writes=[B_qb])
                if s + 1 < NSUB:
                    emit_pre(s + 1)
                S.do("vector", lambda e, qb=qb: e.tensor_scalar(
                    qb[:, :], qb[:, :], EPS_FEAT, None, ALU.add),
                    reads=[B_qb], writes=[B_qb])
                for z in range(NZ):
                    bank = 5 + z // 4
                    win = z % 4
                    S.do("tensor", lambda e, z=z, bank=bank, win=win, qb=qb: e.transpose(
                        pb[bank][:, 128 * win:128 * (win + 1)], qb[:, 128 * z:128 * (z + 1)],
                        ident[:, :]), reads=[B_qb, B_ident], writes=[PB[bank]],
                        mm=(PB[bank], True, True, 128))
                for bi, bank in enumerate((5, 6)):
                    S.do("vector", lambda e, bi=bi, bank=bank: e.tensor_copy(
                        qf_sb[:, 4 * bi:4 * (bi + 1), :],
                        pb[bank].ap().rearrange("p (a b) -> p a b", b=128)),
                        reads=[PB[bank]], writes=[B_qf])
                for z in range(NZ):
                    bidx = z // 3 if z < 6 else 2
                    widx = z % 3 if z < 6 else z - 6
                    S.do("tensor", lambda e, z=z, bidx=bidx, widx=widx: e.matmul(
                        qk_ps[bidx][:, widx, :], qf_sb[:, z, :], kvr[:, z, :],
                        start=True, stop=True),
                        reads=[B_qf, B_kvr], writes=[PB[(7, 0, 3)[bidx]]],
                        mm=(PB[(7, 0, 3)[bidx]], True, True, 130))
                for bidx, (bank, z0, zn) in enumerate(((7, 0, 3), (0, 3, 6), (3, 6, 8))):
                    S.do("vector", lambda e, bidx=bidx, z0=z0, zn=zn: e.tensor_copy(
                        nrm[:, 2 * z0:2 * zn].rearrange("p (a b) -> p a b", b=2),
                        qk_ps[bidx][:, 0:zn - z0, 128:130]),
                        reads=[PB[bank]], writes=[B_nrm])
                S.do("vector", lambda e: e.tensor_scalar(
                    nrm[:, :], nrm[:, :], EPS_DEN, None, ALU.max),
                    reads=[B_nrm], writes=[B_nrm])
                S.do("vector", lambda e: e.reciprocal(nrm[:, :], nrm[:, :]),
                    reads=[B_nrm], writes=[B_nrm])
                for z in range(NZ):
                    bidx = z // 3 if z < 6 else 2
                    widx = z % 3 if z < 6 else z - 6
                    bank = (7, 0, 3)[bidx]
                    for h2 in range(2):
                        S.do("vector", lambda e, z=z, bidx=bidx, widx=widx, h2=h2, bank=bank: e.tensor_scalar(
                            ctx_sb[:, 128 * z + 64 * h2:128 * z + 64 * (h2 + 1)],
                            qk_ps[bidx][:, widx, 64 * h2:64 * (h2 + 1)],
                            nrm[:, 2 * z + h2:2 * z + h2 + 1], None, ALU.mult),
                            reads=[PB[bank], B_nrm], writes=[B_ctx])
                for z in range(NZ):
                    bank = 5 + z // 4
                    win = z % 4
                    S.do("tensor", lambda e, z=z, bank=bank, win=win: e.transpose(
                        pb[bank][:, 128 * win:128 * (win + 1)], ctx_sb[:, 128 * z:128 * (z + 1)],
                        ident[:, :]), reads=[B_ctx, B_ident], writes=[PB[bank]],
                        mm=(PB[bank], True, True, 128))
                for bi, bank in enumerate((5, 6)):
                    S.do("vector", lambda e, bi=bi, bank=bank, s=s: e.tensor_copy(
                        ctxfh[:, 4 * bi:4 * (bi + 1), 128 * s:128 * (s + 1)],
                        pb[bank].ap().rearrange("p (a b) -> p a b", b=128)),
                        reads=[PB[bank]], writes=[B_ctxf])
                    S.do("vector", lambda e, bi=bi, bank=bank, s=s: e.tensor_tensor(
                        ctxfl[:, 4 * bi:4 * (bi + 1), 128 * s:128 * (s + 1)],
                        pb[bank].ap().rearrange("p (a b) -> p a b", b=128),
                        ctxfh[:, 4 * bi:4 * (bi + 1), 128 * s:128 * (s + 1)],
                        ALU.subtract), reads=[PB[bank], B_ctxf], writes=[B_ctxf])

                if debug and m == 0 and s == 0:
                    S.do("gpsimd", lambda e, qb=qb: e.dma_start(out=dbg["d_qp"][:, :], in_=qb[:, :]), dma=True, reads=[B_qb])

            if m + 1 < nmac:
                c_load(m + 1)
                c_apply(m + 1)
            for dz in range(NZ):
                bank = 3 + dz % 2
                for pi, (wop, xop) in enumerate(
                        ((woh, ctxfh), (woh, ctxfl), (wol, ctxfh))):
                    for z in range(NZ):
                        st = (pi == 0 and z == 0)
                        sp = (pi == 2 and z == NZ - 1)
                        S.do("tensor", lambda e, dz=dz, z=z, bank=bank, wop=wop,
                             xop=xop, st=st, sp=sp: e.matmul(
                            pb[bank][:, :], wop[:, z, 128 * dz:128 * (dz + 1)],
                            xop[:, z, :], start=st, stop=sp),
                            reads=[B_wo, B_ctxf], writes=[PB[bank]],
                            mm=(PB[bank], st, sp, 512))
                S.do("vector", lambda e, dz=dz, bank=bank, xb=xb: e.scalar_tensor_tensor(
                    out=xb[:, dz, :], in0=pb[bank][:, :], scalar=smalls[:, dz, 3:4],
                    in1=xb[:, dz, :], op0=ALU.add, op1=ALU.add),
                    reads=[PB[bank], B_smalls, B_xb], writes=[B_xb])
            if not SKIP_ATTN_OUT:
                S.do("gpsimd", lambda e, m=m, xb=xb: e.dma_start(
                    out=out.rearrange("(z p) t -> p z t", p=128)[:, :, TM * m:TM * (m + 1)],
                    in_=xb[:, :, :], accum_op=ALU.add), dma=True,
                    reads=[B_xb], writes=[B_out])

            if debug and m == 0:
                pass

        S.barrier()
        with nc.Block() as blockC:
            S.flush(blockC, sems)
        stC.close()

    return nc


def kernel(**inputs):
    x = np.asarray(inputs["L0_attn_query"], dtype=np.float32)
    Bn, N, Dm = x.shape

    wq = np.ascontiguousarray(np.asarray(inputs["L0_attn_Wq"], np.float32).T)
    wk = np.ascontiguousarray(np.asarray(inputs["L0_attn_Wk"], np.float32).T)
    wv = np.ascontiguousarray(np.asarray(inputs["L0_attn_Wv"], np.float32).T)
    wo = np.ascontiguousarray(np.asarray(inputs["L0_attn_Wo"], np.float32).T)
    wr = np.ascontiguousarray(np.asarray(inputs["L0_ffn_W_r"], np.float32).T)
    wkf = np.ascontiguousarray(np.asarray(inputs["L0_ffn_W_k"], np.float32).T)
    wvf = np.ascontiguousarray(np.asarray(inputs["L0_ffn_W_v"], np.float32).T)
    proj = np.asarray(inputs["L0_attn_proj"], np.float32)

    projP = np.zeros((NZ, 128, 128), np.float32)
    for z in range(NZ):
        for a in range(2):
            projP[z, 64 * a:64 * (a + 1), 64 * a:64 * (a + 1)] = proj[2 * z + a].T

    bias_rows = np.zeros((8, D), np.float32)
    bias_rows[0] = np.asarray(inputs["L0_attn_bq"], np.float32)
    bias_rows[1] = np.asarray(inputs["L0_attn_bk"], np.float32)
    bias_rows[2] = np.asarray(inputs["L0_attn_bv"], np.float32)
    bias_rows[3] = np.asarray(inputs["L0_attn_bo"], np.float32)
    bias_rows[4] = np.asarray(inputs["L0_ffn_mu_r"], np.float32)
    bias_rows[5] = np.asarray(inputs["L0_ffn_mu_k"], np.float32)
    biasT = np.ascontiguousarray(bias_rows.T)

    ntok = N // 2
    in_maps = []
    for c in range(8):
        b, h = c // 2, c % 2
        xTc = np.zeros((D, ntok + 1), np.float32)
        xTc[:, 1:] = x[b, ntok * h:ntok * (h + 1)].T
        if h == 1:
            xTc[:, 0] = x[b, ntok - 1]
        in_maps.append({
            "xT": xTc, "wq": wq, "wk": wk, "wv": wv, "wo": wo,
            "wr": wr, "wkf": wkf, "wvf": wvf,
            "projP": projP, "biasT": biasT, "bias_rows": bias_rows,
        })

    nc = build_nc(nmac=ntok // TM, debug=DEBUG)
    res = run_bass_kernel_spmd(nc, in_maps, core_ids=list(range(8)))

    y = np.empty((Bn, N, Dm), np.float32)
    for c in range(8):
        b, h = c // 2, c % 2
        y[b, ntok * h:ntok * (h + 1)] = res.results[c]["out"].T
    if DEBUG:
        kernel.debug_results = res.results
    return y
